# version 59
# baseline (speedup 1.0000x reference)
import contextlib
import numpy as np
import ml_dtypes
import concourse.bass as bass
import concourse.mybir as mybir
from concourse.bass_utils import run_bass_kernel_spmd

F32 = mybir.dt.float32
BF16 = mybir.dt.bfloat16
AF = mybir.ActivationFunctionType
ALU = mybir.AluOpType
AX = mybir.AxisListType
BFNP = ml_dtypes.bfloat16

T, D, DI, NS, RK = 2048, 1024, 2048, 16, 128
NDB, NDD, NHB = 16, 8, 32
TSUB = 16
TC, NCH = 512, 4
NW = 4096 * 1024
STK = ("ssq", "ssum", "mu", "e2", "mu2", "var", "lv", "rstd", "nb")


class Pr:

    def __init__(self, nc):
        self.nc = nc
        self.h = {}
        self.cnt = {}
        self.seen = {}

    def sem(self, name):
        if name not in self.h:
            self.h[name] = self.nc.alloc_semaphore("sem_" + name)
            self.cnt[name] = 0
        return self.h[name]

    def wait(self, eng, ename, tokens):
        need = {}
        for (n, v) in tokens:
            if v > need.get(n, 0):
                need[n] = v
        for n, v in sorted(need.items()):
            k = (ename, n)
            if self.seen.get(k, -1) >= v:
                continue
            eng.wait_ge(self.sem(n), v)
            self.seen[k] = v

    def inc(self, ins, name, k=1):
        s = self.sem(name)
        ins.then_inc(s, k)
        self.cnt[name] += k
        return (name, self.cnt[name])


class Buf:
    __slots__ = ("ap", "ready", "users")

    def __init__(self, ap):
        self.ap = ap
        self.ready = []
        self.users = []


_BUFSEM = {}


def bufsem(buf):
    if id(buf) not in _BUFSEM:
        _BUFSEM[id(buf)] = f"dm{len(_BUFSEM)}"
    return _BUFSEM[id(buf)]


def _deps_for(writes, reads, extra):
    deps = list(extra)
    for b in reads:
        deps += b.ready
    for b in writes:
        deps += b.ready + b.users
    return deps


def build(taps=()):
    taps = set(taps)
    _BUFSEM.clear()
    nc = bass.Bass("TRN2", target_bir_lowering=False, debug=False)
    pr = Pr(nc)

    def din(name, shape, dt):
        return nc.dram_tensor(name, list(shape), dt, kind="ExternalInput").ap()

    def dscr(name, shape, dt):
        kind = "ExternalOutput" if name in taps else "Internal"
        return nc.dram_tensor(name, list(shape), dt, kind=kind).ap()

    d_x = din("x", (T, D), F32)
    d_win = din("w_in_r", (128, NDD * 4096), F32)
    d_wx = din("w_x_r", (128, NDB * 160), F32)
    d_wdt = din("w_dt", (RK, DI), F32)
    d_wout = din("w_out_r", (128, NDB * 1024), F32)
    d_cw = din("conv_w_r", (128, NDB * 4), F32)
    d_cb = din("conv_b_r", (128, NDB), F32)
    d_bdt = din("b_dt_r", (128, NDB), F32)
    d_alog = din("a_log_r", (128, NDB * NS), F32)
    d_D = din("d_r", (128, NDB), F32)
    d_wfc = din("wfc", (4096, 1024), F32)
    d_wproj = din("wproj", (1024, 4096), F32)
    d_lnw = din("lnw_r", (128, NHB), F32)
    d_lnb = din("lnb_r", (128, NHB), F32)
    d_consts = din("consts", (128, 4), F32)
    d_id16 = din("id16", (128, 128), BF16)
    d_ones = din("ones_col16", (128, 1), BF16)

    sc_xhat = dscr("sc_xhat", (T, D), BF16)
    sc_ucb = dscr("sc_ucb", (DI, T), BF16)
    sc_sz = dscr("sc_sz", (DI, T), BF16)
    sc_b = dscr("sc_b", (NS, T), BF16)
    sc_c = dscr("sc_c", (NS, T), BF16)
    sc_x1 = dscr("sc_x1", (T, D), F32)
    sc_xh1 = dscr("sc_xh1", (T, D), BF16)
    sc_wqfc = dscr("sc_wqfc", (4096, 1024), BF16)
    sc_wqproj = dscr("sc_wqproj", (1024, 4096), BF16)
    sc_ahat = dscr("sc_ahat", (NHB, 128, T), BF16)
    sc_ab = dscr("sc_ab", (NHB, 128, 512), BF16)
    sc_rows16 = dscr("sc_rows16", (2, T), BF16)
    sc_rows = dscr("sc_rows", (4, T), F32)
    d_out = nc.dram_tensor("out", [T, D], F32, kind="ExternalOutput").ap()

    ENG = {}
    for nm, e in (("v", nc.vector), ("s", nc.scalar), ("t", nc.tensor),
                  ("g", nc.gpsimd), ("y", nc.sync)):
        ENG[id(e)] = nm

    def emit(eng, make, sem, writes=(), reads=(), extra=(), dk=1):
        deps = _deps_for(writes, reads, extra)
        pr.wait(eng, ENG[id(eng)], deps)
        ins = make()
        if sem is None:
            assert dk == 16 and writes
            sem = bufsem(writes[0])
        tok = pr.inc(ins, sem, dk)
        for b in writes:
            b.ready = [tok]
            b.users = []
        for b in reads:
            b.users.append(tok)
        return tok

    FENCE = [[]]

    def sb(name, shape, dt, stack=None, **kw):
        if stack is None:
            b = Buf(nc.alloc_sbuf_tensor(name, list(shape), dt, **kw).ap())
        else:
            h = stack.enter_context(nc.sbuf_tensor(name, list(shape), dt, **kw))
            b = Buf(h.ap())
        b.ready = list(FENCE[0])
        return b

    def ps(name, shape, stack):
        h = stack.enter_context(nc.psum_tensor(name, list(shape), F32))
        return Buf(h.ap())

    def bcastP(ap2d, p=128):
        assert ap2d.shape[0] == 1, ap2d.shape
        dims = [list(dd) for dd in ap2d.ap]
        return bass.AP(ap2d.tensor, ap2d.offset, [[0, p]] + dims[1:])

    def mm_group(pbuf, items, reads, extra=()):
        deps = _deps_for([pbuf], reads, extra)
        pr.wait(nc.tensor, "t", deps)
        last = None
        for (o, l, r, st, sp) in items:
            last = nc.tensor.matmul(o, l, r, start=st, stop=sp)
        tok = pr.inc(last, "t", 1)
        pbuf.ready = [tok]
        pbuf.users = []
        for b in reads:
            b.users.append(tok)
        return tok

    def barrier():
        toks = [(n, c) for n, c in pr.cnt.items() if c > 0]
        for nm, e in (("v", nc.vector), ("s", nc.scalar), ("t", nc.tensor),
                      ("g", nc.gpsimd), ("y", nc.sync)):
            pr.wait(e, nm, toks)

    s_cw = sb("s_cw", (128, NDB * 4), F32)
    s_cb = sb("s_cb", (128, NDB), F32)
    s_bdt = sb("s_bdt", (128, NDB), F32)
    s_A = sb("s_A", (128, NDB * NS), F32)
    s_D = sb("s_D", (128, NDB), F32)
    s_lnw = sb("s_lnw", (128, NHB), F32)
    s_lnb = sb("s_lnb", (128, NHB), F32)
    s_consts = sb("s_consts", (128, 4), F32)
    s_id16 = sb("s_id16", (128, 128), BF16)
    s_ones = sb("s_ones", (128, 1), BF16)
    s_diagD = sb("s_diagD", (128, NDB * 128), BF16)
    s_dtr = sb("s_dtr", (RK, DI), BF16)
    s_carry = sb("s_carry", (128, NDB * NS), BF16)
    s_dump = sb("s_dump", (128, 1024), F32)
    s_wdt = sb("s_wdt", (RK, DI), BF16)
    s_gacc = sb("s_gacc", (128, 80), F32)
    s_thr = {"fc": sb("s_thrpfc", (128, 2), F32), "pj": sb("s_thrppj", (128, 2), F32)}

    ONE = s_consts.ap[:, 0:1]
    EPS = s_consts.ap[:, 1:2]

    for src, dst in ((d_cw, s_cw), (d_cb, s_cb), (d_bdt, s_bdt), (d_alog, s_A),
                     (d_D, s_D), (d_lnw, s_lnw), (d_lnb, s_lnb), (d_consts, s_consts),
                     (d_id16, s_id16), (d_ones, s_ones)):
        emit(nc.sync, lambda s=src, d=dst: nc.sync.dma_start(d.ap, s), None,
             writes=[dst], dk=16)
    emit(nc.gpsimd, lambda: nc.gpsimd.dma_start(s_wdt.ap, d_wdt), None,
         writes=[s_wdt], dk=16)

    emit(nc.scalar, lambda: nc.scalar.activation(s_A.ap, s_A.ap, AF.Exp),
         "s", reads=[s_A], extra=s_A.ready)
    s_A.ready = [("s", pr.cnt["s"])]
    emit(nc.vector, lambda: nc.vector.tensor_scalar(s_A.ap, s_A.ap, -1.0, None,
         op0=ALU.mult), "v", reads=[s_A], extra=s_A.ready)
    s_A.ready = [("v", pr.cnt["v"])]
    for db in range(NDB):
        emit(nc.vector, lambda db=db: nc.vector.tensor_scalar(
            s_diagD.ap[:, db * 128:(db + 1) * 128], s_id16.ap,
            s_D.ap[:, db:db + 1], None, op0=ALU.mult),
            "v", writes=[s_diagD], reads=[s_id16, s_D])

    def ln_rows(src_ap, src_buf, st, out_buf, out_ap, nfeat):
        inv = 1.0 / nfeat
        emit(nc.scalar, lambda: nc.scalar.activation(
            s_dump.ap[:, 0:nfeat], src_ap, AF.Square, accum_out=st["ssq"].ap),
            "s", writes=[st["ssq"]], reads=[src_buf])
        emit(nc.vector, lambda: nc.vector.reduce_sum(st["ssum"].ap, src_ap, axis=AX.X),
            "v", writes=[st["ssum"]], reads=[src_buf], extra=st["ssq"].ready)
        emit(nc.vector, lambda: nc.vector.tensor_scalar(st["mu"].ap, st["ssum"].ap,
            inv, None, op0=ALU.mult), "v", writes=[st["mu"]], reads=[st["ssum"]])
        emit(nc.vector, lambda: nc.vector.tensor_scalar(st["e2"].ap, st["ssq"].ap,
            inv, None, op0=ALU.mult), "v", writes=[st["e2"]], reads=[st["ssq"]])
        emit(nc.vector, lambda: nc.vector.tensor_mul(st["mu2"].ap, st["mu"].ap,
            st["mu"].ap), "v", writes=[st["mu2"]], reads=[st["mu"]])
        emit(nc.vector, lambda: nc.vector.tensor_sub(st["var"].ap, st["e2"].ap,
            st["mu2"].ap), "v", writes=[st["var"]], reads=[st["e2"], st["mu2"]])
        emit(nc.scalar, lambda: nc.scalar.activation(st["lv"].ap, st["var"].ap,
            AF.Ln, bias=EPS), "s", writes=[st["lv"]], reads=[st["var"], s_consts])
        emit(nc.scalar, lambda: nc.scalar.activation(st["rstd"].ap, st["lv"].ap,
            AF.Exp, scale=-0.5), "s", writes=[st["rstd"]], reads=[st["lv"]])
        emit(nc.vector, lambda: nc.vector.scalar_tensor_tensor(st["nb"].ap,
            st["mu"].ap, -1.0, st["rstd"].ap, op0=ALU.mult, op1=ALU.mult),
            "v", writes=[st["nb"]], reads=[st["mu"], st["rstd"]])
        emit(nc.scalar, lambda: nc.scalar.activation(out_ap, src_ap, AF.Identity,
            scale=st["rstd"].ap[:, 0:1], bias=st["nb"].ap[:, 0:1]),
            "s", writes=[out_buf], reads=[src_buf, st["rstd"], st["nb"]])

    stWin = contextlib.ExitStack()
    s_win = sb("s_win", (128, NDD * 4096), BF16, stWin)
    for wk in range(8):
        emit(nc.gpsimd, lambda wk=wk: nc.gpsimd.dma_start(
            s_win.ap[:, wk * 4096:(wk + 1) * 4096],
            d_win[:, wk * 4096:(wk + 1) * 4096]), None,
            writes=[s_win], dk=16)

    s_xhatT = sb("s_xhatT", (128, NDD * T), BF16, stWin)
    with contextlib.ExitStack() as stE:
        NE = 4
        s_xe = [sb(f"s_xe{i}", (128, D), F32, stE) for i in range(NE)]
        s_xh = [sb(f"s_xh{i}", (128, D), BF16, stE) for i in range(2)]
        statsE = [{k: sb(f"e_{k}{i}", (128, 1), F32, stE) for k in STK}
                  for i in range(2)]
        p_xt = [ps(f"p_xt{i}", (128, 128), stE) for i in range(2)]
        xtsem = bufsem(s_xhatT)

        def load_xe(ts):
            if ts < TSUB:
                qe = nc.sync if ts % 2 == 0 else nc.scalar
                emit(qe, lambda ts=ts, qe=qe: qe.dma_start(
                    s_xe[ts % NE].ap, d_x[ts * 128:(ts + 1) * 128, :]), None,
                    writes=[s_xe[ts % NE]], dk=16)

        for ts in range(NE - 1):
            load_xe(ts)
        for ts in range(TSUB):
            i = ts % 2
            load_xe(ts + NE - 1)
            ln_rows(s_xe[ts % NE].ap, s_xe[ts % NE], statsE[i], s_xh[i], s_xh[i].ap, D)
            for dd in range(NDD):
                pb = p_xt[dd % 2]
                mm_group(pb, [(pb.ap, s_xh[i].ap[:, dd * 128:(dd + 1) * 128],
                               s_id16.ap, True, True)],
                         reads=[s_xh[i], s_id16])
                emit(nc.vector, lambda pb=pb, dd=dd, ts=ts: nc.vector.tensor_copy(
                    s_xhatT.ap[:, dd * T + ts * 128: dd * T + (ts + 1) * 128],
                    pb.ap), xtsem, reads=[pb])
        s_xhatT.ready = [(xtsem, pr.cnt[xtsem])]
        barrier()

    ucb_spill = {}
    sz_spill = {}
    with contextlib.ExitStack() as st1:
        pu = [ps(f"pu{i}", (128, T), st1) for i in range(2)]
        s_u = [sb(f"s_u{i}", (128, T), F32, st1) for i in range(2)]
        s_cv = [sb(f"s_cv{i}", (128, T), F32, st1) for i in range(2)]
        s_uct = [sb(f"s_uct{i}", (128, T), BF16, st1) for i in range(2)]
        s_szt = [sb(f"s_szt{i}", (128, T), BF16, st1) for i in range(2)]
        s_gw = [sb(f"s_gw{i}", (128, 2048), F32, st1) for i in range(2)]
        s_dumpg = sb("s_dumpg", (128, 2048), F32, st1)

        g_tiles = []
        for k in range(32):
            g_tiles.append((d_wfc[k * 128:(k + 1) * 128, :], 1024))
        for k in range(8):
            g_tiles.append((d_wproj[k * 128:(k + 1) * 128, 0:2048], 2048))
            g_tiles.append((d_wproj[k * 128:(k + 1) * 128, 2048:4096], 2048))

        def gamma_tile(gi):
            src, w = g_tiles[gi]
            i = gi % 2
            emit(nc.sync, lambda: nc.sync.dma_start(s_gw[i].ap[:, 0:w], src), None,
                 writes=[s_gw[i]], dk=16)
            emit(nc.scalar, lambda: nc.scalar.activation(
                s_dumpg.ap[:, 0:w], s_gw[i].ap[:, 0:w], AF.Abs,
                accum_out=s_gacc.ap[:, gi:gi + 1]),
                "s", writes=[s_gacc], reads=[s_gw[i]])

        qtiles = []
        for nm, dsrc, ddst, rows, wdt in (("fc", d_wfc, sc_wqfc, 32, 1024),
                                          ("pj", d_wproj, sc_wqproj, 8, 4096)):
            for r in range(rows):
                for cs in range(0, wdt, 1024):
                    qtiles.append((nm, dsrc, ddst, r, cs))
        qstate = {"k": 0}
        s_qw = [None, None]
        s_qc = [None, None]
        s_qq = [None, None]

        def mk_qbufs():
            qqv = s_dumpg.ap.bitcast(BF16)
            for i in range(2):
                for lst, src, apv in (
                        (s_qw, s_gw[0], s_gw[0].ap[:, i * 1024:(i + 1) * 1024]),
                        (s_qc, s_gw[1], s_gw[1].ap[:, i * 1024:(i + 1) * 1024]),
                        (s_qq, s_dumpg, qqv[:, i * 1024:(i + 1) * 1024])):
                    b = Buf(apv)
                    b.ready = list(src.ready) + list(src.users)
                    lst[i] = b

        def qload(k):
            if k >= len(qtiles):
                return
            nm, dsrc, ddst, r, cs = qtiles[k]
            i = k % 2
            qeng = nc.sync if k % 2 == 0 else nc.scalar
            emit(qeng, lambda i=i, r=r, cs=cs, dsrc=dsrc, qeng=qeng: qeng.dma_start(
                s_qw[i].ap, dsrc[r * 128:(r + 1) * 128, cs:cs + 1024]), None,
                writes=[s_qw[i]], dk=16)

        def qstep():
            k = qstate["k"]
            if k >= len(qtiles):
                return
            qstate["k"] = k + 1
            if k == 0:
                qload(0)
                qload(1)
            nm, dsrc, ddst, r, cs = qtiles[k]
            i = k % 2
            thrp = s_thr[nm]
            emit(nc.vector, lambda i=i, thrp=thrp: nc.vector.tensor_scalar(
                s_qc[i].ap, s_qw[i].ap, thrp.ap[:, 0:1], thrp.ap[:, 1:2],
                op0=ALU.min, op1=ALU.max), "v",
                writes=[s_qc[i]], reads=[s_qw[i], thrp])
            emit(nc.vector, lambda i=i: nc.vector.tensor_sub(
                s_qc[i].ap, s_qw[i].ap, s_qc[i].ap), "v",
                writes=[s_qc[i]], reads=[s_qw[i]])
            qload(k + 2)
            emit(nc.scalar, lambda i=i: nc.scalar.activation(
                s_qq[i].ap, s_qc[i].ap, AF.Sign), "s",
                writes=[s_qq[i]], reads=[s_qc[i]])
            emit(nc.sync, lambda i=i, r=r, cs=cs, ddst=ddst: nc.sync.dma_start(
                ddst[r * 128:(r + 1) * 128, cs:cs + 1024], s_qq[i].ap),
                f"wQ{k % 4}", reads=[s_qq[i]], dk=16)

        def do_thresholds():
            for j, (nm, cols, off) in enumerate((("fc", 32, 0), ("pj", 16, 32))):
                g1 = sb(f"s_g1{nm}", (128, 1), F32, st1)
                emit(nc.vector, lambda cols=cols, off=off, g1=g1:
                     nc.vector.reduce_sum(
                         g1.ap, s_gacc.ap[:, off:off + cols], axis=AX.X), "v",
                     writes=[g1], reads=[s_gacc])
                emit(nc.sync, lambda g1=g1, j=j: nc.sync.dma_start(
                    sc_rows[2:3, j * 128:(j + 1) * 128],
                    g1.ap), "wR", reads=[g1], dk=16)
                gr = sb(f"s_gr{nm}", (1, 128), F32, st1)
                emit(nc.sync, lambda gr=gr, j=j: nc.sync.dma_start(
                    gr.ap, sc_rows[2:3, j * 128:(j + 1) * 128]),
                    None, writes=[gr], extra=g1.users, dk=16)
                gs = sb(f"s_gs{nm}", (1, 1), F32, st1)
                emit(nc.vector, lambda gr=gr, gs=gs: nc.vector.reduce_sum(
                    gs.ap, gr.ap, axis=AX.X), "v", writes=[gs], reads=[gr])
                thr2 = sb(f"s_thr2{nm}", (1, 2), F32, st1)
                emit(nc.vector, lambda gs=gs, thr2=thr2: nc.vector.tensor_scalar(
                    thr2.ap[:, 0:1], gs.ap, 0.5 / NW, 0.5e-6,
                    op0=ALU.mult, op1=ALU.add),
                    "v", writes=[thr2], reads=[gs])
                emit(nc.vector, lambda thr2=thr2: nc.vector.tensor_scalar(
                    thr2.ap[:, 1:2], thr2.ap[:, 0:1], -1.0, None, op0=ALU.mult),
                    "v", reads=[thr2], extra=thr2.ready)
                t2tok = ("v", pr.cnt["v"])
                thr2.ready = [t2tok]
                emit(nc.sync, lambda thr2=thr2, j=j: nc.sync.dma_start(
                    sc_rows[3:4, j * 2:j * 2 + 2], thr2.ap),
                    "wR", reads=[thr2], dk=16)
                thrp = s_thr[nm]
                emit(nc.sync, lambda thrp=thrp, j=j: nc.sync.dma_start(
                    thrp.ap, bcastP(sc_rows[3:4, j * 2:j * 2 + 2])),
                    None, writes=[thrp], extra=thr2.users, dk=16)

        pend_silu = None
        pend_zsilu = None

        def do_zsilu(pi):
            i, db = pi
            emit(nc.scalar, lambda i=i: nc.scalar.activation(
                s_szt[i].ap, pu[i].ap, AF.Silu), "s",
                writes=[s_szt[i]], reads=[pu[i]])
            sz_spill[db] = emit(nc.sync, lambda i=i, db=db: nc.sync.dma_start(
                sc_sz[db * 128:(db + 1) * 128, :], s_szt[i].ap), f"wZ{i}",
                reads=[s_szt[i]], dk=16)

        def do_silu(pi):
            i, db = pi
            cv = s_cv[i]
            emit(nc.scalar, lambda i=i, db=db, cv=cv: nc.scalar.activation(
                s_uct[i].ap, cv.ap, AF.Silu, bias=s_cb.ap[:, db:db + 1]),
                "s", writes=[s_uct[i]], reads=[cv, s_cb])
            ucb_spill[db] = emit(nc.sync, lambda i=i, db=db: nc.sync.dma_start(
                sc_ucb[db * 128:(db + 1) * 128, :], s_uct[i].ap), f"wU{i}",
                reads=[s_uct[i]], dk=16)

        for fi in range(32):
            i = fi % 2
            items = []
            for db in range(NDD):
                for q in range(4):
                    items.append((pu[i].ap[:, q * 512:(q + 1) * 512],
                                  s_win.ap[:, db * 4096 + fi * 128: db * 4096 + (fi + 1) * 128],
                                  s_xhatT.ap[:, db * T + q * 512: db * T + (q + 1) * 512],
                                  db == 0, db == NDD - 1))
            mm_group(pu[i], items, reads=[s_win, s_xhatT])
            if fi < 16:
                db = fi
                emit(nc.scalar, lambda i=i: nc.scalar.copy(s_u[i].ap, pu[i].ap), "s",
                     writes=[s_u[i]], reads=[pu[i]])
                if pend_silu is not None:
                    do_silu(pend_silu)
                    pend_silu = None
                cv = s_cv[i]
                emit(nc.vector, lambda i=i, db=db, cv=cv: nc.vector.tensor_scalar(
                    cv.ap, s_u[i].ap, s_cw.ap[:, db * 4 + 3:db * 4 + 4], None,
                    op0=ALU.mult), "v", writes=[cv], reads=[s_u[i], s_cw])
                for k in range(3):
                    sh = 3 - k
                    emit(nc.vector, lambda i=i, db=db, k=k, sh=sh, cv=cv:
                         nc.vector.scalar_tensor_tensor(
                             cv.ap[:, sh:T], s_u[i].ap[:, 0:T - sh],
                             s_cw.ap[:, db * 4 + k:db * 4 + k + 1],
                             cv.ap[:, sh:T], op0=ALU.mult, op1=ALU.add),
                         "v", writes=[cv], reads=[s_u[i], s_cw])
                pend_silu = (i, db)
            else:
                if pend_silu is not None:
                    do_silu(pend_silu)
                    pend_silu = None
                db = fi - 16
                if pend_zsilu is not None:
                    do_zsilu(pend_zsilu)
                pend_zsilu = (i, db)
            if fi < 24:
                gamma_tile(2 * fi)
                gamma_tile(2 * fi + 1)
            else:
                if fi == 24:
                    do_thresholds()
                    mk_qbufs()
                for _ in range(6):
                    qstep()

        if pend_zsilu is not None:
            do_zsilu(pend_zsilu)
            pend_zsilu = None
        while qstate["k"] < len(qtiles):
            qstep()
        barrier()
    stWin.close()

    qfence = [("v", pr.cnt["v"]), ("s", pr.cnt["s"])]
    for _qi in range(4):
        if pr.cnt.get(f"wQ{_qi}", 0):
            qfence.append((f"wQ{_qi}", pr.cnt[f"wQ{_qi}"]))
    FENCE[0] = qfence

    stW = contextlib.ExitStack()
    s_wout = sb("s_wout", (128, NDB * 1024), BF16, stW)
    for wk in range(8):
        emit(nc.gpsimd, lambda wk=wk: nc.gpsimd.dma_start(
            s_wout.ap[:, wk * 2048:(wk + 1) * 2048],
            d_wout[:, wk * 2048:(wk + 1) * 2048]), None,
            writes=[s_wout], dk=16)
    with contextlib.ExitStack() as st2:
        s_wx = sb("s_wx", (128, NDB * 160), BF16, st2)
        for wk in range(4):
            emit(nc.gpsimd, lambda wk=wk: nc.gpsimd.dma_start(
                s_wx.ap[:, wk * 640:(wk + 1) * 640],
                d_wx[:, wk * 640:(wk + 1) * 640]), None,
                writes=[s_wx], dk=16)
        p_dtr = ps("p_dtr", (RK, T), st2)
        p_bc = ps("p_bc", (32, T), st2)
        s_uc2 = [sb(f"s_uc2{i}", (128, T), BF16, st2) for i in range(4)]

        def load_uc2(db):
            if db >= NDB:
                return
            i = db % 4
            qe = nc.sync if db % 2 == 0 else nc.scalar
            emit(qe, lambda db=db, i=i, qe=qe: qe.dma_start(
                s_uc2[i].ap, sc_ucb[db * 128:(db + 1) * 128, :]), None,
                writes=[s_uc2[i]],
                extra=[("wU0", pr.cnt.get("wU0", 0)),
                       ("wU1", pr.cnt.get("wU1", 0))], dk=16)

        for db in range(3):
            load_uc2(db)
        for db in range(NDB):
            i = db % 4
            load_uc2(db + 3)
            deps = _deps_for([p_dtr, p_bc] if db == 0 else [], [s_wx, s_uc2[i]], [])
            pr.wait(nc.tensor, "t", deps)
            last = None
            for q in range(4):
                last = nc.tensor.matmul(
                    p_dtr.ap[:, q * 512:(q + 1) * 512],
                    s_wx.ap[:, db * 160: db * 160 + 128],
                    s_uc2[i].ap[:, q * 512:(q + 1) * 512],
                    start=(db == 0), stop=(db == NDB - 1))
            for q in range(4):
                last = nc.tensor.matmul(
                    p_bc.ap[:, q * 512:(q + 1) * 512],
                    s_wx.ap[:, db * 160 + 128: db * 160 + 160],
                    s_uc2[i].ap[:, q * 512:(q + 1) * 512],
                    start=(db == 0), stop=(db == NDB - 1))
            tok2 = pr.inc(last, "t", 1)
            s_uc2[i].users.append(tok2)
            if db == NDB - 1:
                p_dtr.ready = [tok2]
                p_bc.ready = [tok2]
        emit(nc.scalar, lambda: nc.scalar.copy(s_dtr.ap, p_dtr.ap), "s",
             writes=[s_dtr], reads=[p_dtr])
        s_bcv = sb("s_bcv", (32, T), BF16, st2)
        emit(nc.scalar, lambda: nc.scalar.copy(s_bcv.ap, p_bc.ap), "s",
             writes=[s_bcv], reads=[p_bc])
        b_spill = emit(nc.sync, lambda: nc.sync.dma_start(sc_b, s_bcv.ap[0:16, :]),
                       "wB", reads=[s_bcv], dk=16)
        c_spill = emit(nc.sync, lambda: nc.sync.dma_start(sc_c, s_bcv.ap[16:32, :]),
                       "wB", reads=[s_bcv], dk=16)
        barrier()

    with contextlib.ExitStack() as st3:
        _sb0 = sb("s_brepb0", (128, NS * TC), BF16, st3)
        s_brepb = [_sb0, _sb0]
        _sc0 = sb("s_crepb0", (128, NS * TC), BF16, st3)
        s_crepb = [_sc0, _sc0]
        s_e = [sb("s_e0", (128, TC), F32, st3)] * 2
        s_dt = [sb(f"s_dt{i}", (128, TC), F32, st3) for i in range(2)]
        s_wt = [sb("s_wt0", (128, TC), BF16, st3)] * 2
        s_da = [sb(f"s_da{i}", (128, TC), BF16, st3) for i in range(6)]
        s_dbufb = [sb(f"s_dbuf{i}", (128, NS * TC), BF16, st3) for i in range(2)]
        _sh0 = sb("s_hb0", (128, NS * TC), BF16, st3)
        s_hb = [_sh0, _sh0]
        s_uc3 = [sb(f"s_uc3{i}", (128, TC), BF16, st3) for i in range(2)]
        s_sz3 = [sb(f"s_sz3{i}", (128, TC), BF16, st3) for i in range(2)]
        s_g = sb("s_g", (128, NDB * TC), BF16, st3)
        p_p3 = [ps(f"p_p3{i}", (RK, TC), st3) for i in range(2)]
        p_p4 = ps("p_p4", (128, 1024), st3)
        p_y = [Buf(p_p4.ap[:, 0:TC]), Buf(p_p4.ap[:, TC:2 * TC])]
        s_x0g = [sb("s_x0g0", (128, D), F32, st3)] * 2
        s_x1t = [sb("s_x1t0", (128, D), F32, st3)] * 2
        s_xh1t = [sb("s_xh1t0", (128, D), BF16, st3)] * 2
        stats4 = [{k: sb(f"g4_{k}{i}", (128, 1), F32, st3) for k in STK}
                  for i in range(2)]
        stats5 = stats4

        s_wq4 = [sb(f"s_wq4_{i}", (128, NDD * 256), BF16, st3,
                    is_transpose_dest=True) for i in range(2)]
        s_xh1Tq = sb("s_xh1Tq", (128, NDD * 512), BF16, st3,
                     is_transpose_dest=True)
        s_abq = [sb(f"s_abq{i}", (128, 512), BF16, st3) for i in range(2)]
        s_abr = [sb(f"s_abr{i}", (128, 512), BF16, st3) for i in range(3)]
        s_sqb = [sb(f"s_sqb{i}", (128, 512), BF16, st3) for i in range(2)]
        p_a = [ps(f"p_a{i}", (128, 512), st3) for i in range(2)]
        p_s1 = ps("p_s1", (1, 512), st3)
        p_s2 = ps("p_s2", (1, 512), st3)
        s_mub = sb("s_mub", (128, 512), BF16, st3)
        s_rsb = sb("s_rsb", (128, 512), BF16, st3)
        s_mu1 = sb("s_mu1", (1, 512), BF16, st3)
        s_rs1 = sb("s_rs1", (1, 512), F32, st3)
        s_rs16 = sb("s_rs16", (1, 512), BF16, st3)
        s_t1 = sb("s_t15", (128, 512), F32, st3)
        s_aht = [sb(f"s_aht{i}", (128, 512), BF16, st3) for i in range(2)]
        tokqs = [(f"wQ{_qi}", pr.cnt.get(f"wQ{_qi}", 0)) for _qi in range(4)]
        ab_dbuf = [Buf(sc_ab[hb]) for hb in range(NHB)]

        def p5_load_slot(tq, g):
            if g >= 16:
                return
            slot = s_wq4[g % 2]
            ssem = bufsem(slot)
            for db in range(NDD):
                emit(nc.sync, lambda g=g, db=db, slot=slot: nc.sync.dma_start_transpose(
                    slot.ap[:, db * 256:(db + 1) * 256],
                    sc_wqfc[g * 256:(g + 1) * 256, db * 128:(db + 1) * 128]), ssem,
                    writes=[slot] if db == 0 else [],
                    extra=(tokqs + slot.ready + slot.users) if db == 0 else [], dk=16)
            slot.ready = [(ssem, pr.cnt[ssem])]

        def p5_xpose(tq):
            t0 = tq * 512
            hsem = bufsem(s_xh1Tq)
            tokh = ("wH1", pr.cnt.get("wH1", 0))
            for db in range(NDD):
                emit(nc.sync, lambda db=db, t0=t0: nc.sync.dma_start_transpose(
                    s_xh1Tq.ap[:, db * 512:(db + 1) * 512],
                    sc_xh1[t0:t0 + 512, db * 128:(db + 1) * 128]), hsem,
                    writes=[s_xh1Tq] if db == 0 else [],
                    extra=([tokh] + s_xh1Tq.ready + s_xh1Tq.users) if db == 0 else [],
                    dk=16)
            s_xh1Tq.ready = [(hsem, pr.cnt[hsem])]

        def p5_colsums(hb, sq_dep):
            i = hb % 2
            mm_group(p_s1, [(p_s1.ap, s_ones.ap, s_abq[i].ap,
                             hb == 0, hb == NHB - 1)],
                     reads=[s_ones, s_abq[i]], extra=sq_dep)
            mm_group(p_s2, [(p_s2.ap, s_ones.ap, s_sqb[i].ap,
                             hb == 0, hb == NHB - 1)],
                     reads=[s_ones, s_sqb[i]])

        def p5_gelu2(hb0):
            if hb0 > 0:
                p5_colsums(hb0 - 1, prev_sq[0])
            for hb in (hb0, hb0 + 1):
                i = hb % 2
                emit(nc.scalar, lambda i=i: nc.scalar.activation(
                    s_abq[i].ap, p_a[i].ap, AF.Gelu), "s",
                    writes=[s_abq[i]], reads=[p_a[i]])
                emit(nc.gpsimd, lambda i=i, hb=hb: nc.gpsimd.dma_start(
                    sc_ab[hb], s_abq[i].ap), f"wAB{i}",
                    writes=[ab_dbuf[hb]], reads=[s_abq[i]], dk=16)
            sqtoks = {}
            for hb in (hb0, hb0 + 1):
                i = hb % 2
                emit(nc.scalar, lambda i=i: nc.scalar.activation(
                    s_sqb[i].ap, s_abq[i].ap, AF.Square), "s",
                    writes=[s_sqb[i]], reads=[s_abq[i]])
                sqtoks[hb] = [("s", pr.cnt["s"])]
            p5_colsums(hb0, sqtoks[hb0])
            prev_sq[0] = sqtoks[hb0 + 1]

        def p5_mm2(tq, hb2):
            if hb2 > 0:
                p5_gelu2(2 * hb2 - 2)
            for hb in (2 * hb2, 2 * hb2 + 1):
                i = hb % 2
                slot = s_wq4[hb2 % 2]
                o = (hb - 2 * hb2) * 128
                items = [(p_a[i].ap,
                          slot.ap[:, db * 256 + o: db * 256 + o + 128],
                          s_xh1Tq.ap[:, db * 512:(db + 1) * 512],
                          db == 0, db == NDD - 1) for db in range(NDD)]
                mm_group(p_a[i], items, reads=[slot, s_xh1Tq])
            p5_load_slot(tq, hb2 + 2)

        def p5_reload(hb):
            if hb >= NHB:
                return
            j = hb % 3
            emit(nc.gpsimd, lambda hb=hb, j=j: nc.gpsimd.dma_start(
                s_abr[j].ap, sc_ab[hb]), None, writes=[s_abr[j]],
                reads=[ab_dbuf[hb]],
                extra=[("wAB0", pr.cnt.get("wAB0", 0)),
                       ("wAB1", pr.cnt.get("wAB1", 0))], dk=16)

        def p5_stats(tq):
            t0 = tq * 512
            p5_gelu2(NHB - 2)
            p5_colsums(NHB - 1, prev_sq[0])
            r_e2 = p_s2.ap[0:1, :]
            r_lv = p_a[0].ap[0:1, 0:512]
            emit(nc.vector, lambda: nc.vector.tensor_scalar(
                s_mu1.ap, p_s1.ap, 1.0 / 4096, None, op0=ALU.mult), "v",
                writes=[s_mu1], reads=[p_s1])
            emit(nc.vector, lambda r_e2=r_e2: nc.vector.tensor_scalar(
                r_e2, r_e2, 1.0 / 4096, None, op0=ALU.mult), "v",
                reads=[p_s2], extra=p_s2.ready)
            emit(nc.vector, lambda: nc.vector.tensor_mul(
                s_rs1.ap, s_mu1.ap, s_mu1.ap), "v", reads=[s_mu1],
                writes=[s_rs1])
            emit(nc.vector, lambda r_e2=r_e2: nc.vector.tensor_sub(
                r_e2, r_e2, s_rs1.ap), "v", reads=[p_s2, s_rs1],
                extra=[("v", pr.cnt["v"])])
            vtok = ("v", pr.cnt["v"])
            emit(nc.scalar, lambda t0=t0: nc.scalar.dma_start(
                sc_rows16[0:1, t0:t0 + 512], s_mu1.ap), "wMRa",
                reads=[s_mu1], dk=16)
            emit(nc.scalar, lambda r_lv=r_lv, r_e2=r_e2: nc.scalar.activation(
                r_lv, r_e2, AF.Ln, bias=EPS[0:1, :]), "s",
                reads=[p_a[0], s_consts], extra=[vtok])
            emit(nc.scalar, lambda r_lv=r_lv: nc.scalar.activation(
                s_rs16.ap, r_lv, AF.Exp, scale=-0.5), "s",
                writes=[s_rs16], reads=[p_a[0]],
                extra=[("s", pr.cnt["s"])] + s_rs16.users)
            emit(nc.scalar, lambda t0=t0: nc.scalar.dma_start(
                sc_rows16[1:2, t0:t0 + 512], s_rs16.ap), "wMRb",
                reads=[s_rs16], dk=16)
            mrtoks = [("wMRa", pr.cnt["wMRa"]), ("wMRb", pr.cnt["wMRb"])]
            emit(nc.scalar, lambda t0=t0: nc.scalar.dma_start(
                s_mub.ap, bcastP(sc_rows16[0:1, t0:t0 + 512])), None,
                writes=[s_mub], extra=mrtoks, dk=16)
            emit(nc.scalar, lambda t0=t0: nc.scalar.dma_start(
                s_rsb.ap, bcastP(sc_rows16[1:2, t0:t0 + 512])), None,
                writes=[s_rsb], extra=mrtoks, dk=16)
            for pb in (p_s1, p_s2, p_a[0], p_a[1]):
                pb.users.append(("v", pr.cnt["v"]))
                pb.users.append(("s", pr.cnt["s"]))
                pb.users.append(("wMRa", pr.cnt["wMRa"]))
                pb.users.append(("wMRb", pr.cnt["wMRb"]))
            for hb in range(3):
                p5_reload(hb)

        def p5_apply4(tq, k):
            t0 = tq * 512
            for hb in range(4 * k, 4 * k + 4):
                j = hb % 3
                i = hb % 2
                emit(nc.vector, lambda j=j: nc.vector.tensor_sub(
                    s_t1.ap, s_abr[j].ap, s_mub.ap), "v",
                    writes=[s_t1], reads=[s_abr[j], s_mub])
                emit(nc.vector, lambda: nc.vector.tensor_mul(
                    s_t1.ap, s_t1.ap, s_rsb.ap), "v",
                    writes=[s_t1], reads=[s_rsb], extra=s_t1.ready)
                emit(nc.scalar, lambda i=i, hb=hb: nc.scalar.activation(
                    s_aht[i].ap, s_t1.ap, AF.Identity,
                    scale=s_lnw.ap[:, hb:hb + 1], bias=s_lnb.ap[:, hb:hb + 1]),
                    "s", writes=[s_aht[i]], reads=[s_t1, s_lnw, s_lnb])
                emit(nc.scalar, lambda i=i, hb=hb, t0=t0: nc.scalar.dma_start(
                    sc_ahat[hb:hb + 1, :, t0:t0 + 512], s_aht[i].ap), f"wA{i}",
                    reads=[s_aht[i]], dk=16)
                p5_reload(hb + 3)

        def p5_steps(tq):
            steps = [lambda tq=tq: p5_xpose(tq),
                     lambda tq=tq: p5_load_slot(tq, 0),
                     lambda tq=tq: p5_load_slot(tq, 1)]
            for hb2 in range(NHB // 2):
                steps.append(lambda tq=tq, hb2=hb2: p5_mm2(tq, hb2))
            steps.append(lambda tq=tq: p5_stats(tq))
            for k in range(NHB // 4):
                steps.append(lambda tq=tq, k=k: p5_apply4(tq, k))
            return steps

        prev_sq = [None]
        p5q = []
        p5_done = [0]

        def bc3(dst2d, src, t0):
            base = src[0:1, t0:t0 + TC]
            dims = [list(dd) for dd in base.ap]

            def ap_for(q):
                b4 = src[4 * q:4 * q + 1, t0:t0 + TC]
                return bass.AP(b4.tensor, b4.offset, [[0, 128], [T, 4], [1, TC]])
            return ap_for

        def load_reps(tc):
            if tc >= NCH:
                return
            t0 = tc * TC
            k = tc % 2
            bq = bc3(None, sc_b, t0)
            cq = bc3(None, sc_c, t0)
            bs, cs2 = bufsem(s_brepb[k]), bufsem(s_crepb[k])
            for q in range(4):
                emit(nc.scalar, lambda q=q, k=k, bq=bq: nc.scalar.dma_start(
                    s_brepb[k].ap[:, q * 4 * TC:(q + 1) * 4 * TC], bq(q)), bs,
                    writes=[s_brepb[k]] if q == 0 else [],
                    extra=([("wB", pr.cnt["wB"])] + qfence
                           + s_brepb[k].ready + s_brepb[k].users)
                    if q == 0 else [], dk=16)
                emit(nc.scalar, lambda q=q, k=k, cq=cq: nc.scalar.dma_start(
                    s_crepb[k].ap[:, q * 4 * TC:(q + 1) * 4 * TC], cq(q)), cs2,
                    writes=[s_crepb[k]] if q == 0 else [],
                    extra=([("wB", pr.cnt["wB"])] + qfence
                           + s_crepb[k].ready + s_crepb[k].users)
                    if q == 0 else [], dk=16)
            s_brepb[k].ready = [(bs, pr.cnt[bs])]
            s_crepb[k].ready = [(cs2, pr.cnt[cs2])]

        def load_uc_sz(tc, db):
            if db >= NDB:
                tc, db = tc + 1, db - NDB
            if tc >= NCH:
                return
            t0 = tc * TC
            i = db % 2
            emit(nc.sync, lambda db=db, i=i, t0=t0: nc.sync.dma_start(
                s_uc3[i].ap, sc_ucb[db * 128:(db + 1) * 128, t0:t0 + TC]), None,
                writes=[s_uc3[i]],
                extra=[("wU0", pr.cnt.get("wU0", 0)),
                       ("wU1", pr.cnt.get("wU1", 0))], dk=16)
            emit(nc.scalar, lambda db=db, i=i, t0=t0: nc.scalar.dma_start(
                s_sz3[i].ap, sc_sz[db * 128:(db + 1) * 128, t0:t0 + TC]), None,
                writes=[s_sz3[i]],
                extra=[("wZ0", pr.cnt.get("wZ0", 0)),
                       ("wZ1", pr.cnt.get("wZ1", 0))], dk=16)

        load_uc_sz(0, 0)
        pend_gate = None
        da_next = None

        def do_gate(pg):
            i, db = pg
            emit(nc.vector, lambda i=i, db=db: nc.vector.tensor_mul(
                s_g.ap[:, db * TC:(db + 1) * TC], p_y[i].ap, s_sz3[i].ap), "v",
                reads=[p_y[i], s_sz3[i]],
                extra=(s_g.ready + s_g.users) if db == 0 else [])

        for tc in range(NCH):
            t0 = tc * TC
            if tc == 0:
                load_reps(0)
            s_brep = s_brepb[tc % 2]
            s_crep = s_crepb[tc % 2]
            for db in range(NDB):
                i = db % 2
                if tc == 0 and db == 0:
                    mm_group(p_p3[0], [(p_p3[0].ap, s_wdt.ap[0:RK, 0:128],
                                       s_dtr.ap[:, 0:TC], True, True)],
                             reads=[s_wdt, s_dtr])
                    emit(nc.scalar, lambda: nc.scalar.activation(
                        s_e[0].ap, p_p3[0].ap, AF.Exp, bias=s_bdt.ap[:, 0:1]), "s",
                        writes=[s_e[0]], reads=[p_p3[0], s_bdt])
                    emit(nc.scalar, lambda: nc.scalar.activation(
                        s_dt[0].ap, s_e[0].ap, AF.Ln, bias=ONE), "s",
                        writes=[s_dt[0]], reads=[s_e[0], s_consts])
                emit(nc.vector, lambda i=i: nc.vector.tensor_mul(
                    s_wt[i].ap, s_dt[i].ap, s_uc3[i].ap), "v",
                    writes=[s_wt[i]], reads=[s_dt[i], s_uc3[i]])
                da_toks = da_next if da_next is not None else {}
                da_next = None

                def emit_da(n, i=i, db=db, da_toks=da_toks):
                    if n >= NS:
                        return
                    j = n % 6
                    da_toks[n] = emit(
                        nc.scalar, lambda i=i, j=j, db=db, n=n: nc.scalar.activation(
                            s_da[j].ap, s_dt[i].ap, AF.Exp,
                            scale=s_A.ap[:, db * NS + n: db * NS + n + 1]), "s",
                        writes=[s_da[j]], reads=[s_dt[i], s_A])

                for n in range(6):
                    if n not in da_toks:
                        emit_da(n)
                s_dbuf = s_dbufb[i]
                s_h = s_hb[i]
                dbuf_war = s_dbuf.ready + s_dbuf.users
                wtr = bass.AP(s_wt[i].ap.tensor, s_wt[i].ap.offset,
                              [[TC, 128], [0, NS], [1, TC]])
                emit(nc.vector, lambda s_brep=s_brep, s_dbuf=s_dbuf, wtr=wtr:
                     nc.vector.tensor_mul(
                         s_dbuf.ap.rearrange("p (n t) -> p n t", t=TC), wtr,
                         s_brep.ap.rearrange("p (n t) -> p n t", t=TC)),
                     "v", reads=[s_wt[i], s_brep], extra=dbuf_war)
                dbuf_tok = ("v", pr.cnt["v"])
                s_dbuf.ready = [dbuf_tok]
                s_dbuf.users = []
                ndb, ntc = db + 1, tc
                if ndb >= NDB:
                    ndb, ntc = 0, tc + 1
                if ntc < NCH:
                    nt0 = ntc * TC
                    ni = ndb % 2
                    mm_group(p_p3[ni],
                             [(p_p3[ni].ap,
                               s_wdt.ap[:, ndb * 128:(ndb + 1) * 128],
                               s_dtr.ap[:, nt0:nt0 + TC], True, True)],
                             reads=[s_wdt, s_dtr])
                    emit(nc.scalar, lambda ni=ni, ndb=ndb: nc.scalar.activation(
                        s_e[ni].ap, p_p3[ni].ap, AF.Exp,
                        bias=s_bdt.ap[:, ndb:ndb + 1]), "s",
                        writes=[s_e[ni]], reads=[p_p3[ni], s_bdt])
                    emit(nc.scalar, lambda ni=ni: nc.scalar.activation(
                        s_dt[ni].ap, s_e[ni].ap, AF.Ln, bias=ONE), "s",
                        writes=[s_dt[ni]], reads=[s_e[ni], s_consts])
                h_war = s_h.ready + s_h.users
                for n in range(NS):
                    j = n % 6
                    if tc == 0:
                        init = 0.0
                    else:
                        init = s_carry.ap[:, db * NS + n: db * NS + n + 1]
                    emit(nc.vector, lambda j=j, n=n, init=init, s_h=s_h, s_dbuf=s_dbuf:
                         nc.vector.tensor_tensor_scan(
                             s_h.ap[:, n * TC:(n + 1) * TC], s_da[j].ap,
                             s_dbuf.ap[:, n * TC:(n + 1) * TC], init,
                             op0=ALU.mult, op1=ALU.add),
                         "v", reads=[s_da[j]] + ([s_carry] if tc > 0 else []),
                         extra=[da_toks[n]]
                         + ((h_war + [dbuf_tok]) if n == 0 else []))
                    emit_da(n + 6)
                htok = ("v", pr.cnt["v"])
                s_h.ready = [htok]
                s_h.users = []
                if pend_gate is not None:
                    do_gate(pend_gate)
                    pend_gate = None
                hview = s_h.ap.rearrange("p (n t) -> p n t", t=TC)
                carry_tok = emit(nc.scalar, lambda db=db, hview=hview:
                    nc.scalar.copy(
                        s_carry.ap[:, db * NS:(db + 1) * NS].rearrange(
                            "p (n o) -> p n o", o=1), hview[:, :, TC - 1:TC]), "s",
                    writes=[s_carry], reads=[s_h])
                emit(nc.vector, lambda s_crep=s_crep, s_h=s_h, s_dbuf=s_dbuf:
                    nc.vector.tensor_mul(
                        s_dbuf.ap, s_h.ap, s_crep.ap), "v",
                    reads=[s_h, s_crep],
                    extra=s_dbuf.ready + s_dbuf.users)
                hctok = ("v", pr.cnt["v"])
                s_dbuf.ready = [hctok]
                s_dbuf.users = []
                s_h.users.append(hctok)
                items = [(p_y[i].ap, s_id16.ap, s_dbuf.ap[:, n * TC:(n + 1) * TC],
                          n == 0, False) for n in range(NS)]
                items.append((p_y[i].ap, s_diagD.ap[:, db * 128:(db + 1) * 128],
                              s_uc3[i].ap, False, True))
                mm_group(p_y[i], items, reads=[s_dbuf, s_id16, s_diagD, s_uc3[i]])
                load_uc_sz(tc, db + 1)
                pend_gate = (i, db)
                ndb2, ntc2 = db + 1, tc
                if ndb2 >= NDB:
                    ndb2, ntc2 = 0, tc + 1
                if ntc2 < NCH:
                    da_next = {}

                    def emit_da_pre(n, i2=ndb2 % 2, db2=ndb2, dd=da_next):
                        j = n % 6
                        dd[n] = emit(
                            nc.scalar,
                            lambda: nc.scalar.activation(
                                s_da[j].ap, s_dt[i2].ap, AF.Exp,
                                scale=s_A.ap[:, db2 * NS + n: db2 * NS + n + 1]),
                            "s", writes=[s_da[j]], reads=[s_dt[i2], s_A])

                    for n in range(3):
                        emit_da_pre(n)
                want = ((db + 1) * 28 + 19) // 20
                while p5q and p5_done[0] < want:
                    p5q.pop(0)()
                    p5_done[0] += 1
            if pend_gate is not None:
                do_gate(pend_gate)
                pend_gate = None
            load_reps(tc + 1)
            gtok = ("v", pr.cnt["v"])
            s_g.ready = [gtok]
            s_g.users = []
            for su in range(TC // 128):
                tglob = tc * (TC // 128) + su
                i = su % 2
                items = []
                for db in range(NDB):
                    stat = s_g.ap[:, db * TC + su * 128: db * TC + (su + 1) * 128]
                    for dmh in range(2):
                        items.append((p_p4.ap[:, dmh * 512:(dmh + 1) * 512], stat,
                                      s_wout.ap[:, db * 1024 + dmh * 512: db * 1024 + (dmh + 1) * 512],
                                      db == 0, db == NDB - 1))
                mm_group(p_p4, items, reads=[s_g, s_wout])
                emit(nc.sync, lambda tglob=tglob, i=i: nc.sync.dma_start(
                    s_x0g[i].ap, d_x[tglob * 128:(tglob + 1) * 128, :]), None,
                    writes=[s_x0g[i]], dk=16)
                ln_rows(p_p4.ap, p_p4, stats4[i], s_x1t[i], s_x1t[i].ap, D)
                emit(nc.vector, lambda i=i: nc.vector.tensor_add(
                    s_x1t[i].ap, s_x1t[i].ap, s_x0g[i].ap), "v",
                    writes=[s_x1t[i]], reads=[s_x0g[i]], extra=s_x1t[i].ready)
                emit(nc.sync, lambda tglob=tglob, i=i: nc.sync.dma_start(
                    sc_x1[tglob * 128:(tglob + 1) * 128, :], s_x1t[i].ap), "wX1",
                    reads=[s_x1t[i]], dk=16)
                ln_rows(s_x1t[i].ap, s_x1t[i], stats5[i], s_xh1t[i], s_xh1t[i].ap, D)
                emit(nc.sync, lambda tglob=tglob, i=i: nc.sync.dma_start(
                    sc_xh1[tglob * 128:(tglob + 1) * 128, :], s_xh1t[i].ap), "wH1",
                    reads=[s_xh1t[i]], dk=16)
                if p5q:
                    p5q.pop(0)()
                    p5_done[0] += 1
            for pyh in p_y:
                pyh.users.append(("s", pr.cnt["s"]))
                pyh.users.append(("v", pr.cnt["v"]))
            p5q.extend(p5_steps(tc))
            p5_done[0] = 0
        for f in p5q:
            f()
        barrier()
    stW.close()

    with contextlib.ExitStack() as st6:
        s_wqprojT = sb("s_wqprojT", (128, NHB * 1024), BF16, st6,
                       is_transpose_dest=True)
        s_ah6 = [sb(f"s_ah6{i}", (128, NHB * 128), BF16, st6) for i in range(2)]
        p_p6b = [ps(f"p_p6{i}", (128, 1024), st6) for i in range(2)]
        s_v6 = [sb(f"s_v6{i}", (128, D), F32, st6) for i in range(2)]
        s_x1r = [sb(f"s_x1r{i}", (128, D), F32, st6) for i in range(2)]
        s_o = [sb(f"s_o{i}", (128, D), F32, st6) for i in range(2)]
        stats6 = [{k: sb(f"g6_{k}{i}", (128, 1), F32, st6) for k in STK}
                  for i in range(2)]

        def load_p6(su):
            if su >= TSUB:
                return
            i = su % 2
            emit(nc.sync, lambda su=su, i=i: nc.sync.dma_start(
                s_ah6[i].ap, sc_ahat[:, :, su * 128:(su + 1) * 128].rearrange(
                    "h p t -> p h t")), None, writes=[s_ah6[i]], dk=16)
            emit(nc.scalar, lambda su=su, i=i: nc.scalar.dma_start(
                s_x1r[i].ap, sc_x1[su * 128:(su + 1) * 128, :]), None,
                writes=[s_x1r[i]], dk=16)

        load_p6(0)
        tokps = [(f"wQ{_qi}", pr.cnt.get(f"wQ{_qi}", 0)) for _qi in range(4)]
        qpsem = bufsem(s_wqprojT)
        pj_tok = {}
        for hb in range(NHB):
            emit(nc.sync, lambda hb=hb: nc.sync.dma_start_transpose(
                s_wqprojT.ap[:, hb * 1024:(hb + 1) * 1024],
                sc_wqproj[:, hb * 128:(hb + 1) * 128]), qpsem, extra=tokps, dk=16)
            pj_tok[hb] = (qpsem, pr.cnt[qpsem])
        s_wqprojT.ready = [(qpsem, pr.cnt[qpsem])]

        for su in range(TSUB):
            i = su % 2
            p_p6 = p_p6b[i]
            load_p6(su + 1)
            items = []
            for hb in range(NHB):
                stat = s_ah6[i].ap[:, hb * 128:(hb + 1) * 128]
                for dmh in range(2):
                    items.append((p_p6.ap[:, dmh * 512:(dmh + 1) * 512], stat,
                                  s_wqprojT.ap[:, hb * 1024 + dmh * 512: hb * 1024 + (dmh + 1) * 512],
                                  hb == 0, hb == NHB - 1))
            if su == 0:
                deps = _deps_for([p_p6], [s_ah6[i]], [])
                pr.wait(nc.tensor, "t", deps)
                last = None
                for hb in range(NHB):
                    pr.wait(nc.tensor, "t", [pj_tok[hb]])
                    for dmh in range(2):
                        it = items[hb * 2 + dmh]
                        last = nc.tensor.matmul(it[0], it[1], it[2],
                                                start=it[3], stop=it[4])
                tok6 = pr.inc(last, "t", 1)
                p_p6.ready = [tok6]
                p_p6.users = []
                s_ah6[i].users.append(tok6)
                s_wqprojT.users.append(tok6)
            else:
                mm_group(p_p6, items, reads=[s_ah6[i], s_wqprojT])
            ln_rows(p_p6.ap, p_p6, stats6[i], s_v6[i], s_v6[i].ap, D)
            emit(nc.vector, lambda i=i: nc.vector.tensor_add(
                s_o[i].ap, s_v6[i].ap, s_x1r[i].ap), "v",
                writes=[s_o[i]], reads=[s_v6[i], s_x1r[i]])
            emit(nc.sync, lambda su=su, i=i: nc.sync.dma_start(
                d_out[su * 128:(su + 1) * 128, :], s_o[i].ap), f"dO{i}",
                reads=[s_o[i]], dk=16)

    nc.sync.wait_ge(pr.sem("dO0"), pr.cnt["dO0"])
    nc.sync.wait_ge(pr.sem("dO1"), pr.cnt["dO1"])
    return nc


def _prep_common(inputs):
    f = np.float32

    def r(a):
        return np.ascontiguousarray(a)

    W_in = np.asarray(inputs["L0_attn_W_in"], f)
    W_x = np.asarray(inputs["L0_attn_W_x"], f)
    W_dt = np.asarray(inputs["L0_attn_W_dt"], f)
    W_out = np.asarray(inputs["L0_attn_W_out"], f)
    conv_w = np.asarray(inputs["L0_attn_conv_w"], f)
    conv_b = np.asarray(inputs["L0_attn_conv_b"], f)
    b_dt = np.asarray(inputs["L0_attn_b_dt"], f)
    A_log = np.asarray(inputs["L0_attn_A_log"], f)
    D_p = np.asarray(inputs["L0_attn_D_param"], f)
    Wfc = np.asarray(inputs["L0_ffn_W_fc"], f)
    lnw = np.asarray(inputs["L0_ffn_ln_w"], f)
    lnb = np.asarray(inputs["L0_ffn_ln_b"], f)
    Wproj = np.asarray(inputs["L0_ffn_W_proj"], f)

    consts = np.zeros((128, 4), f)
    consts[:, 0] = 1.0
    consts[:, 1] = 1e-5
    return {
        "w_in_r": r(W_in.reshape(NDD, 128, 4096).transpose(1, 0, 2).reshape(128, NDD * 4096)),
        "w_x_r": r(W_x.reshape(NDB, 128, 160).transpose(1, 0, 2).reshape(128, NDB * 160)),
        "w_dt": r(W_dt),
        "w_out_r": r(W_out.reshape(NDB, 128, 1024).transpose(1, 0, 2).reshape(128, NDB * 1024)),
        "conv_w_r": r(conv_w.reshape(NDB, 128, 4).transpose(1, 0, 2).reshape(128, NDB * 4)),
        "conv_b_r": r(conv_b.reshape(NDB, 128).T),
        "b_dt_r": r(b_dt.reshape(NDB, 128).T),
        "a_log_r": r(A_log.reshape(NDB, 128, NS).transpose(1, 0, 2).reshape(128, NDB * NS)),
        "d_r": r(D_p.reshape(NDB, 128).T),
        "wfc": r(Wfc),
        "wproj": r(Wproj),
        "lnw_r": r(lnw.reshape(NHB, 128).T),
        "lnb_r": r(lnb.reshape(NHB, 128).T),
        "consts": consts,
        "id16": np.eye(128, dtype=BFNP),
        "ones_col16": np.ones((128, 1), dtype=BFNP),
    }


def kernel(**inputs):
    x_all = np.asarray(inputs["L0_attn_x"], np.float32)
    common = _prep_common(inputs)
    nc = build()
    in_maps = [dict(common, x=np.ascontiguousarray(x_all[i])) for i in range(8)]
    res = run_bass_kernel_spmd(nc, in_maps, core_ids=list(range(8)))
    return np.stack([np.asarray(res.results[i]["out"]) for i in range(8)], axis=0)


# version 60
# speedup vs baseline: 1.1449x; 1.1449x over previous
import contextlib
import numpy as np
import ml_dtypes
import concourse.bass as bass
import concourse.mybir as mybir
from concourse.bass_utils import run_bass_kernel_spmd

F32 = mybir.dt.float32
BF16 = mybir.dt.bfloat16
AF = mybir.ActivationFunctionType
ALU = mybir.AluOpType
AX = mybir.AxisListType
BFNP = ml_dtypes.bfloat16

T, D, DI, NS, RK = 2048, 1024, 2048, 16, 128
NDB, NDD, NHB = 16, 8, 32
TSUB = 16
TC, NCH = 512, 4
NW = 4096 * 1024
STK = ("ssq", "ssum", "mu", "e2", "mu2", "var", "lv", "rstd", "nb")


class Pr:

    def __init__(self, nc):
        self.nc = nc
        self.h = {}
        self.cnt = {}
        self.seen = {}

    def sem(self, name):
        if name not in self.h:
            self.h[name] = self.nc.alloc_semaphore("sem_" + name)
            self.cnt[name] = 0
        return self.h[name]

    def wait(self, eng, ename, tokens):
        need = {}
        for (n, v) in tokens:
            if v > need.get(n, 0):
                need[n] = v
        for n, v in sorted(need.items()):
            k = (ename, n)
            if self.seen.get(k, -1) >= v:
                continue
            eng.wait_ge(self.sem(n), v)
            self.seen[k] = v

    def inc(self, ins, name, k=1):
        s = self.sem(name)
        ins.then_inc(s, k)
        self.cnt[name] += k
        return (name, self.cnt[name])


class Buf:
    __slots__ = ("ap", "ready", "users")

    def __init__(self, ap):
        self.ap = ap
        self.ready = []
        self.users = []


_BUFSEM = {}


def bufsem(buf):
    if id(buf) not in _BUFSEM:
        _BUFSEM[id(buf)] = f"dm{len(_BUFSEM)}"
    return _BUFSEM[id(buf)]


def _deps_for(writes, reads, extra):
    deps = list(extra)
    for b in reads:
        deps += b.ready
    for b in writes:
        deps += b.ready + b.users
    return deps


def build(taps=()):
    taps = set(taps)
    _BUFSEM.clear()
    nc = bass.Bass("TRN2", target_bir_lowering=False, debug=False)
    pr = Pr(nc)

    def din(name, shape, dt):
        return nc.dram_tensor(name, list(shape), dt, kind="ExternalInput").ap()

    def dscr(name, shape, dt):
        kind = "ExternalOutput" if name in taps else "Internal"
        return nc.dram_tensor(name, list(shape), dt, kind=kind).ap()

    d_x = din("x", (T, D), F32)
    d_win = din("w_in_r", (128, NDD * 4096), F32)
    d_wx = din("w_x_r", (128, NDB * 160), F32)
    d_wdt = din("w_dt", (RK, DI), F32)
    d_wout = din("w_out_r", (128, NDB * 1024), F32)
    d_cw = din("conv_w_r", (128, NDB * 4), F32)
    d_cb = din("conv_b_r", (128, NDB), F32)
    d_bdt = din("b_dt_r", (128, NDB), F32)
    d_alog = din("a_log_r", (128, NDB * NS), F32)
    d_D = din("d_r", (128, NDB), F32)
    d_wfc = din("wfc", (4096, 1024), F32)
    d_wproj = din("wproj", (1024, 4096), F32)
    d_lnw = din("lnw_r", (128, NHB), F32)
    d_lnb = din("lnb_r", (128, NHB), F32)
    d_consts = din("consts", (128, 4), F32)
    d_id16 = din("id16", (128, 128), BF16)
    d_ones = din("ones_col16", (128, 1), BF16)

    sc_xhat = dscr("sc_xhat", (T, D), BF16)
    sc_ucb = dscr("sc_ucb", (DI, T), BF16)
    sc_sz = dscr("sc_sz", (DI, T), BF16)
    sc_b = dscr("sc_b", (NS, T), BF16)
    sc_c = dscr("sc_c", (NS, T), BF16)
    sc_x1 = dscr("sc_x1", (T, D), F32)
    sc_xh1 = dscr("sc_xh1", (T, D), BF16)
    sc_wqfc = dscr("sc_wqfc", (4096, 1024), BF16)
    sc_wqproj = dscr("sc_wqproj", (1024, 4096), BF16)
    sc_ahat = dscr("sc_ahat", (NHB, 128, T), BF16)
    sc_ab = dscr("sc_ab", (NHB, 128, 512), BF16)
    sc_rows16 = dscr("sc_rows16", (2, T), BF16)
    sc_rows = dscr("sc_rows", (4, T), F32)
    d_out = nc.dram_tensor("out", [T, D], F32, kind="ExternalOutput").ap()

    ENG = {}
    for nm, e in (("v", nc.vector), ("s", nc.scalar), ("t", nc.tensor),
                  ("g", nc.gpsimd), ("y", nc.sync)):
        ENG[id(e)] = nm

    def emit(eng, make, sem, writes=(), reads=(), extra=(), dk=1):
        deps = _deps_for(writes, reads, extra)
        pr.wait(eng, ENG[id(eng)], deps)
        ins = make()
        if sem is None:
            assert dk == 16 and writes
            sem = bufsem(writes[0])
        tok = pr.inc(ins, sem, dk)
        for b in writes:
            b.ready = [tok]
            b.users = []
        for b in reads:
            b.users.append(tok)
        return tok

    FENCE = [[]]

    def sb(name, shape, dt, stack=None, **kw):
        if stack is None:
            b = Buf(nc.alloc_sbuf_tensor(name, list(shape), dt, **kw).ap())
        else:
            h = stack.enter_context(nc.sbuf_tensor(name, list(shape), dt, **kw))
            b = Buf(h.ap())
        b.ready = list(FENCE[0])
        return b

    def ps(name, shape, stack):
        h = stack.enter_context(nc.psum_tensor(name, list(shape), F32))
        return Buf(h.ap())

    def bcastP(ap2d, p=128):
        assert ap2d.shape[0] == 1, ap2d.shape
        dims = [list(dd) for dd in ap2d.ap]
        return bass.AP(ap2d.tensor, ap2d.offset, [[0, p]] + dims[1:])

    def mm_group(pbuf, items, reads, extra=()):
        deps = _deps_for([pbuf], reads, extra)
        pr.wait(nc.tensor, "t", deps)
        last = None
        for (o, l, r, st, sp) in items:
            last = nc.tensor.matmul(o, l, r, start=st, stop=sp)
        tok = pr.inc(last, "t", 1)
        pbuf.ready = [tok]
        pbuf.users = []
        for b in reads:
            b.users.append(tok)
        return tok

    def barrier():
        toks = [(n, c) for n, c in pr.cnt.items() if c > 0]
        for nm, e in (("v", nc.vector), ("s", nc.scalar), ("t", nc.tensor),
                      ("g", nc.gpsimd), ("y", nc.sync)):
            pr.wait(e, nm, toks)

    s_cw = sb("s_cw", (128, NDB * 4), F32)
    s_cb = sb("s_cb", (128, NDB), F32)
    s_bdt = sb("s_bdt", (128, NDB), F32)
    s_A = sb("s_A", (128, NDB * NS), F32)
    s_D = sb("s_D", (128, NDB), F32)
    s_lnw = sb("s_lnw", (128, NHB), F32)
    s_lnb = sb("s_lnb", (128, NHB), F32)
    s_consts = sb("s_consts", (128, 4), F32)
    s_id16 = sb("s_id16", (128, 128), BF16)
    s_ones = sb("s_ones", (128, 1), BF16)
    s_diagD = sb("s_diagD", (128, NDB * 128), BF16)
    s_dtr = sb("s_dtr", (RK, DI), BF16)
    s_carry = sb("s_carry", (128, NDB * NS), BF16)
    s_dump = sb("s_dump", (128, 1024), F32)
    s_wdt = sb("s_wdt", (RK, DI), BF16)
    s_gacc = sb("s_gacc", (128, 80), F32)
    s_thr = {"fc": sb("s_thrpfc", (128, 2), F32), "pj": sb("s_thrppj", (128, 2), F32)}

    ONE = s_consts.ap[:, 0:1]
    EPS = s_consts.ap[:, 1:2]

    for src, dst in ((d_cw, s_cw), (d_cb, s_cb), (d_bdt, s_bdt), (d_alog, s_A),
                     (d_D, s_D), (d_lnw, s_lnw), (d_lnb, s_lnb), (d_consts, s_consts),
                     (d_id16, s_id16), (d_ones, s_ones)):
        emit(nc.sync, lambda s=src, d=dst: nc.sync.dma_start(d.ap, s), None,
             writes=[dst], dk=16)
    emit(nc.gpsimd, lambda: nc.gpsimd.dma_start(s_wdt.ap, d_wdt), None,
         writes=[s_wdt], dk=16)

    emit(nc.scalar, lambda: nc.scalar.activation(s_A.ap, s_A.ap, AF.Exp),
         "s", reads=[s_A], extra=s_A.ready)
    s_A.ready = [("s", pr.cnt["s"])]
    emit(nc.vector, lambda: nc.vector.tensor_scalar(s_A.ap, s_A.ap, -1.0, None,
         op0=ALU.mult), "v", reads=[s_A], extra=s_A.ready)
    s_A.ready = [("v", pr.cnt["v"])]
    for db in range(NDB):
        emit(nc.vector, lambda db=db: nc.vector.tensor_scalar(
            s_diagD.ap[:, db * 128:(db + 1) * 128], s_id16.ap,
            s_D.ap[:, db:db + 1], None, op0=ALU.mult),
            "v", writes=[s_diagD], reads=[s_id16, s_D])

    def ln_rows(src_ap, src_buf, st, out_buf, out_ap, nfeat):
        inv = 1.0 / nfeat
        emit(nc.scalar, lambda: nc.scalar.activation(
            s_dump.ap[:, 0:nfeat], src_ap, AF.Square, accum_out=st["ssq"].ap),
            "s", writes=[st["ssq"]], reads=[src_buf])
        emit(nc.vector, lambda: nc.vector.reduce_sum(st["ssum"].ap, src_ap, axis=AX.X),
            "v", writes=[st["ssum"]], reads=[src_buf], extra=st["ssq"].ready)
        emit(nc.vector, lambda: nc.vector.tensor_scalar(st["mu"].ap, st["ssum"].ap,
            inv, None, op0=ALU.mult), "v", writes=[st["mu"]], reads=[st["ssum"]])
        emit(nc.vector, lambda: nc.vector.tensor_scalar(st["e2"].ap, st["ssq"].ap,
            inv, None, op0=ALU.mult), "v", writes=[st["e2"]], reads=[st["ssq"]])
        emit(nc.vector, lambda: nc.vector.tensor_mul(st["mu2"].ap, st["mu"].ap,
            st["mu"].ap), "v", writes=[st["mu2"]], reads=[st["mu"]])
        emit(nc.vector, lambda: nc.vector.tensor_sub(st["var"].ap, st["e2"].ap,
            st["mu2"].ap), "v", writes=[st["var"]], reads=[st["e2"], st["mu2"]])
        emit(nc.scalar, lambda: nc.scalar.activation(st["lv"].ap, st["var"].ap,
            AF.Ln, bias=EPS), "s", writes=[st["lv"]], reads=[st["var"], s_consts])
        emit(nc.scalar, lambda: nc.scalar.activation(st["rstd"].ap, st["lv"].ap,
            AF.Exp, scale=-0.5), "s", writes=[st["rstd"]], reads=[st["lv"]])
        emit(nc.vector, lambda: nc.vector.scalar_tensor_tensor(st["nb"].ap,
            st["mu"].ap, -1.0, st["rstd"].ap, op0=ALU.mult, op1=ALU.mult),
            "v", writes=[st["nb"]], reads=[st["mu"], st["rstd"]])
        emit(nc.scalar, lambda: nc.scalar.activation(out_ap, src_ap, AF.Identity,
            scale=st["rstd"].ap[:, 0:1], bias=st["nb"].ap[:, 0:1]),
            "s", writes=[out_buf], reads=[src_buf, st["rstd"], st["nb"]])

    stWin = contextlib.ExitStack()
    s_win = sb("s_win", (128, NDD * 4096), BF16, stWin)
    for wk in range(8):
        emit(nc.gpsimd, lambda wk=wk: nc.gpsimd.dma_start(
            s_win.ap[:, wk * 4096:(wk + 1) * 4096],
            d_win[:, wk * 4096:(wk + 1) * 4096]), None,
            writes=[s_win], dk=16)

    s_xhatT = sb("s_xhatT", (128, NDD * T), BF16, stWin)
    with contextlib.ExitStack() as stE:
        NE = 4
        s_xe = [sb(f"s_xe{i}", (128, D), F32, stE) for i in range(NE)]
        s_xh = [sb(f"s_xh{i}", (128, D), BF16, stE) for i in range(2)]
        statsE = [{k: sb(f"e_{k}{i}", (128, 1), F32, stE) for k in STK}
                  for i in range(2)]
        p_xt = [ps(f"p_xt{i}", (128, 128), stE) for i in range(2)]
        xtsem = bufsem(s_xhatT)

        def load_xe(ts):
            if ts < TSUB:
                qe = nc.sync if ts % 2 == 0 else nc.scalar
                emit(qe, lambda ts=ts, qe=qe: qe.dma_start(
                    s_xe[ts % NE].ap, d_x[ts * 128:(ts + 1) * 128, :]), None,
                    writes=[s_xe[ts % NE]], dk=16)

        for ts in range(NE - 1):
            load_xe(ts)
        for ts in range(TSUB):
            i = ts % 2
            load_xe(ts + NE - 1)
            ln_rows(s_xe[ts % NE].ap, s_xe[ts % NE], statsE[i], s_xh[i], s_xh[i].ap, D)
            for dd in range(NDD):
                pb = p_xt[dd % 2]
                mm_group(pb, [(pb.ap, s_xh[i].ap[:, dd * 128:(dd + 1) * 128],
                               s_id16.ap, True, True)],
                         reads=[s_xh[i], s_id16])
                emit(nc.vector, lambda pb=pb, dd=dd, ts=ts: nc.vector.tensor_copy(
                    s_xhatT.ap[:, dd * T + ts * 128: dd * T + (ts + 1) * 128],
                    pb.ap), xtsem, reads=[pb])
        s_xhatT.ready = [(xtsem, pr.cnt[xtsem])]
        barrier()

    ucb_spill = {}
    sz_spill = {}
    with contextlib.ExitStack() as st1:
        pu = [ps(f"pu{i}", (128, T), st1) for i in range(2)]
        s_u = [sb(f"s_u{i}", (128, T), F32, st1) for i in range(2)]
        s_cv = [sb(f"s_cv{i}", (128, T), F32, st1) for i in range(2)]
        s_uct = [sb(f"s_uct{i}", (128, T), BF16, st1) for i in range(2)]
        s_szt = [sb(f"s_szt{i}", (128, T), BF16, st1) for i in range(2)]
        s_gw = [sb(f"s_gw{i}", (128, 2048), F32, st1) for i in range(2)]
        s_dumpg = sb("s_dumpg", (128, 2048), F32, st1)

        g_tiles = []
        for k in range(32):
            g_tiles.append((d_wfc[k * 128:(k + 1) * 128, :], 1024))
        for k in range(8):
            g_tiles.append((d_wproj[k * 128:(k + 1) * 128, 0:2048], 2048))
            g_tiles.append((d_wproj[k * 128:(k + 1) * 128, 2048:4096], 2048))

        def gamma_tile(gi):
            src, w = g_tiles[gi]
            i = gi % 2
            emit(nc.sync, lambda: nc.sync.dma_start(s_gw[i].ap[:, 0:w], src), None,
                 writes=[s_gw[i]], dk=16)
            emit(nc.scalar, lambda: nc.scalar.activation(
                s_dumpg.ap[:, 0:w], s_gw[i].ap[:, 0:w], AF.Abs,
                accum_out=s_gacc.ap[:, gi:gi + 1]),
                "s", writes=[s_gacc], reads=[s_gw[i]])

        qtiles = []
        for nm, dsrc, ddst, rows, wdt in (("fc", d_wfc, sc_wqfc, 32, 1024),
                                          ("pj", d_wproj, sc_wqproj, 8, 4096)):
            for r in range(rows):
                for cs in range(0, wdt, 1024):
                    qtiles.append((nm, dsrc, ddst, r, cs))
        qstate = {"k": 0}
        s_qw = [None, None]
        s_qc = [None, None]
        s_qq = [None, None]

        def mk_qbufs():
            qqv = s_dumpg.ap.bitcast(BF16)
            for i in range(2):
                for lst, src, apv in (
                        (s_qw, s_gw[0], s_gw[0].ap[:, i * 1024:(i + 1) * 1024]),
                        (s_qc, s_gw[1], s_gw[1].ap[:, i * 1024:(i + 1) * 1024]),
                        (s_qq, s_dumpg, qqv[:, i * 1024:(i + 1) * 1024])):
                    b = Buf(apv)
                    b.ready = list(src.ready) + list(src.users)
                    lst[i] = b

        def qload(k):
            if k >= len(qtiles):
                return
            nm, dsrc, ddst, r, cs = qtiles[k]
            i = k % 2
            qeng = nc.sync if k % 2 == 0 else nc.scalar
            emit(qeng, lambda i=i, r=r, cs=cs, dsrc=dsrc, qeng=qeng: qeng.dma_start(
                s_qw[i].ap, dsrc[r * 128:(r + 1) * 128, cs:cs + 1024]), None,
                writes=[s_qw[i]], dk=16)

        def qstep():
            k = qstate["k"]
            if k >= len(qtiles):
                return
            qstate["k"] = k + 1
            if k == 0:
                qload(0)
                qload(1)
            nm, dsrc, ddst, r, cs = qtiles[k]
            i = k % 2
            thrp = s_thr[nm]
            emit(nc.vector, lambda i=i, thrp=thrp: nc.vector.tensor_scalar(
                s_qc[i].ap, s_qw[i].ap, thrp.ap[:, 0:1], thrp.ap[:, 1:2],
                op0=ALU.min, op1=ALU.max), "v",
                writes=[s_qc[i]], reads=[s_qw[i], thrp])
            emit(nc.vector, lambda i=i: nc.vector.tensor_sub(
                s_qc[i].ap, s_qw[i].ap, s_qc[i].ap), "v",
                writes=[s_qc[i]], reads=[s_qw[i]])
            qload(k + 2)
            emit(nc.scalar, lambda i=i: nc.scalar.activation(
                s_qq[i].ap, s_qc[i].ap, AF.Sign), "s",
                writes=[s_qq[i]], reads=[s_qc[i]])
            emit(nc.sync, lambda i=i, r=r, cs=cs, ddst=ddst: nc.sync.dma_start(
                ddst[r * 128:(r + 1) * 128, cs:cs + 1024], s_qq[i].ap),
                f"wQ{k % 4}", reads=[s_qq[i]], dk=16)

        def do_thresholds():
            for j, (nm, cols, off) in enumerate((("fc", 32, 0), ("pj", 16, 32))):
                g1 = sb(f"s_g1{nm}", (128, 1), F32, st1)
                emit(nc.vector, lambda cols=cols, off=off, g1=g1:
                     nc.vector.reduce_sum(
                         g1.ap, s_gacc.ap[:, off:off + cols], axis=AX.X), "v",
                     writes=[g1], reads=[s_gacc])
                emit(nc.sync, lambda g1=g1, j=j: nc.sync.dma_start(
                    sc_rows[2:3, j * 128:(j + 1) * 128],
                    g1.ap), "wR", reads=[g1], dk=16)
                gr = sb(f"s_gr{nm}", (1, 128), F32, st1)
                emit(nc.sync, lambda gr=gr, j=j: nc.sync.dma_start(
                    gr.ap, sc_rows[2:3, j * 128:(j + 1) * 128]),
                    None, writes=[gr], extra=g1.users, dk=16)
                gs = sb(f"s_gs{nm}", (1, 1), F32, st1)
                emit(nc.vector, lambda gr=gr, gs=gs: nc.vector.reduce_sum(
                    gs.ap, gr.ap, axis=AX.X), "v", writes=[gs], reads=[gr])
                thr2 = sb(f"s_thr2{nm}", (1, 2), F32, st1)
                emit(nc.vector, lambda gs=gs, thr2=thr2: nc.vector.tensor_scalar(
                    thr2.ap[:, 0:1], gs.ap, 0.5 / NW, 0.5e-6,
                    op0=ALU.mult, op1=ALU.add),
                    "v", writes=[thr2], reads=[gs])
                emit(nc.vector, lambda thr2=thr2: nc.vector.tensor_scalar(
                    thr2.ap[:, 1:2], thr2.ap[:, 0:1], -1.0, None, op0=ALU.mult),
                    "v", reads=[thr2], extra=thr2.ready)
                t2tok = ("v", pr.cnt["v"])
                thr2.ready = [t2tok]
                emit(nc.sync, lambda thr2=thr2, j=j: nc.sync.dma_start(
                    sc_rows[3:4, j * 2:j * 2 + 2], thr2.ap),
                    "wR", reads=[thr2], dk=16)
                thrp = s_thr[nm]
                emit(nc.sync, lambda thrp=thrp, j=j: nc.sync.dma_start(
                    thrp.ap, bcastP(sc_rows[3:4, j * 2:j * 2 + 2])),
                    None, writes=[thrp], extra=thr2.users, dk=16)

        pend_silu = None
        pend_zsilu = None

        def do_zsilu(pi):
            i, db = pi
            emit(nc.scalar, lambda i=i: nc.scalar.activation(
                s_szt[i].ap, pu[i].ap, AF.Silu), "s",
                writes=[s_szt[i]], reads=[pu[i]])
            sz_spill[db] = emit(nc.sync, lambda i=i, db=db: nc.sync.dma_start(
                sc_sz[db * 128:(db + 1) * 128, :], s_szt[i].ap), f"wZ{i}",
                reads=[s_szt[i]], dk=16)

        def do_silu(pi):
            i, db = pi
            cv = s_cv[i]
            emit(nc.scalar, lambda i=i, db=db, cv=cv: nc.scalar.activation(
                s_uct[i].ap, cv.ap, AF.Silu, bias=s_cb.ap[:, db:db + 1]),
                "s", writes=[s_uct[i]], reads=[cv, s_cb])
            ucb_spill[db] = emit(nc.sync, lambda i=i, db=db: nc.sync.dma_start(
                sc_ucb[db * 128:(db + 1) * 128, :], s_uct[i].ap), f"wU{i}",
                reads=[s_uct[i]], dk=16)

        for fi in range(32):
            i = fi % 2
            items = []
            for db in range(NDD):
                for q in range(4):
                    items.append((pu[i].ap[:, q * 512:(q + 1) * 512],
                                  s_win.ap[:, db * 4096 + fi * 128: db * 4096 + (fi + 1) * 128],
                                  s_xhatT.ap[:, db * T + q * 512: db * T + (q + 1) * 512],
                                  db == 0, db == NDD - 1))
            mm_group(pu[i], items, reads=[s_win, s_xhatT])
            if fi < 16:
                db = fi
                emit(nc.scalar, lambda i=i: nc.scalar.copy(s_u[i].ap, pu[i].ap), "s",
                     writes=[s_u[i]], reads=[pu[i]])
                if pend_silu is not None:
                    do_silu(pend_silu)
                    pend_silu = None
                cv = s_cv[i]
                emit(nc.vector, lambda i=i, db=db, cv=cv: nc.vector.tensor_scalar(
                    cv.ap, s_u[i].ap, s_cw.ap[:, db * 4 + 3:db * 4 + 4], None,
                    op0=ALU.mult), "v", writes=[cv], reads=[s_u[i], s_cw])
                for k in range(3):
                    sh = 3 - k
                    emit(nc.vector, lambda i=i, db=db, k=k, sh=sh, cv=cv:
                         nc.vector.scalar_tensor_tensor(
                             cv.ap[:, sh:T], s_u[i].ap[:, 0:T - sh],
                             s_cw.ap[:, db * 4 + k:db * 4 + k + 1],
                             cv.ap[:, sh:T], op0=ALU.mult, op1=ALU.add),
                         "v", writes=[cv], reads=[s_u[i], s_cw])
                pend_silu = (i, db)
            else:
                if pend_silu is not None:
                    do_silu(pend_silu)
                    pend_silu = None
                db = fi - 16
                if pend_zsilu is not None:
                    do_zsilu(pend_zsilu)
                pend_zsilu = (i, db)
            if fi < 24:
                gamma_tile(2 * fi)
                gamma_tile(2 * fi + 1)
            else:
                if fi == 24:
                    do_thresholds()
                    mk_qbufs()
                for _ in range(6):
                    qstep()

        if pend_zsilu is not None:
            do_zsilu(pend_zsilu)
            pend_zsilu = None
        while qstate["k"] < len(qtiles):
            qstep()
        barrier()
    stWin.close()

    qfence = [("v", pr.cnt["v"]), ("s", pr.cnt["s"])]
    for _qi in range(4):
        if pr.cnt.get(f"wQ{_qi}", 0):
            qfence.append((f"wQ{_qi}", pr.cnt[f"wQ{_qi}"]))
    FENCE[0] = qfence

    stW = contextlib.ExitStack()
    s_wout = sb("s_wout", (128, NDB * 1024), BF16, stW)
    for wk in range(8):
        emit(nc.gpsimd, lambda wk=wk: nc.gpsimd.dma_start(
            s_wout.ap[:, wk * 2048:(wk + 1) * 2048],
            d_wout[:, wk * 2048:(wk + 1) * 2048]), None,
            writes=[s_wout], dk=16)
    with contextlib.ExitStack() as st2:
        s_wx = sb("s_wx", (128, NDB * 160), BF16, st2)
        for wk in range(4):
            emit(nc.gpsimd, lambda wk=wk: nc.gpsimd.dma_start(
                s_wx.ap[:, wk * 640:(wk + 1) * 640],
                d_wx[:, wk * 640:(wk + 1) * 640]), None,
                writes=[s_wx], dk=16)
        p_dtr = ps("p_dtr", (RK, T), st2)
        p_bc = ps("p_bc", (32, T), st2)
        s_uc2 = [sb(f"s_uc2{i}", (128, T), BF16, st2) for i in range(4)]

        def load_uc2(db):
            if db >= NDB:
                return
            i = db % 4
            qe = nc.sync if db % 2 == 0 else nc.scalar
            emit(qe, lambda db=db, i=i, qe=qe: qe.dma_start(
                s_uc2[i].ap, sc_ucb[db * 128:(db + 1) * 128, :]), None,
                writes=[s_uc2[i]],
                extra=[("wU0", pr.cnt.get("wU0", 0)),
                       ("wU1", pr.cnt.get("wU1", 0))], dk=16)

        for db in range(3):
            load_uc2(db)
        for db in range(NDB):
            i = db % 4
            load_uc2(db + 3)
            deps = _deps_for([p_dtr, p_bc] if db == 0 else [], [s_wx, s_uc2[i]], [])
            pr.wait(nc.tensor, "t", deps)
            last = None
            for q in range(4):
                last = nc.tensor.matmul(
                    p_dtr.ap[:, q * 512:(q + 1) * 512],
                    s_wx.ap[:, db * 160: db * 160 + 128],
                    s_uc2[i].ap[:, q * 512:(q + 1) * 512],
                    start=(db == 0), stop=(db == NDB - 1))
            for q in range(4):
                last = nc.tensor.matmul(
                    p_bc.ap[:, q * 512:(q + 1) * 512],
                    s_wx.ap[:, db * 160 + 128: db * 160 + 160],
                    s_uc2[i].ap[:, q * 512:(q + 1) * 512],
                    start=(db == 0), stop=(db == NDB - 1))
            tok2 = pr.inc(last, "t", 1)
            s_uc2[i].users.append(tok2)
            if db == NDB - 1:
                p_dtr.ready = [tok2]
                p_bc.ready = [tok2]
        emit(nc.scalar, lambda: nc.scalar.copy(s_dtr.ap, p_dtr.ap), "s",
             writes=[s_dtr], reads=[p_dtr])
        s_bcv = sb("s_bcv", (32, T), BF16, st2)
        emit(nc.scalar, lambda: nc.scalar.copy(s_bcv.ap, p_bc.ap), "s",
             writes=[s_bcv], reads=[p_bc])
        b_spill = emit(nc.sync, lambda: nc.sync.dma_start(sc_b, s_bcv.ap[0:16, :]),
                       "wB", reads=[s_bcv], dk=16)
        c_spill = emit(nc.sync, lambda: nc.sync.dma_start(sc_c, s_bcv.ap[16:32, :]),
                       "wB", reads=[s_bcv], dk=16)
        barrier()

    with contextlib.ExitStack() as st3:
        _sb0 = sb("s_brepb0", (128, NS * TC), BF16, st3)
        s_brepb = [_sb0, _sb0]
        _sc0 = sb("s_crepb0", (128, NS * TC), BF16, st3)
        s_crepb = [_sc0, _sc0]
        s_e = [sb("s_e0", (128, TC), F32, st3)] * 2
        s_dt = [sb(f"s_dt{i}", (128, TC), F32, st3) for i in range(2)]
        s_wt = [sb("s_wt0", (128, TC), BF16, st3)] * 2
        s_da = [sb(f"s_da{i}", (128, TC), BF16, st3) for i in range(6)]
        s_dbufb = [sb(f"s_dbuf{i}", (128, NS * TC), BF16, st3) for i in range(2)]
        _sh0 = sb("s_hb0", (128, NS * TC), BF16, st3)
        s_hb = [_sh0, _sh0]
        s_uc3 = [sb(f"s_uc3{i}", (128, TC), BF16, st3) for i in range(2)]
        s_sz3 = [sb(f"s_sz3{i}", (128, TC), BF16, st3) for i in range(2)]
        s_g = sb("s_g", (128, NDB * TC), BF16, st3)
        p_p3 = [ps(f"p_p3{i}", (RK, TC), st3) for i in range(2)]
        p_p4 = ps("p_p4", (128, 1024), st3)
        p_y = [Buf(p_p4.ap[:, 0:TC]), Buf(p_p4.ap[:, TC:2 * TC])]
        s_x0g = [sb("s_x0g0", (128, D), F32, st3)] * 2
        s_x1t = [sb("s_x1t0", (128, D), F32, st3)] * 2
        s_xh1t = [sb("s_xh1t0", (128, D), BF16, st3)] * 2
        stats4 = [{k: sb(f"g4_{k}{i}", (128, 1), F32, st3) for k in STK}
                  for i in range(2)]
        stats5 = stats4

        s_wq4 = [sb(f"s_wq4_{i}", (128, NDD * 256), BF16, st3,
                    is_transpose_dest=True) for i in range(2)]
        s_xh1Tq = sb("s_xh1Tq", (128, NDD * 512), BF16, st3,
                     is_transpose_dest=True)
        s_abq = [sb(f"s_abq{i}", (128, 512), BF16, st3) for i in range(2)]
        s_abr = [sb(f"s_abr{i}", (128, 512), BF16, st3) for i in range(3)]
        s_sqb = [sb(f"s_sqb{i}", (128, 512), BF16, st3) for i in range(2)]
        p_a = [ps(f"p_a{i}", (128, 512), st3) for i in range(2)]
        p_s1 = ps("p_s1", (1, 512), st3)
        p_s2 = ps("p_s2", (1, 512), st3)
        s_mub = sb("s_mub", (128, 512), BF16, st3)
        s_rsb = sb("s_rsb", (128, 512), BF16, st3)
        s_mu1 = sb("s_mu1", (1, 512), BF16, st3)
        s_rs1 = sb("s_rs1", (1, 512), F32, st3)
        s_rs16 = sb("s_rs16", (1, 512), BF16, st3)
        s_t1 = sb("s_t15", (128, 512), F32, st3)
        s_aht = [sb(f"s_aht{i}", (128, 512), BF16, st3) for i in range(2)]
        tokqs = [(f"wQ{_qi}", pr.cnt.get(f"wQ{_qi}", 0)) for _qi in range(4)]
        ab_dbuf = [Buf(sc_ab[hb]) for hb in range(NHB)]

        def p5_load_slot(tq, g):
            if g >= 16:
                return
            slot = s_wq4[g % 2]
            ssem = bufsem(slot)
            for db in range(NDD):
                emit(nc.sync, lambda g=g, db=db, slot=slot: nc.sync.dma_start_transpose(
                    slot.ap[:, db * 256:(db + 1) * 256],
                    sc_wqfc[g * 256:(g + 1) * 256, db * 128:(db + 1) * 128]), ssem,
                    writes=[slot] if db == 0 else [],
                    extra=(tokqs + slot.ready + slot.users) if db == 0 else [], dk=16)
            slot.ready = [(ssem, pr.cnt[ssem])]

        def p5_xpose(tq):
            t0 = tq * 512
            hsem = bufsem(s_xh1Tq)
            tokh = ("wH1", pr.cnt.get("wH1", 0))
            for db in range(NDD):
                emit(nc.sync, lambda db=db, t0=t0: nc.sync.dma_start_transpose(
                    s_xh1Tq.ap[:, db * 512:(db + 1) * 512],
                    sc_xh1[t0:t0 + 512, db * 128:(db + 1) * 128]), hsem,
                    writes=[s_xh1Tq] if db == 0 else [],
                    extra=([tokh] + s_xh1Tq.ready + s_xh1Tq.users) if db == 0 else [],
                    dk=16)
            s_xh1Tq.ready = [(hsem, pr.cnt[hsem])]

        def p5_colsums(hb, sq_dep):
            i = hb % 2
            mm_group(p_s1, [(p_s1.ap, s_ones.ap, s_abq[i].ap,
                             hb == 0, hb == NHB - 1)],
                     reads=[s_ones, s_abq[i]], extra=sq_dep)
            mm_group(p_s2, [(p_s2.ap, s_ones.ap, s_sqb[i].ap,
                             hb == 0, hb == NHB - 1)],
                     reads=[s_ones, s_sqb[i]])

        def p5_gelu2(hb0):
            if hb0 > 0:
                p5_colsums(hb0 - 1, prev_sq[0])
            for hb in (hb0, hb0 + 1):
                i = hb % 2
                emit(nc.scalar, lambda i=i: nc.scalar.activation(
                    s_abq[i].ap, p_a[i].ap, AF.Gelu), "s",
                    writes=[s_abq[i]], reads=[p_a[i]])
                emit(nc.gpsimd, lambda i=i, hb=hb: nc.gpsimd.dma_start(
                    sc_ab[hb], s_abq[i].ap), f"wAB{i}",
                    writes=[ab_dbuf[hb]], reads=[s_abq[i]], dk=16)
            sqtoks = {}
            for hb in (hb0, hb0 + 1):
                i = hb % 2
                emit(nc.scalar, lambda i=i: nc.scalar.activation(
                    s_sqb[i].ap, s_abq[i].ap, AF.Square), "s",
                    writes=[s_sqb[i]], reads=[s_abq[i]])
                sqtoks[hb] = [("s", pr.cnt["s"])]
            p5_colsums(hb0, sqtoks[hb0])
            prev_sq[0] = sqtoks[hb0 + 1]

        def p5_mm2(tq, hb2):
            if hb2 > 0:
                p5_gelu2(2 * hb2 - 2)
            for hb in (2 * hb2, 2 * hb2 + 1):
                i = hb % 2
                slot = s_wq4[hb2 % 2]
                o = (hb - 2 * hb2) * 128
                items = [(p_a[i].ap,
                          slot.ap[:, db * 256 + o: db * 256 + o + 128],
                          s_xh1Tq.ap[:, db * 512:(db + 1) * 512],
                          db == 0, db == NDD - 1) for db in range(NDD)]
                mm_group(p_a[i], items, reads=[slot, s_xh1Tq])
            p5_load_slot(tq, hb2 + 2)

        def p5_reload(hb):
            if hb >= NHB:
                return
            j = hb % 3
            emit(nc.gpsimd, lambda hb=hb, j=j: nc.gpsimd.dma_start(
                s_abr[j].ap, sc_ab[hb]), None, writes=[s_abr[j]],
                reads=[ab_dbuf[hb]],
                extra=[("wAB0", pr.cnt.get("wAB0", 0)),
                       ("wAB1", pr.cnt.get("wAB1", 0))], dk=16)

        def p5_stats(tq):
            t0 = tq * 512
            p5_gelu2(NHB - 2)
            p5_colsums(NHB - 1, prev_sq[0])
            r_e2 = p_s2.ap[0:1, :]
            r_lv = p_a[0].ap[0:1, 0:512]
            emit(nc.vector, lambda: nc.vector.tensor_scalar(
                s_mu1.ap, p_s1.ap, 1.0 / 4096, None, op0=ALU.mult), "v",
                writes=[s_mu1], reads=[p_s1])
            emit(nc.vector, lambda r_e2=r_e2: nc.vector.tensor_scalar(
                r_e2, r_e2, 1.0 / 4096, None, op0=ALU.mult), "v",
                reads=[p_s2], extra=p_s2.ready)
            emit(nc.vector, lambda: nc.vector.tensor_mul(
                s_rs1.ap, s_mu1.ap, s_mu1.ap), "v", reads=[s_mu1],
                writes=[s_rs1])
            emit(nc.vector, lambda r_e2=r_e2: nc.vector.tensor_sub(
                r_e2, r_e2, s_rs1.ap), "v", reads=[p_s2, s_rs1],
                extra=[("v", pr.cnt["v"])])
            vtok = ("v", pr.cnt["v"])
            emit(nc.scalar, lambda t0=t0: nc.scalar.dma_start(
                sc_rows16[0:1, t0:t0 + 512], s_mu1.ap), "wMRa",
                reads=[s_mu1], dk=16)
            emit(nc.scalar, lambda r_lv=r_lv, r_e2=r_e2: nc.scalar.activation(
                r_lv, r_e2, AF.Ln, bias=EPS[0:1, :]), "s",
                reads=[p_a[0], s_consts], extra=[vtok])
            emit(nc.scalar, lambda r_lv=r_lv: nc.scalar.activation(
                s_rs16.ap, r_lv, AF.Exp, scale=-0.5), "s",
                writes=[s_rs16], reads=[p_a[0]],
                extra=[("s", pr.cnt["s"])] + s_rs16.users)
            emit(nc.scalar, lambda t0=t0: nc.scalar.dma_start(
                sc_rows16[1:2, t0:t0 + 512], s_rs16.ap), "wMRb",
                reads=[s_rs16], dk=16)
            mrtoks = [("wMRa", pr.cnt["wMRa"]), ("wMRb", pr.cnt["wMRb"])]
            emit(nc.scalar, lambda t0=t0: nc.scalar.dma_start(
                s_mub.ap, bcastP(sc_rows16[0:1, t0:t0 + 512])), None,
                writes=[s_mub], extra=mrtoks, dk=16)
            emit(nc.scalar, lambda t0=t0: nc.scalar.dma_start(
                s_rsb.ap, bcastP(sc_rows16[1:2, t0:t0 + 512])), None,
                writes=[s_rsb], extra=mrtoks, dk=16)
            for pb in (p_s1, p_s2, p_a[0], p_a[1]):
                pb.users.append(("v", pr.cnt["v"]))
                pb.users.append(("s", pr.cnt["s"]))
                pb.users.append(("wMRa", pr.cnt["wMRa"]))
                pb.users.append(("wMRb", pr.cnt["wMRb"]))
            for hb in range(3):
                p5_reload(hb)

        def p5_apply4(tq, k):
            t0 = tq * 512
            for hb in range(4 * k, 4 * k + 4):
                j = hb % 3
                i = hb % 2
                emit(nc.vector, lambda j=j: nc.vector.tensor_sub(
                    s_t1.ap, s_abr[j].ap, s_mub.ap), "v",
                    writes=[s_t1], reads=[s_abr[j], s_mub])
                emit(nc.vector, lambda: nc.vector.tensor_mul(
                    s_t1.ap, s_t1.ap, s_rsb.ap), "v",
                    writes=[s_t1], reads=[s_rsb], extra=s_t1.ready)
                emit(nc.vector, lambda i=i, hb=hb: nc.vector.tensor_scalar(
                    s_aht[i].ap, s_t1.ap, s_lnw.ap[:, hb:hb + 1],
                    s_lnb.ap[:, hb:hb + 1], op0=ALU.mult, op1=ALU.add), "v",
                    writes=[s_aht[i]], reads=[s_t1, s_lnw, s_lnb])
                emit(nc.scalar, lambda i=i, hb=hb, t0=t0: nc.scalar.dma_start(
                    sc_ahat[hb:hb + 1, :, t0:t0 + 512], s_aht[i].ap), f"wA{i}",
                    reads=[s_aht[i]], dk=16)
                p5_reload(hb + 3)

        def p5_steps(tq):
            steps = [lambda tq=tq: p5_xpose(tq),
                     lambda tq=tq: p5_load_slot(tq, 0),
                     lambda tq=tq: p5_load_slot(tq, 1)]
            for hb2 in range(NHB // 2):
                steps.append(lambda tq=tq, hb2=hb2: p5_mm2(tq, hb2))
            steps.append(lambda tq=tq: p5_stats(tq))
            for k in range(NHB // 4):
                steps.append(lambda tq=tq, k=k: p5_apply4(tq, k))
            return steps

        prev_sq = [None]
        p5q = []
        p5_done = [0]

        def bc3(dst2d, src, t0):
            base = src[0:1, t0:t0 + TC]
            dims = [list(dd) for dd in base.ap]

            def ap_for(q):
                b4 = src[4 * q:4 * q + 1, t0:t0 + TC]
                return bass.AP(b4.tensor, b4.offset, [[0, 128], [T, 4], [1, TC]])
            return ap_for

        def load_reps(tc):
            if tc >= NCH:
                return
            t0 = tc * TC
            k = tc % 2
            bq = bc3(None, sc_b, t0)
            cq = bc3(None, sc_c, t0)
            bs, cs2 = bufsem(s_brepb[k]), bufsem(s_crepb[k])
            for q in range(4):
                emit(nc.scalar, lambda q=q, k=k, bq=bq: nc.scalar.dma_start(
                    s_brepb[k].ap[:, q * 4 * TC:(q + 1) * 4 * TC], bq(q)), bs,
                    writes=[s_brepb[k]] if q == 0 else [],
                    extra=([("wB", pr.cnt["wB"])] + qfence
                           + s_brepb[k].ready + s_brepb[k].users)
                    if q == 0 else [], dk=16)
                emit(nc.scalar, lambda q=q, k=k, cq=cq: nc.scalar.dma_start(
                    s_crepb[k].ap[:, q * 4 * TC:(q + 1) * 4 * TC], cq(q)), cs2,
                    writes=[s_crepb[k]] if q == 0 else [],
                    extra=([("wB", pr.cnt["wB"])] + qfence
                           + s_crepb[k].ready + s_crepb[k].users)
                    if q == 0 else [], dk=16)
            s_brepb[k].ready = [(bs, pr.cnt[bs])]
            s_crepb[k].ready = [(cs2, pr.cnt[cs2])]

        def load_uc_sz(tc, db):
            if db >= NDB:
                tc, db = tc + 1, db - NDB
            if tc >= NCH:
                return
            t0 = tc * TC
            i = db % 2
            emit(nc.sync, lambda db=db, i=i, t0=t0: nc.sync.dma_start(
                s_uc3[i].ap, sc_ucb[db * 128:(db + 1) * 128, t0:t0 + TC]), None,
                writes=[s_uc3[i]],
                extra=[("wU0", pr.cnt.get("wU0", 0)),
                       ("wU1", pr.cnt.get("wU1", 0))], dk=16)
            emit(nc.scalar, lambda db=db, i=i, t0=t0: nc.scalar.dma_start(
                s_sz3[i].ap, sc_sz[db * 128:(db + 1) * 128, t0:t0 + TC]), None,
                writes=[s_sz3[i]],
                extra=[("wZ0", pr.cnt.get("wZ0", 0)),
                       ("wZ1", pr.cnt.get("wZ1", 0))], dk=16)

        load_uc_sz(0, 0)
        pend_gate = None
        da_next = None

        def do_gate(pg):
            i, db = pg
            emit(nc.vector, lambda i=i, db=db: nc.vector.tensor_mul(
                s_g.ap[:, db * TC:(db + 1) * TC], p_y[i].ap, s_sz3[i].ap), "v",
                reads=[p_y[i], s_sz3[i]],
                extra=(s_g.ready + s_g.users) if db == 0 else [])

        for tc in range(NCH):
            t0 = tc * TC
            if tc == 0:
                load_reps(0)
            s_brep = s_brepb[tc % 2]
            s_crep = s_crepb[tc % 2]
            for db in range(NDB):
                i = db % 2
                if tc == 0 and db == 0:
                    mm_group(p_p3[0], [(p_p3[0].ap, s_wdt.ap[0:RK, 0:128],
                                       s_dtr.ap[:, 0:TC], True, True)],
                             reads=[s_wdt, s_dtr])
                    emit(nc.scalar, lambda: nc.scalar.activation(
                        s_e[0].ap, p_p3[0].ap, AF.Exp, bias=s_bdt.ap[:, 0:1]), "s",
                        writes=[s_e[0]], reads=[p_p3[0], s_bdt])
                    emit(nc.scalar, lambda: nc.scalar.activation(
                        s_dt[0].ap, s_e[0].ap, AF.Ln, bias=ONE), "s",
                        writes=[s_dt[0]], reads=[s_e[0], s_consts])
                emit(nc.vector, lambda i=i: nc.vector.tensor_mul(
                    s_wt[i].ap, s_dt[i].ap, s_uc3[i].ap), "v",
                    writes=[s_wt[i]], reads=[s_dt[i], s_uc3[i]])
                da_toks = da_next if da_next is not None else {}
                da_next = None

                def emit_da(n, i=i, db=db, da_toks=da_toks):
                    if n >= NS:
                        return
                    j = n % 6
                    da_toks[n] = emit(
                        nc.scalar, lambda i=i, j=j, db=db, n=n: nc.scalar.activation(
                            s_da[j].ap, s_dt[i].ap, AF.Exp,
                            scale=s_A.ap[:, db * NS + n: db * NS + n + 1]), "s",
                        writes=[s_da[j]], reads=[s_dt[i], s_A])

                for n in range(6):
                    if n not in da_toks:
                        emit_da(n)
                s_dbuf = s_dbufb[i]
                s_h = s_hb[i]
                dbuf_war = s_dbuf.ready + s_dbuf.users
                wtr = bass.AP(s_wt[i].ap.tensor, s_wt[i].ap.offset,
                              [[TC, 128], [0, NS], [1, TC]])
                emit(nc.vector, lambda s_brep=s_brep, s_dbuf=s_dbuf, wtr=wtr:
                     nc.vector.tensor_mul(
                         s_dbuf.ap.rearrange("p (n t) -> p n t", t=TC), wtr,
                         s_brep.ap.rearrange("p (n t) -> p n t", t=TC)),
                     "v", reads=[s_wt[i], s_brep], extra=dbuf_war)
                dbuf_tok = ("v", pr.cnt["v"])
                s_dbuf.ready = [dbuf_tok]
                s_dbuf.users = []
                ndb, ntc = db + 1, tc
                if ndb >= NDB:
                    ndb, ntc = 0, tc + 1
                if ntc < NCH:
                    nt0 = ntc * TC
                    ni = ndb % 2
                    mm_group(p_p3[ni],
                             [(p_p3[ni].ap,
                               s_wdt.ap[:, ndb * 128:(ndb + 1) * 128],
                               s_dtr.ap[:, nt0:nt0 + TC], True, True)],
                             reads=[s_wdt, s_dtr])
                    emit(nc.scalar, lambda ni=ni, ndb=ndb: nc.scalar.activation(
                        s_e[ni].ap, p_p3[ni].ap, AF.Exp,
                        bias=s_bdt.ap[:, ndb:ndb + 1]), "s",
                        writes=[s_e[ni]], reads=[p_p3[ni], s_bdt])
                    emit(nc.scalar, lambda ni=ni: nc.scalar.activation(
                        s_dt[ni].ap, s_e[ni].ap, AF.Ln, bias=ONE), "s",
                        writes=[s_dt[ni]], reads=[s_e[ni], s_consts])
                h_war = s_h.ready + s_h.users
                for n in range(NS):
                    j = n % 6
                    if tc == 0:
                        init = 0.0
                    else:
                        init = s_carry.ap[:, db * NS + n: db * NS + n + 1]
                    emit(nc.vector, lambda j=j, n=n, init=init, s_h=s_h, s_dbuf=s_dbuf:
                         nc.vector.tensor_tensor_scan(
                             s_h.ap[:, n * TC:(n + 1) * TC], s_da[j].ap,
                             s_dbuf.ap[:, n * TC:(n + 1) * TC], init,
                             op0=ALU.mult, op1=ALU.add),
                         "v", reads=[s_da[j]] + ([s_carry] if tc > 0 else []),
                         extra=[da_toks[n]]
                         + ((h_war + [dbuf_tok]) if n == 0 else []))
                    emit_da(n + 6)
                htok = ("v", pr.cnt["v"])
                s_h.ready = [htok]
                s_h.users = []
                if pend_gate is not None:
                    do_gate(pend_gate)
                    pend_gate = None
                hview = s_h.ap.rearrange("p (n t) -> p n t", t=TC)
                carry_tok = emit(nc.scalar, lambda db=db, hview=hview:
                    nc.scalar.copy(
                        s_carry.ap[:, db * NS:(db + 1) * NS].rearrange(
                            "p (n o) -> p n o", o=1), hview[:, :, TC - 1:TC]), "s",
                    writes=[s_carry], reads=[s_h])
                emit(nc.vector, lambda s_crep=s_crep, s_h=s_h, s_dbuf=s_dbuf:
                    nc.vector.tensor_mul(
                        s_dbuf.ap, s_h.ap, s_crep.ap), "v",
                    reads=[s_h, s_crep],
                    extra=s_dbuf.ready + s_dbuf.users)
                hctok = ("v", pr.cnt["v"])
                s_dbuf.ready = [hctok]
                s_dbuf.users = []
                s_h.users.append(hctok)
                items = [(p_y[i].ap, s_id16.ap, s_dbuf.ap[:, n * TC:(n + 1) * TC],
                          n == 0, False) for n in range(NS)]
                items.append((p_y[i].ap, s_diagD.ap[:, db * 128:(db + 1) * 128],
                              s_uc3[i].ap, False, True))
                mm_group(p_y[i], items, reads=[s_dbuf, s_id16, s_diagD, s_uc3[i]])
                load_uc_sz(tc, db + 1)
                pend_gate = (i, db)
                ndb2, ntc2 = db + 1, tc
                if ndb2 >= NDB:
                    ndb2, ntc2 = 0, tc + 1
                if ntc2 < NCH:
                    da_next = {}

                    def emit_da_pre(n, i2=ndb2 % 2, db2=ndb2, dd=da_next):
                        j = n % 6
                        dd[n] = emit(
                            nc.scalar,
                            lambda: nc.scalar.activation(
                                s_da[j].ap, s_dt[i2].ap, AF.Exp,
                                scale=s_A.ap[:, db2 * NS + n: db2 * NS + n + 1]),
                            "s", writes=[s_da[j]], reads=[s_dt[i2], s_A])

                    for n in range(3):
                        emit_da_pre(n)
                want = ((db + 1) * 28 + 19) // 20
                while p5q and p5_done[0] < want:
                    p5q.pop(0)()
                    p5_done[0] += 1
            if pend_gate is not None:
                do_gate(pend_gate)
                pend_gate = None
            load_reps(tc + 1)
            gtok = ("v", pr.cnt["v"])
            s_g.ready = [gtok]
            s_g.users = []
            for su in range(TC // 128):
                tglob = tc * (TC // 128) + su
                i = su % 2
                items = []
                for db in range(NDB):
                    stat = s_g.ap[:, db * TC + su * 128: db * TC + (su + 1) * 128]
                    for dmh in range(2):
                        items.append((p_p4.ap[:, dmh * 512:(dmh + 1) * 512], stat,
                                      s_wout.ap[:, db * 1024 + dmh * 512: db * 1024 + (dmh + 1) * 512],
                                      db == 0, db == NDB - 1))
                mm_group(p_p4, items, reads=[s_g, s_wout])
                emit(nc.sync, lambda tglob=tglob, i=i: nc.sync.dma_start(
                    s_x0g[i].ap, d_x[tglob * 128:(tglob + 1) * 128, :]), None,
                    writes=[s_x0g[i]], dk=16)
                ln_rows(p_p4.ap, p_p4, stats4[i], s_x1t[i], s_x1t[i].ap, D)
                emit(nc.vector, lambda i=i: nc.vector.tensor_add(
                    s_x1t[i].ap, s_x1t[i].ap, s_x0g[i].ap), "v",
                    writes=[s_x1t[i]], reads=[s_x0g[i]], extra=s_x1t[i].ready)
                emit(nc.sync, lambda tglob=tglob, i=i: nc.sync.dma_start(
                    sc_x1[tglob * 128:(tglob + 1) * 128, :], s_x1t[i].ap), "wX1",
                    reads=[s_x1t[i]], dk=16)
                ln_rows(s_x1t[i].ap, s_x1t[i], stats5[i], s_xh1t[i], s_xh1t[i].ap, D)
                emit(nc.sync, lambda tglob=tglob, i=i: nc.sync.dma_start(
                    sc_xh1[tglob * 128:(tglob + 1) * 128, :], s_xh1t[i].ap), "wH1",
                    reads=[s_xh1t[i]], dk=16)
                if p5q:
                    p5q.pop(0)()
                    p5_done[0] += 1
            for pyh in p_y:
                pyh.users.append(("s", pr.cnt["s"]))
                pyh.users.append(("v", pr.cnt["v"]))
            p5q.extend(p5_steps(tc))
            p5_done[0] = 0
        for f in p5q:
            f()
        barrier()
    stW.close()

    with contextlib.ExitStack() as st6:
        s_wqprojT = sb("s_wqprojT", (128, NHB * 1024), BF16, st6,
                       is_transpose_dest=True)
        s_ah6 = [sb(f"s_ah6{i}", (128, NHB * 128), BF16, st6) for i in range(2)]
        p_p6b = [ps(f"p_p6{i}", (128, 1024), st6) for i in range(2)]
        s_v6 = [sb(f"s_v6{i}", (128, D), F32, st6) for i in range(2)]
        s_x1r = [sb(f"s_x1r{i}", (128, D), F32, st6) for i in range(2)]
        s_o = [sb(f"s_o{i}", (128, D), F32, st6) for i in range(2)]
        stats6 = [{k: sb(f"g6_{k}{i}", (128, 1), F32, st6) for k in STK}
                  for i in range(2)]

        def load_p6(su):
            if su >= TSUB:
                return
            i = su % 2
            emit(nc.sync, lambda su=su, i=i: nc.sync.dma_start(
                s_ah6[i].ap, sc_ahat[:, :, su * 128:(su + 1) * 128].rearrange(
                    "h p t -> p h t")), None, writes=[s_ah6[i]], dk=16)
            emit(nc.scalar, lambda su=su, i=i: nc.scalar.dma_start(
                s_x1r[i].ap, sc_x1[su * 128:(su + 1) * 128, :]), None,
                writes=[s_x1r[i]], dk=16)

        load_p6(0)
        tokps = [(f"wQ{_qi}", pr.cnt.get(f"wQ{_qi}", 0)) for _qi in range(4)]
        qpsem = bufsem(s_wqprojT)
        pj_tok = {}
        for hb in range(NHB):
            emit(nc.sync, lambda hb=hb: nc.sync.dma_start_transpose(
                s_wqprojT.ap[:, hb * 1024:(hb + 1) * 1024],
                sc_wqproj[:, hb * 128:(hb + 1) * 128]), qpsem, extra=tokps, dk=16)
            pj_tok[hb] = (qpsem, pr.cnt[qpsem])
        s_wqprojT.ready = [(qpsem, pr.cnt[qpsem])]

        for su in range(TSUB):
            i = su % 2
            p_p6 = p_p6b[i]
            load_p6(su + 1)
            items = []
            for hb in range(NHB):
                stat = s_ah6[i].ap[:, hb * 128:(hb + 1) * 128]
                for dmh in range(2):
                    items.append((p_p6.ap[:, dmh * 512:(dmh + 1) * 512], stat,
                                  s_wqprojT.ap[:, hb * 1024 + dmh * 512: hb * 1024 + (dmh + 1) * 512],
                                  hb == 0, hb == NHB - 1))
            if su == 0:
                deps = _deps_for([p_p6], [s_ah6[i]], [])
                pr.wait(nc.tensor, "t", deps)
                last = None
                for hb in range(NHB):
                    pr.wait(nc.tensor, "t", [pj_tok[hb]])
                    for dmh in range(2):
                        it = items[hb * 2 + dmh]
                        last = nc.tensor.matmul(it[0], it[1], it[2],
                                                start=it[3], stop=it[4])
                tok6 = pr.inc(last, "t", 1)
                p_p6.ready = [tok6]
                p_p6.users = []
                s_ah6[i].users.append(tok6)
                s_wqprojT.users.append(tok6)
            else:
                mm_group(p_p6, items, reads=[s_ah6[i], s_wqprojT])
            ln_rows(p_p6.ap, p_p6, stats6[i], s_v6[i], s_v6[i].ap, D)
            emit(nc.vector, lambda i=i: nc.vector.tensor_add(
                s_o[i].ap, s_v6[i].ap, s_x1r[i].ap), "v",
                writes=[s_o[i]], reads=[s_v6[i], s_x1r[i]])
            emit(nc.sync, lambda su=su, i=i: nc.sync.dma_start(
                d_out[su * 128:(su + 1) * 128, :], s_o[i].ap), f"dO{i}",
                reads=[s_o[i]], dk=16)

    nc.sync.wait_ge(pr.sem("dO0"), pr.cnt["dO0"])
    nc.sync.wait_ge(pr.sem("dO1"), pr.cnt["dO1"])
    return nc


def _prep_common(inputs):
    f = np.float32

    def r(a):
        return np.ascontiguousarray(a)

    W_in = np.asarray(inputs["L0_attn_W_in"], f)
    W_x = np.asarray(inputs["L0_attn_W_x"], f)
    W_dt = np.asarray(inputs["L0_attn_W_dt"], f)
    W_out = np.asarray(inputs["L0_attn_W_out"], f)
    conv_w = np.asarray(inputs["L0_attn_conv_w"], f)
    conv_b = np.asarray(inputs["L0_attn_conv_b"], f)
    b_dt = np.asarray(inputs["L0_attn_b_dt"], f)
    A_log = np.asarray(inputs["L0_attn_A_log"], f)
    D_p = np.asarray(inputs["L0_attn_D_param"], f)
    Wfc = np.asarray(inputs["L0_ffn_W_fc"], f)
    lnw = np.asarray(inputs["L0_ffn_ln_w"], f)
    lnb = np.asarray(inputs["L0_ffn_ln_b"], f)
    Wproj = np.asarray(inputs["L0_ffn_W_proj"], f)

    consts = np.zeros((128, 4), f)
    consts[:, 0] = 1.0
    consts[:, 1] = 1e-5
    return {
        "w_in_r": r(W_in.reshape(NDD, 128, 4096).transpose(1, 0, 2).reshape(128, NDD * 4096)),
        "w_x_r": r(W_x.reshape(NDB, 128, 160).transpose(1, 0, 2).reshape(128, NDB * 160)),
        "w_dt": r(W_dt),
        "w_out_r": r(W_out.reshape(NDB, 128, 1024).transpose(1, 0, 2).reshape(128, NDB * 1024)),
        "conv_w_r": r(conv_w.reshape(NDB, 128, 4).transpose(1, 0, 2).reshape(128, NDB * 4)),
        "conv_b_r": r(conv_b.reshape(NDB, 128).T),
        "b_dt_r": r(b_dt.reshape(NDB, 128).T),
        "a_log_r": r(A_log.reshape(NDB, 128, NS).transpose(1, 0, 2).reshape(128, NDB * NS)),
        "d_r": r(D_p.reshape(NDB, 128).T),
        "wfc": r(Wfc),
        "wproj": r(Wproj),
        "lnw_r": r(lnw.reshape(NHB, 128).T),
        "lnb_r": r(lnb.reshape(NHB, 128).T),
        "consts": consts,
        "id16": np.eye(128, dtype=BFNP),
        "ones_col16": np.ones((128, 1), dtype=BFNP),
    }


def kernel(**inputs):
    x_all = np.asarray(inputs["L0_attn_x"], np.float32)
    common = _prep_common(inputs)
    nc = build()
    in_maps = [dict(common, x=np.ascontiguousarray(x_all[i])) for i in range(8)]
    res = run_bass_kernel_spmd(nc, in_maps, core_ids=list(range(8)))
    return np.stack([np.asarray(res.results[i]["out"]) for i in range(8)], axis=0)


# version 61 (final)
# speedup vs baseline: 1.1701x; 1.0220x over previous
import contextlib
import numpy as np
import ml_dtypes
import concourse.bass as bass
import concourse.mybir as mybir
from concourse.bass_utils import run_bass_kernel_spmd

F32 = mybir.dt.float32
BF16 = mybir.dt.bfloat16
AF = mybir.ActivationFunctionType
ALU = mybir.AluOpType
AX = mybir.AxisListType
BFNP = ml_dtypes.bfloat16

T, D, DI, NS, RK = 2048, 1024, 2048, 16, 128
NDB, NDD, NHB = 16, 8, 32
TSUB = 16
TC, NCH = 512, 4
NW = 4096 * 1024
STK = ("ssq", "ssum", "mu", "e2", "mu2", "var", "lv", "rstd", "nb")


class Pr:

    def __init__(self, nc):
        self.nc = nc
        self.h = {}
        self.cnt = {}
        self.seen = {}

    def sem(self, name):
        if name not in self.h:
            self.h[name] = self.nc.alloc_semaphore("sem_" + name)
            self.cnt[name] = 0
        return self.h[name]

    def wait(self, eng, ename, tokens):
        need = {}
        for (n, v) in tokens:
            if v > need.get(n, 0):
                need[n] = v
        for n, v in sorted(need.items()):
            k = (ename, n)
            if self.seen.get(k, -1) >= v:
                continue
            eng.wait_ge(self.sem(n), v)
            self.seen[k] = v

    def inc(self, ins, name, k=1):
        s = self.sem(name)
        ins.then_inc(s, k)
        self.cnt[name] += k
        return (name, self.cnt[name])


class Buf:
    __slots__ = ("ap", "ready", "users")

    def __init__(self, ap):
        self.ap = ap
        self.ready = []
        self.users = []


_BUFSEM = {}


def bufsem(buf):
    if id(buf) not in _BUFSEM:
        _BUFSEM[id(buf)] = f"dm{len(_BUFSEM)}"
    return _BUFSEM[id(buf)]


def _deps_for(writes, reads, extra):
    deps = list(extra)
    for b in reads:
        deps += b.ready
    for b in writes:
        deps += b.ready + b.users
    return deps


def build(taps=()):
    taps = set(taps)
    _BUFSEM.clear()
    nc = bass.Bass("TRN2", target_bir_lowering=False, debug=False)
    pr = Pr(nc)

    def din(name, shape, dt):
        return nc.dram_tensor(name, list(shape), dt, kind="ExternalInput").ap()

    def dscr(name, shape, dt):
        kind = "ExternalOutput" if name in taps else "Internal"
        return nc.dram_tensor(name, list(shape), dt, kind=kind).ap()

    d_x = din("x", (T, D), F32)
    d_win = din("w_in_r", (128, NDD * 4096), F32)
    d_wx = din("w_x_r", (128, NDB * 160), F32)
    d_wdt = din("w_dt", (RK, DI), F32)
    d_wout = din("w_out_r", (128, NDB * 1024), F32)
    d_cw = din("conv_w_r", (128, NDB * 4), F32)
    d_cb = din("conv_b_r", (128, NDB), F32)
    d_bdt = din("b_dt_r", (128, NDB), F32)
    d_alog = din("a_log_r", (128, NDB * NS), F32)
    d_D = din("d_r", (128, NDB), F32)
    d_wfc = din("wfc", (4096, 1024), F32)
    d_wproj = din("wproj", (1024, 4096), F32)
    d_lnw = din("lnw_r", (128, NHB), F32)
    d_lnb = din("lnb_r", (128, NHB), F32)
    d_consts = din("consts", (128, 4), F32)
    d_id16 = din("id16", (128, 128), BF16)
    d_ones = din("ones_col16", (128, 1), BF16)

    sc_xhat = dscr("sc_xhat", (T, D), BF16)
    sc_ucb = dscr("sc_ucb", (DI, T), BF16)
    sc_sz = dscr("sc_sz", (DI, T), BF16)
    sc_b = dscr("sc_b", (NS, T), BF16)
    sc_c = dscr("sc_c", (NS, T), BF16)
    sc_x1 = dscr("sc_x1", (T, D), F32)
    sc_xh1 = dscr("sc_xh1", (T, D), BF16)
    sc_wqfc = dscr("sc_wqfc", (4096, 1024), BF16)
    sc_wqproj = dscr("sc_wqproj", (1024, 4096), BF16)
    sc_ahat = dscr("sc_ahat", (NHB, 128, T), BF16)
    sc_ab = dscr("sc_ab", (NHB, 128, 512), BF16)
    sc_rows16 = dscr("sc_rows16", (2, T), BF16)
    sc_rows = dscr("sc_rows", (4, T), F32)
    d_out = nc.dram_tensor("out", [T, D], F32, kind="ExternalOutput").ap()

    ENG = {}
    for nm, e in (("v", nc.vector), ("s", nc.scalar), ("t", nc.tensor),
                  ("g", nc.gpsimd), ("y", nc.sync)):
        ENG[id(e)] = nm

    def emit(eng, make, sem, writes=(), reads=(), extra=(), dk=1):
        deps = _deps_for(writes, reads, extra)
        pr.wait(eng, ENG[id(eng)], deps)
        ins = make()
        if sem is None:
            assert dk == 16 and writes
            sem = bufsem(writes[0])
        tok = pr.inc(ins, sem, dk)
        for b in writes:
            b.ready = [tok]
            b.users = []
        for b in reads:
            b.users.append(tok)
        return tok

    FENCE = [[]]

    def sb(name, shape, dt, stack=None, **kw):
        if stack is None:
            b = Buf(nc.alloc_sbuf_tensor(name, list(shape), dt, **kw).ap())
        else:
            h = stack.enter_context(nc.sbuf_tensor(name, list(shape), dt, **kw))
            b = Buf(h.ap())
        b.ready = list(FENCE[0])
        return b

    def ps(name, shape, stack):
        h = stack.enter_context(nc.psum_tensor(name, list(shape), F32))
        return Buf(h.ap())

    def bcastP(ap2d, p=128):
        assert ap2d.shape[0] == 1, ap2d.shape
        dims = [list(dd) for dd in ap2d.ap]
        return bass.AP(ap2d.tensor, ap2d.offset, [[0, p]] + dims[1:])

    def mm_group(pbuf, items, reads, extra=()):
        deps = _deps_for([pbuf], reads, extra)
        pr.wait(nc.tensor, "t", deps)
        last = None
        for (o, l, r, st, sp) in items:
            last = nc.tensor.matmul(o, l, r, start=st, stop=sp)
        tok = pr.inc(last, "t", 1)
        pbuf.ready = [tok]
        pbuf.users = []
        for b in reads:
            b.users.append(tok)
        return tok

    def barrier():
        toks = [(n, c) for n, c in pr.cnt.items() if c > 0]
        for nm, e in (("v", nc.vector), ("s", nc.scalar), ("t", nc.tensor),
                      ("g", nc.gpsimd), ("y", nc.sync)):
            pr.wait(e, nm, toks)

    s_cw = sb("s_cw", (128, NDB * 4), F32)
    s_cb = sb("s_cb", (128, NDB), F32)
    s_bdt = sb("s_bdt", (128, NDB), F32)
    s_A = sb("s_A", (128, NDB * NS), F32)
    s_D = sb("s_D", (128, NDB), F32)
    s_lnw = sb("s_lnw", (128, NHB), F32)
    s_lnb = sb("s_lnb", (128, NHB), F32)
    s_consts = sb("s_consts", (128, 4), F32)
    s_id16 = sb("s_id16", (128, 128), BF16)
    s_ones = sb("s_ones", (128, 1), BF16)
    s_diagD = sb("s_diagD", (128, NDB * 128), BF16)
    s_dtr = sb("s_dtr", (RK, DI), BF16)
    s_carry = sb("s_carry", (128, NDB * NS), BF16)
    s_dump = sb("s_dump", (128, 1024), F32)
    s_wdt = sb("s_wdt", (RK, DI), BF16)
    s_gacc = sb("s_gacc", (128, 80), F32)
    s_thr = {"fc": sb("s_thrpfc", (128, 2), F32), "pj": sb("s_thrppj", (128, 2), F32)}

    ONE = s_consts.ap[:, 0:1]
    EPS = s_consts.ap[:, 1:2]

    for src, dst in ((d_cw, s_cw), (d_cb, s_cb), (d_bdt, s_bdt), (d_alog, s_A),
                     (d_D, s_D), (d_lnw, s_lnw), (d_lnb, s_lnb), (d_consts, s_consts),
                     (d_id16, s_id16), (d_ones, s_ones)):
        emit(nc.sync, lambda s=src, d=dst: nc.sync.dma_start(d.ap, s), None,
             writes=[dst], dk=16)
    emit(nc.gpsimd, lambda: nc.gpsimd.dma_start(s_wdt.ap, d_wdt), None,
         writes=[s_wdt], dk=16)

    emit(nc.scalar, lambda: nc.scalar.activation(s_A.ap, s_A.ap, AF.Exp),
         "s", reads=[s_A], extra=s_A.ready)
    s_A.ready = [("s", pr.cnt["s"])]
    emit(nc.vector, lambda: nc.vector.tensor_scalar(s_A.ap, s_A.ap, -1.0, None,
         op0=ALU.mult), "v", reads=[s_A], extra=s_A.ready)
    s_A.ready = [("v", pr.cnt["v"])]
    for db in range(NDB):
        emit(nc.vector, lambda db=db: nc.vector.tensor_scalar(
            s_diagD.ap[:, db * 128:(db + 1) * 128], s_id16.ap,
            s_D.ap[:, db:db + 1], None, op0=ALU.mult),
            "v", writes=[s_diagD], reads=[s_id16, s_D])

    def ln_rows(src_ap, src_buf, st, out_buf, out_ap, nfeat):
        inv = 1.0 / nfeat
        emit(nc.scalar, lambda: nc.scalar.activation(
            s_dump.ap[:, 0:nfeat], src_ap, AF.Square, accum_out=st["ssq"].ap),
            "s", writes=[st["ssq"]], reads=[src_buf])
        emit(nc.vector, lambda: nc.vector.reduce_sum(st["ssum"].ap, src_ap, axis=AX.X),
            "v", writes=[st["ssum"]], reads=[src_buf], extra=st["ssq"].ready)
        emit(nc.vector, lambda: nc.vector.tensor_scalar(st["mu"].ap, st["ssum"].ap,
            inv, None, op0=ALU.mult), "v", writes=[st["mu"]], reads=[st["ssum"]])
        emit(nc.vector, lambda: nc.vector.tensor_scalar(st["e2"].ap, st["ssq"].ap,
            inv, None, op0=ALU.mult), "v", writes=[st["e2"]], reads=[st["ssq"]])
        emit(nc.vector, lambda: nc.vector.tensor_mul(st["mu2"].ap, st["mu"].ap,
            st["mu"].ap), "v", writes=[st["mu2"]], reads=[st["mu"]])
        emit(nc.vector, lambda: nc.vector.tensor_sub(st["var"].ap, st["e2"].ap,
            st["mu2"].ap), "v", writes=[st["var"]], reads=[st["e2"], st["mu2"]])
        emit(nc.scalar, lambda: nc.scalar.activation(st["lv"].ap, st["var"].ap,
            AF.Ln, bias=EPS), "s", writes=[st["lv"]], reads=[st["var"], s_consts])
        emit(nc.scalar, lambda: nc.scalar.activation(st["rstd"].ap, st["lv"].ap,
            AF.Exp, scale=-0.5), "s", writes=[st["rstd"]], reads=[st["lv"]])
        emit(nc.vector, lambda: nc.vector.scalar_tensor_tensor(st["nb"].ap,
            st["mu"].ap, -1.0, st["rstd"].ap, op0=ALU.mult, op1=ALU.mult),
            "v", writes=[st["nb"]], reads=[st["mu"], st["rstd"]])
        emit(nc.scalar, lambda: nc.scalar.activation(out_ap, src_ap, AF.Identity,
            scale=st["rstd"].ap[:, 0:1], bias=st["nb"].ap[:, 0:1]),
            "s", writes=[out_buf], reads=[src_buf, st["rstd"], st["nb"]])

    stWin = contextlib.ExitStack()
    s_win = sb("s_win", (128, NDD * 4096), BF16, stWin)
    for wk in range(8):
        emit(nc.gpsimd, lambda wk=wk: nc.gpsimd.dma_start(
            s_win.ap[:, wk * 4096:(wk + 1) * 4096],
            d_win[:, wk * 4096:(wk + 1) * 4096]), None,
            writes=[s_win], dk=16)

    s_xhatT = sb("s_xhatT", (128, NDD * T), BF16, stWin)
    with contextlib.ExitStack() as stE:
        NE = 4
        s_xe = [sb(f"s_xe{i}", (128, D), F32, stE) for i in range(NE)]
        s_xh = [sb(f"s_xh{i}", (128, D), BF16, stE) for i in range(2)]
        statsE = [{k: sb(f"e_{k}{i}", (128, 1), F32, stE) for k in STK}
                  for i in range(2)]
        p_xt = [ps(f"p_xt{i}", (128, 128), stE) for i in range(2)]
        xtsem = bufsem(s_xhatT)

        def load_xe(ts):
            if ts < TSUB:
                emit(nc.sync, lambda ts=ts: nc.sync.dma_start(
                    s_xe[ts % NE].ap, d_x[ts * 128:(ts + 1) * 128, :]), None,
                    writes=[s_xe[ts % NE]], dk=16)

        for ts in range(NE - 1):
            load_xe(ts)
        for ts in range(TSUB):
            i = ts % 2
            load_xe(ts + NE - 1)
            ln_rows(s_xe[ts % NE].ap, s_xe[ts % NE], statsE[i], s_xh[i], s_xh[i].ap, D)
            for dd in range(NDD):
                pb = p_xt[dd % 2]
                mm_group(pb, [(pb.ap, s_xh[i].ap[:, dd * 128:(dd + 1) * 128],
                               s_id16.ap, True, True)],
                         reads=[s_xh[i], s_id16])
                emit(nc.vector, lambda pb=pb, dd=dd, ts=ts: nc.vector.tensor_copy(
                    s_xhatT.ap[:, dd * T + ts * 128: dd * T + (ts + 1) * 128],
                    pb.ap), xtsem, reads=[pb])
        s_xhatT.ready = [(xtsem, pr.cnt[xtsem])]
        barrier()

    ucb_spill = {}
    sz_spill = {}
    with contextlib.ExitStack() as st1:
        pu = [ps(f"pu{i}", (128, T), st1) for i in range(2)]
        s_u = [sb(f"s_u{i}", (128, T), F32, st1) for i in range(2)]
        s_cv = [sb(f"s_cv{i}", (128, T), F32, st1) for i in range(2)]
        s_uct = [sb(f"s_uct{i}", (128, T), BF16, st1) for i in range(2)]
        s_szt = [sb(f"s_szt{i}", (128, T), BF16, st1) for i in range(2)]
        s_gw = [sb(f"s_gw{i}", (128, 2048), F32, st1) for i in range(2)]
        s_dumpg = sb("s_dumpg", (128, 2048), F32, st1)

        g_tiles = []
        for k in range(32):
            g_tiles.append((d_wfc[k * 128:(k + 1) * 128, :], 1024))
        for k in range(8):
            g_tiles.append((d_wproj[k * 128:(k + 1) * 128, 0:2048], 2048))
            g_tiles.append((d_wproj[k * 128:(k + 1) * 128, 2048:4096], 2048))

        def gamma_tile(gi):
            src, w = g_tiles[gi]
            i = gi % 2
            emit(nc.sync, lambda: nc.sync.dma_start(s_gw[i].ap[:, 0:w], src), None,
                 writes=[s_gw[i]], dk=16)
            emit(nc.scalar, lambda: nc.scalar.activation(
                s_dumpg.ap[:, 0:w], s_gw[i].ap[:, 0:w], AF.Abs,
                accum_out=s_gacc.ap[:, gi:gi + 1]),
                "s", writes=[s_gacc], reads=[s_gw[i]])

        qtiles = []
        for nm, dsrc, ddst, rows, wdt in (("fc", d_wfc, sc_wqfc, 32, 1024),
                                          ("pj", d_wproj, sc_wqproj, 8, 4096)):
            for r in range(rows):
                for cs in range(0, wdt, 1024):
                    qtiles.append((nm, dsrc, ddst, r, cs))
        qstate = {"k": 0}
        s_qw = [None, None]
        s_qc = [None, None]
        s_qq = [None, None]

        def mk_qbufs():
            qqv = s_dumpg.ap.bitcast(BF16)
            for i in range(2):
                for lst, src, apv in (
                        (s_qw, s_gw[0], s_gw[0].ap[:, i * 1024:(i + 1) * 1024]),
                        (s_qc, s_gw[1], s_gw[1].ap[:, i * 1024:(i + 1) * 1024]),
                        (s_qq, s_dumpg, qqv[:, i * 1024:(i + 1) * 1024])):
                    b = Buf(apv)
                    b.ready = list(src.ready) + list(src.users)
                    lst[i] = b

        def qload(k):
            if k >= len(qtiles):
                return
            nm, dsrc, ddst, r, cs = qtiles[k]
            i = k % 2
            qeng = nc.sync if k % 2 == 0 else nc.scalar
            emit(qeng, lambda i=i, r=r, cs=cs, dsrc=dsrc, qeng=qeng: qeng.dma_start(
                s_qw[i].ap, dsrc[r * 128:(r + 1) * 128, cs:cs + 1024]), None,
                writes=[s_qw[i]], dk=16)

        def qstep():
            k = qstate["k"]
            if k >= len(qtiles):
                return
            qstate["k"] = k + 1
            if k == 0:
                qload(0)
                qload(1)
            nm, dsrc, ddst, r, cs = qtiles[k]
            i = k % 2
            thrp = s_thr[nm]
            emit(nc.vector, lambda i=i, thrp=thrp: nc.vector.tensor_scalar(
                s_qc[i].ap, s_qw[i].ap, thrp.ap[:, 0:1], thrp.ap[:, 1:2],
                op0=ALU.min, op1=ALU.max), "v",
                writes=[s_qc[i]], reads=[s_qw[i], thrp])
            emit(nc.vector, lambda i=i: nc.vector.tensor_sub(
                s_qc[i].ap, s_qw[i].ap, s_qc[i].ap), "v",
                writes=[s_qc[i]], reads=[s_qw[i]])
            qload(k + 2)
            emit(nc.scalar, lambda i=i: nc.scalar.activation(
                s_qq[i].ap, s_qc[i].ap, AF.Sign), "s",
                writes=[s_qq[i]], reads=[s_qc[i]])
            emit(nc.sync, lambda i=i, r=r, cs=cs, ddst=ddst: nc.sync.dma_start(
                ddst[r * 128:(r + 1) * 128, cs:cs + 1024], s_qq[i].ap),
                f"wQ{k % 4}", reads=[s_qq[i]], dk=16)

        def do_thresholds():
            for j, (nm, cols, off) in enumerate((("fc", 32, 0), ("pj", 16, 32))):
                g1 = sb(f"s_g1{nm}", (128, 1), F32, st1)
                emit(nc.vector, lambda cols=cols, off=off, g1=g1:
                     nc.vector.reduce_sum(
                         g1.ap, s_gacc.ap[:, off:off + cols], axis=AX.X), "v",
                     writes=[g1], reads=[s_gacc])
                emit(nc.sync, lambda g1=g1, j=j: nc.sync.dma_start(
                    sc_rows[2:3, j * 128:(j + 1) * 128],
                    g1.ap), "wR", reads=[g1], dk=16)
                gr = sb(f"s_gr{nm}", (1, 128), F32, st1)
                emit(nc.sync, lambda gr=gr, j=j: nc.sync.dma_start(
                    gr.ap, sc_rows[2:3, j * 128:(j + 1) * 128]),
                    None, writes=[gr], extra=g1.users, dk=16)
                gs = sb(f"s_gs{nm}", (1, 1), F32, st1)
                emit(nc.vector, lambda gr=gr, gs=gs: nc.vector.reduce_sum(
                    gs.ap, gr.ap, axis=AX.X), "v", writes=[gs], reads=[gr])
                thr2 = sb(f"s_thr2{nm}", (1, 2), F32, st1)
                emit(nc.vector, lambda gs=gs, thr2=thr2: nc.vector.tensor_scalar(
                    thr2.ap[:, 0:1], gs.ap, 0.5 / NW, 0.5e-6,
                    op0=ALU.mult, op1=ALU.add),
                    "v", writes=[thr2], reads=[gs])
                emit(nc.vector, lambda thr2=thr2: nc.vector.tensor_scalar(
                    thr2.ap[:, 1:2], thr2.ap[:, 0:1], -1.0, None, op0=ALU.mult),
                    "v", reads=[thr2], extra=thr2.ready)
                t2tok = ("v", pr.cnt["v"])
                thr2.ready = [t2tok]
                emit(nc.sync, lambda thr2=thr2, j=j: nc.sync.dma_start(
                    sc_rows[3:4, j * 2:j * 2 + 2], thr2.ap),
                    "wR", reads=[thr2], dk=16)
                thrp = s_thr[nm]
                emit(nc.sync, lambda thrp=thrp, j=j: nc.sync.dma_start(
                    thrp.ap, bcastP(sc_rows[3:4, j * 2:j * 2 + 2])),
                    None, writes=[thrp], extra=thr2.users, dk=16)

        pend_silu = None
        pend_zsilu = None

        def do_zsilu(pi):
            i, db = pi
            emit(nc.scalar, lambda i=i: nc.scalar.activation(
                s_szt[i].ap, pu[i].ap, AF.Silu), "s",
                writes=[s_szt[i]], reads=[pu[i]])
            sz_spill[db] = emit(nc.sync, lambda i=i, db=db: nc.sync.dma_start(
                sc_sz[db * 128:(db + 1) * 128, :], s_szt[i].ap), f"wZ{i}",
                reads=[s_szt[i]], dk=16)

        def do_silu(pi):
            i, db = pi
            cv = s_cv[i]
            emit(nc.scalar, lambda i=i, db=db, cv=cv: nc.scalar.activation(
                s_uct[i].ap, cv.ap, AF.Silu, bias=s_cb.ap[:, db:db + 1]),
                "s", writes=[s_uct[i]], reads=[cv, s_cb])
            ucb_spill[db] = emit(nc.sync, lambda i=i, db=db: nc.sync.dma_start(
                sc_ucb[db * 128:(db + 1) * 128, :], s_uct[i].ap), f"wU{i}",
                reads=[s_uct[i]], dk=16)

        for fi in range(32):
            i = fi % 2
            items = []
            for db in range(NDD):
                for q in range(4):
                    items.append((pu[i].ap[:, q * 512:(q + 1) * 512],
                                  s_win.ap[:, db * 4096 + fi * 128: db * 4096 + (fi + 1) * 128],
                                  s_xhatT.ap[:, db * T + q * 512: db * T + (q + 1) * 512],
                                  db == 0, db == NDD - 1))
            mm_group(pu[i], items, reads=[s_win, s_xhatT])
            if fi < 16:
                db = fi
                emit(nc.scalar, lambda i=i: nc.scalar.copy(s_u[i].ap, pu[i].ap), "s",
                     writes=[s_u[i]], reads=[pu[i]])
                if pend_silu is not None:
                    do_silu(pend_silu)
                    pend_silu = None
                cv = s_cv[i]
                emit(nc.vector, lambda i=i, db=db, cv=cv: nc.vector.tensor_scalar(
                    cv.ap, s_u[i].ap, s_cw.ap[:, db * 4 + 3:db * 4 + 4], None,
                    op0=ALU.mult), "v", writes=[cv], reads=[s_u[i], s_cw])
                for k in range(3):
                    sh = 3 - k
                    emit(nc.vector, lambda i=i, db=db, k=k, sh=sh, cv=cv:
                         nc.vector.scalar_tensor_tensor(
                             cv.ap[:, sh:T], s_u[i].ap[:, 0:T - sh],
                             s_cw.ap[:, db * 4 + k:db * 4 + k + 1],
                             cv.ap[:, sh:T], op0=ALU.mult, op1=ALU.add),
                         "v", writes=[cv], reads=[s_u[i], s_cw])
                pend_silu = (i, db)
            else:
                if pend_silu is not None:
                    do_silu(pend_silu)
                    pend_silu = None
                db = fi - 16
                if pend_zsilu is not None:
                    do_zsilu(pend_zsilu)
                pend_zsilu = (i, db)
            if fi < 24:
                gamma_tile(2 * fi)
                gamma_tile(2 * fi + 1)
            else:
                if fi == 24:
                    do_thresholds()
                    mk_qbufs()
                for _ in range(6):
                    qstep()

        if pend_zsilu is not None:
            do_zsilu(pend_zsilu)
            pend_zsilu = None
        while qstate["k"] < len(qtiles):
            qstep()
        barrier()
    stWin.close()

    qfence = [("v", pr.cnt["v"]), ("s", pr.cnt["s"])]
    for _qi in range(4):
        if pr.cnt.get(f"wQ{_qi}", 0):
            qfence.append((f"wQ{_qi}", pr.cnt[f"wQ{_qi}"]))
    FENCE[0] = qfence

    stW = contextlib.ExitStack()
    s_wout = sb("s_wout", (128, NDB * 1024), BF16, stW)
    for wk in range(8):
        emit(nc.gpsimd, lambda wk=wk: nc.gpsimd.dma_start(
            s_wout.ap[:, wk * 2048:(wk + 1) * 2048],
            d_wout[:, wk * 2048:(wk + 1) * 2048]), None,
            writes=[s_wout], dk=16)
    with contextlib.ExitStack() as st2:
        s_wx = sb("s_wx", (128, NDB * 160), BF16, st2)
        for wk in range(4):
            emit(nc.gpsimd, lambda wk=wk: nc.gpsimd.dma_start(
                s_wx.ap[:, wk * 640:(wk + 1) * 640],
                d_wx[:, wk * 640:(wk + 1) * 640]), None,
                writes=[s_wx], dk=16)
        p_dtr = ps("p_dtr", (RK, T), st2)
        p_bc = ps("p_bc", (32, T), st2)
        s_uc2 = [sb(f"s_uc2{i}", (128, T), BF16, st2) for i in range(4)]

        def load_uc2(db):
            if db >= NDB:
                return
            i = db % 4
            qe = nc.sync if db % 2 == 0 else nc.scalar
            emit(qe, lambda db=db, i=i, qe=qe: qe.dma_start(
                s_uc2[i].ap, sc_ucb[db * 128:(db + 1) * 128, :]), None,
                writes=[s_uc2[i]],
                extra=[("wU0", pr.cnt.get("wU0", 0)),
                       ("wU1", pr.cnt.get("wU1", 0))], dk=16)

        for db in range(3):
            load_uc2(db)
        for db in range(NDB):
            i = db % 4
            load_uc2(db + 3)
            deps = _deps_for([p_dtr, p_bc] if db == 0 else [], [s_wx, s_uc2[i]], [])
            pr.wait(nc.tensor, "t", deps)
            last = None
            for q in range(4):
                last = nc.tensor.matmul(
                    p_dtr.ap[:, q * 512:(q + 1) * 512],
                    s_wx.ap[:, db * 160: db * 160 + 128],
                    s_uc2[i].ap[:, q * 512:(q + 1) * 512],
                    start=(db == 0), stop=(db == NDB - 1))
            for q in range(4):
                last = nc.tensor.matmul(
                    p_bc.ap[:, q * 512:(q + 1) * 512],
                    s_wx.ap[:, db * 160 + 128: db * 160 + 160],
                    s_uc2[i].ap[:, q * 512:(q + 1) * 512],
                    start=(db == 0), stop=(db == NDB - 1))
            tok2 = pr.inc(last, "t", 1)
            s_uc2[i].users.append(tok2)
            if db == NDB - 1:
                p_dtr.ready = [tok2]
                p_bc.ready = [tok2]
        emit(nc.scalar, lambda: nc.scalar.copy(s_dtr.ap, p_dtr.ap), "s",
             writes=[s_dtr], reads=[p_dtr])
        s_bcv = sb("s_bcv", (32, T), BF16, st2)
        emit(nc.scalar, lambda: nc.scalar.copy(s_bcv.ap, p_bc.ap), "s",
             writes=[s_bcv], reads=[p_bc])
        b_spill = emit(nc.sync, lambda: nc.sync.dma_start(sc_b, s_bcv.ap[0:16, :]),
                       "wB", reads=[s_bcv], dk=16)
        c_spill = emit(nc.sync, lambda: nc.sync.dma_start(sc_c, s_bcv.ap[16:32, :]),
                       "wB", reads=[s_bcv], dk=16)
        barrier()

    with contextlib.ExitStack() as st3:
        _sb0 = sb("s_brepb0", (128, NS * TC), BF16, st3)
        s_brepb = [_sb0, _sb0]
        _sc0 = sb("s_crepb0", (128, NS * TC), BF16, st3)
        s_crepb = [_sc0, _sc0]
        s_e = [sb("s_e0", (128, TC), F32, st3)] * 2
        s_dt = [sb(f"s_dt{i}", (128, TC), F32, st3) for i in range(2)]
        s_wt = [sb("s_wt0", (128, TC), BF16, st3)] * 2
        s_da = [sb(f"s_da{i}", (128, TC), BF16, st3) for i in range(6)]
        s_dbufb = [sb(f"s_dbuf{i}", (128, NS * TC), BF16, st3) for i in range(2)]
        _sh0 = sb("s_hb0", (128, NS * TC), BF16, st3)
        s_hb = [_sh0, _sh0]
        s_uc3 = [sb(f"s_uc3{i}", (128, TC), BF16, st3) for i in range(2)]
        s_sz3 = [sb(f"s_sz3{i}", (128, TC), BF16, st3) for i in range(2)]
        s_g = sb("s_g", (128, NDB * TC), BF16, st3)
        p_p3 = [ps(f"p_p3{i}", (RK, TC), st3) for i in range(2)]
        p_p4 = ps("p_p4", (128, 1024), st3)
        p_y = [Buf(p_p4.ap[:, 0:TC]), Buf(p_p4.ap[:, TC:2 * TC])]
        s_x0g = [sb("s_x0g0", (128, D), F32, st3)] * 2
        s_x1t = [sb("s_x1t0", (128, D), F32, st3)] * 2
        s_xh1t = [sb("s_xh1t0", (128, D), BF16, st3)] * 2
        stats4 = [{k: sb(f"g4_{k}{i}", (128, 1), F32, st3) for k in STK}
                  for i in range(2)]
        stats5 = stats4

        s_wq4 = [sb(f"s_wq4_{i}", (128, NDD * 256), BF16, st3,
                    is_transpose_dest=True) for i in range(2)]
        s_xh1Tq = sb("s_xh1Tq", (128, NDD * 512), BF16, st3,
                     is_transpose_dest=True)
        s_abq = [sb(f"s_abq{i}", (128, 512), BF16, st3) for i in range(2)]
        s_abr = [sb(f"s_abr{i}", (128, 512), BF16, st3) for i in range(3)]
        s_sqb = [sb(f"s_sqb{i}", (128, 512), BF16, st3) for i in range(2)]
        p_a = [ps(f"p_a{i}", (128, 512), st3) for i in range(2)]
        p_s1 = ps("p_s1", (1, 512), st3)
        p_s2 = ps("p_s2", (1, 512), st3)
        s_mub = sb("s_mub", (128, 512), BF16, st3)
        s_rsb = sb("s_rsb", (128, 512), BF16, st3)
        s_mu1 = sb("s_mu1", (1, 512), BF16, st3)
        s_rs1 = sb("s_rs1", (1, 512), F32, st3)
        s_rs16 = sb("s_rs16", (1, 512), BF16, st3)
        s_t1 = sb("s_t15", (128, 512), F32, st3)
        s_aht = [sb(f"s_aht{i}", (128, 512), BF16, st3) for i in range(2)]
        tokqs = [(f"wQ{_qi}", pr.cnt.get(f"wQ{_qi}", 0)) for _qi in range(4)]
        ab_dbuf = [Buf(sc_ab[hb]) for hb in range(NHB)]

        def p5_load_slot(tq, g):
            if g >= 16:
                return
            slot = s_wq4[g % 2]
            ssem = bufsem(slot)
            for db in range(NDD):
                emit(nc.sync, lambda g=g, db=db, slot=slot: nc.sync.dma_start_transpose(
                    slot.ap[:, db * 256:(db + 1) * 256],
                    sc_wqfc[g * 256:(g + 1) * 256, db * 128:(db + 1) * 128]), ssem,
                    writes=[slot] if db == 0 else [],
                    extra=(tokqs + slot.ready + slot.users) if db == 0 else [], dk=16)
            slot.ready = [(ssem, pr.cnt[ssem])]

        def p5_xpose(tq):
            t0 = tq * 512
            hsem = bufsem(s_xh1Tq)
            tokh = ("wH1", pr.cnt.get("wH1", 0))
            for db in range(NDD):
                emit(nc.sync, lambda db=db, t0=t0: nc.sync.dma_start_transpose(
                    s_xh1Tq.ap[:, db * 512:(db + 1) * 512],
                    sc_xh1[t0:t0 + 512, db * 128:(db + 1) * 128]), hsem,
                    writes=[s_xh1Tq] if db == 0 else [],
                    extra=([tokh] + s_xh1Tq.ready + s_xh1Tq.users) if db == 0 else [],
                    dk=16)
            s_xh1Tq.ready = [(hsem, pr.cnt[hsem])]

        def p5_colsums(hb, sq_dep):
            i = hb % 2
            mm_group(p_s1, [(p_s1.ap, s_ones.ap, s_abq[i].ap,
                             hb == 0, hb == NHB - 1)],
                     reads=[s_ones, s_abq[i]], extra=sq_dep)
            mm_group(p_s2, [(p_s2.ap, s_ones.ap, s_sqb[i].ap,
                             hb == 0, hb == NHB - 1)],
                     reads=[s_ones, s_sqb[i]])

        def p5_gelu2(hb0):
            if hb0 > 0:
                p5_colsums(hb0 - 1, prev_sq[0])
            for hb in (hb0, hb0 + 1):
                i = hb % 2
                emit(nc.scalar, lambda i=i: nc.scalar.activation(
                    s_abq[i].ap, p_a[i].ap, AF.Gelu), "s",
                    writes=[s_abq[i]], reads=[p_a[i]])
                emit(nc.gpsimd, lambda i=i, hb=hb: nc.gpsimd.dma_start(
                    sc_ab[hb], s_abq[i].ap), f"wAB{i}",
                    writes=[ab_dbuf[hb]], reads=[s_abq[i]], dk=16)
            sqtoks = {}
            for hb in (hb0, hb0 + 1):
                i = hb % 2
                emit(nc.scalar, lambda i=i: nc.scalar.activation(
                    s_sqb[i].ap, s_abq[i].ap, AF.Square), "s",
                    writes=[s_sqb[i]], reads=[s_abq[i]])
                sqtoks[hb] = [("s", pr.cnt["s"])]
            p5_colsums(hb0, sqtoks[hb0])
            prev_sq[0] = sqtoks[hb0 + 1]

        def p5_mm2(tq, hb2):
            if hb2 > 0:
                p5_gelu2(2 * hb2 - 2)
            for hb in (2 * hb2, 2 * hb2 + 1):
                i = hb % 2
                slot = s_wq4[hb2 % 2]
                o = (hb - 2 * hb2) * 128
                items = [(p_a[i].ap,
                          slot.ap[:, db * 256 + o: db * 256 + o + 128],
                          s_xh1Tq.ap[:, db * 512:(db + 1) * 512],
                          db == 0, db == NDD - 1) for db in range(NDD)]
                mm_group(p_a[i], items, reads=[slot, s_xh1Tq])
            p5_load_slot(tq, hb2 + 2)

        def p5_reload(hb):
            if hb >= NHB:
                return
            j = hb % 3
            emit(nc.gpsimd, lambda hb=hb, j=j: nc.gpsimd.dma_start(
                s_abr[j].ap, sc_ab[hb]), None, writes=[s_abr[j]],
                reads=[ab_dbuf[hb]],
                extra=[("wAB0", pr.cnt.get("wAB0", 0)),
                       ("wAB1", pr.cnt.get("wAB1", 0))], dk=16)

        def p5_stats(tq):
            t0 = tq * 512
            p5_gelu2(NHB - 2)
            p5_colsums(NHB - 1, prev_sq[0])
            r_e2 = p_s2.ap[0:1, :]
            r_lv = p_a[0].ap[0:1, 0:512]
            emit(nc.vector, lambda: nc.vector.tensor_scalar(
                s_mu1.ap, p_s1.ap, 1.0 / 4096, None, op0=ALU.mult), "v",
                writes=[s_mu1], reads=[p_s1])
            emit(nc.vector, lambda r_e2=r_e2: nc.vector.tensor_scalar(
                r_e2, r_e2, 1.0 / 4096, None, op0=ALU.mult), "v",
                reads=[p_s2], extra=p_s2.ready)
            emit(nc.vector, lambda: nc.vector.tensor_mul(
                s_rs1.ap, s_mu1.ap, s_mu1.ap), "v", reads=[s_mu1],
                writes=[s_rs1])
            emit(nc.vector, lambda r_e2=r_e2: nc.vector.tensor_sub(
                r_e2, r_e2, s_rs1.ap), "v", reads=[p_s2, s_rs1],
                extra=[("v", pr.cnt["v"])])
            vtok = ("v", pr.cnt["v"])
            emit(nc.scalar, lambda t0=t0: nc.scalar.dma_start(
                sc_rows16[0:1, t0:t0 + 512], s_mu1.ap), "wMRa",
                reads=[s_mu1], dk=16)
            emit(nc.scalar, lambda r_lv=r_lv, r_e2=r_e2: nc.scalar.activation(
                r_lv, r_e2, AF.Ln, bias=EPS[0:1, :]), "s",
                reads=[p_a[0], s_consts], extra=[vtok])
            emit(nc.scalar, lambda r_lv=r_lv: nc.scalar.activation(
                s_rs16.ap, r_lv, AF.Exp, scale=-0.5), "s",
                writes=[s_rs16], reads=[p_a[0]],
                extra=[("s", pr.cnt["s"])] + s_rs16.users)
            emit(nc.scalar, lambda t0=t0: nc.scalar.dma_start(
                sc_rows16[1:2, t0:t0 + 512], s_rs16.ap), "wMRb",
                reads=[s_rs16], dk=16)
            mrtoks = [("wMRa", pr.cnt["wMRa"]), ("wMRb", pr.cnt["wMRb"])]
            emit(nc.scalar, lambda t0=t0: nc.scalar.dma_start(
                s_mub.ap, bcastP(sc_rows16[0:1, t0:t0 + 512])), None,
                writes=[s_mub], extra=mrtoks, dk=16)
            emit(nc.scalar, lambda t0=t0: nc.scalar.dma_start(
                s_rsb.ap, bcastP(sc_rows16[1:2, t0:t0 + 512])), None,
                writes=[s_rsb], extra=mrtoks, dk=16)
            for pb in (p_s1, p_s2, p_a[0], p_a[1]):
                pb.users.append(("v", pr.cnt["v"]))
                pb.users.append(("s", pr.cnt["s"]))
                pb.users.append(("wMRa", pr.cnt["wMRa"]))
                pb.users.append(("wMRb", pr.cnt["wMRb"]))
            for hb in range(3):
                p5_reload(hb)

        def p5_apply4(tq, k):
            t0 = tq * 512
            for hb in range(4 * k, 4 * k + 4):
                j = hb % 3
                i = hb % 2
                emit(nc.vector, lambda j=j: nc.vector.tensor_sub(
                    s_t1.ap, s_abr[j].ap, s_mub.ap), "v",
                    writes=[s_t1], reads=[s_abr[j], s_mub])
                emit(nc.vector, lambda: nc.vector.tensor_mul(
                    s_t1.ap, s_t1.ap, s_rsb.ap), "v",
                    writes=[s_t1], reads=[s_rsb], extra=s_t1.ready)
                emit(nc.vector, lambda i=i, hb=hb: nc.vector.tensor_scalar(
                    s_aht[i].ap, s_t1.ap, s_lnw.ap[:, hb:hb + 1],
                    s_lnb.ap[:, hb:hb + 1], op0=ALU.mult, op1=ALU.add), "v",
                    writes=[s_aht[i]], reads=[s_t1, s_lnw, s_lnb])
                emit(nc.scalar, lambda i=i, hb=hb, t0=t0: nc.scalar.dma_start(
                    sc_ahat[hb:hb + 1, :, t0:t0 + 512], s_aht[i].ap), f"wA{i}",
                    reads=[s_aht[i]], dk=16)
                p5_reload(hb + 3)

        def p5_steps(tq):
            steps = [lambda tq=tq: p5_xpose(tq),
                     lambda tq=tq: p5_load_slot(tq, 0),
                     lambda tq=tq: p5_load_slot(tq, 1)]
            for hb2 in range(NHB // 2):
                steps.append(lambda tq=tq, hb2=hb2: p5_mm2(tq, hb2))
            steps.append(lambda tq=tq: p5_stats(tq))
            for k in range(NHB // 4):
                steps.append(lambda tq=tq, k=k: p5_apply4(tq, k))
            return steps

        prev_sq = [None]
        p5q = []
        p5_done = [0]

        def bc3(dst2d, src, t0):
            base = src[0:1, t0:t0 + TC]
            dims = [list(dd) for dd in base.ap]

            def ap_for(q):
                b4 = src[4 * q:4 * q + 1, t0:t0 + TC]
                return bass.AP(b4.tensor, b4.offset, [[0, 128], [T, 4], [1, TC]])
            return ap_for

        def load_reps(tc):
            if tc >= NCH:
                return
            t0 = tc * TC
            k = tc % 2
            bq = bc3(None, sc_b, t0)
            cq = bc3(None, sc_c, t0)
            bs, cs2 = bufsem(s_brepb[k]), bufsem(s_crepb[k])
            for q in range(4):
                emit(nc.scalar, lambda q=q, k=k, bq=bq: nc.scalar.dma_start(
                    s_brepb[k].ap[:, q * 4 * TC:(q + 1) * 4 * TC], bq(q)), bs,
                    writes=[s_brepb[k]] if q == 0 else [],
                    extra=([("wB", pr.cnt["wB"])] + qfence
                           + s_brepb[k].ready + s_brepb[k].users)
                    if q == 0 else [], dk=16)
                emit(nc.scalar, lambda q=q, k=k, cq=cq: nc.scalar.dma_start(
                    s_crepb[k].ap[:, q * 4 * TC:(q + 1) * 4 * TC], cq(q)), cs2,
                    writes=[s_crepb[k]] if q == 0 else [],
                    extra=([("wB", pr.cnt["wB"])] + qfence
                           + s_crepb[k].ready + s_crepb[k].users)
                    if q == 0 else [], dk=16)
            s_brepb[k].ready = [(bs, pr.cnt[bs])]
            s_crepb[k].ready = [(cs2, pr.cnt[cs2])]

        def load_uc_sz(tc, db):
            if db >= NDB:
                tc, db = tc + 1, db - NDB
            if tc >= NCH:
                return
            t0 = tc * TC
            i = db % 2
            emit(nc.sync, lambda db=db, i=i, t0=t0: nc.sync.dma_start(
                s_uc3[i].ap, sc_ucb[db * 128:(db + 1) * 128, t0:t0 + TC]), None,
                writes=[s_uc3[i]],
                extra=[("wU0", pr.cnt.get("wU0", 0)),
                       ("wU1", pr.cnt.get("wU1", 0))], dk=16)
            emit(nc.scalar, lambda db=db, i=i, t0=t0: nc.scalar.dma_start(
                s_sz3[i].ap, sc_sz[db * 128:(db + 1) * 128, t0:t0 + TC]), None,
                writes=[s_sz3[i]],
                extra=[("wZ0", pr.cnt.get("wZ0", 0)),
                       ("wZ1", pr.cnt.get("wZ1", 0))], dk=16)

        load_uc_sz(0, 0)
        pend_gate = None
        da_next = None

        def do_gate(pg):
            i, db = pg
            emit(nc.vector, lambda i=i, db=db: nc.vector.tensor_mul(
                s_g.ap[:, db * TC:(db + 1) * TC], p_y[i].ap, s_sz3[i].ap), "v",
                reads=[p_y[i], s_sz3[i]],
                extra=(s_g.ready + s_g.users) if db == 0 else [])

        for tc in range(NCH):
            t0 = tc * TC
            if tc == 0:
                load_reps(0)
            s_brep = s_brepb[tc % 2]
            s_crep = s_crepb[tc % 2]
            for db in range(NDB):
                i = db % 2
                if tc == 0 and db == 0:
                    mm_group(p_p3[0], [(p_p3[0].ap, s_wdt.ap[0:RK, 0:128],
                                       s_dtr.ap[:, 0:TC], True, True)],
                             reads=[s_wdt, s_dtr])
                    emit(nc.scalar, lambda: nc.scalar.activation(
                        s_e[0].ap, p_p3[0].ap, AF.Exp, bias=s_bdt.ap[:, 0:1]), "s",
                        writes=[s_e[0]], reads=[p_p3[0], s_bdt])
                    emit(nc.scalar, lambda: nc.scalar.activation(
                        s_dt[0].ap, s_e[0].ap, AF.Ln, bias=ONE), "s",
                        writes=[s_dt[0]], reads=[s_e[0], s_consts])
                emit(nc.vector, lambda i=i: nc.vector.tensor_mul(
                    s_wt[i].ap, s_dt[i].ap, s_uc3[i].ap), "v",
                    writes=[s_wt[i]], reads=[s_dt[i], s_uc3[i]])
                da_toks = da_next if da_next is not None else {}
                da_next = None

                def emit_da(n, i=i, db=db, da_toks=da_toks):
                    if n >= NS:
                        return
                    j = n % 6
                    da_toks[n] = emit(
                        nc.scalar, lambda i=i, j=j, db=db, n=n: nc.scalar.activation(
                            s_da[j].ap, s_dt[i].ap, AF.Exp,
                            scale=s_A.ap[:, db * NS + n: db * NS + n + 1]), "s",
                        writes=[s_da[j]], reads=[s_dt[i], s_A])

                for n in range(6):
                    if n not in da_toks:
                        emit_da(n)
                s_dbuf = s_dbufb[i]
                s_h = s_hb[i]
                dbuf_war = s_dbuf.ready + s_dbuf.users
                wtr = bass.AP(s_wt[i].ap.tensor, s_wt[i].ap.offset,
                              [[TC, 128], [0, NS], [1, TC]])
                emit(nc.vector, lambda s_brep=s_brep, s_dbuf=s_dbuf, wtr=wtr:
                     nc.vector.tensor_mul(
                         s_dbuf.ap.rearrange("p (n t) -> p n t", t=TC), wtr,
                         s_brep.ap.rearrange("p (n t) -> p n t", t=TC)),
                     "v", reads=[s_wt[i], s_brep], extra=dbuf_war)
                dbuf_tok = ("v", pr.cnt["v"])
                s_dbuf.ready = [dbuf_tok]
                s_dbuf.users = []
                ndb, ntc = db + 1, tc
                if ndb >= NDB:
                    ndb, ntc = 0, tc + 1
                if ntc < NCH:
                    nt0 = ntc * TC
                    ni = ndb % 2
                    mm_group(p_p3[ni],
                             [(p_p3[ni].ap,
                               s_wdt.ap[:, ndb * 128:(ndb + 1) * 128],
                               s_dtr.ap[:, nt0:nt0 + TC], True, True)],
                             reads=[s_wdt, s_dtr])
                    emit(nc.scalar, lambda ni=ni, ndb=ndb: nc.scalar.activation(
                        s_e[ni].ap, p_p3[ni].ap, AF.Exp,
                        bias=s_bdt.ap[:, ndb:ndb + 1]), "s",
                        writes=[s_e[ni]], reads=[p_p3[ni], s_bdt])
                    emit(nc.scalar, lambda ni=ni: nc.scalar.activation(
                        s_dt[ni].ap, s_e[ni].ap, AF.Ln, bias=ONE), "s",
                        writes=[s_dt[ni]], reads=[s_e[ni], s_consts])
                h_war = s_h.ready + s_h.users
                for n in range(NS):
                    j = n % 6
                    if tc == 0:
                        init = 0.0
                    else:
                        init = s_carry.ap[:, db * NS + n: db * NS + n + 1]
                    emit(nc.vector, lambda j=j, n=n, init=init, s_h=s_h, s_dbuf=s_dbuf:
                         nc.vector.tensor_tensor_scan(
                             s_h.ap[:, n * TC:(n + 1) * TC], s_da[j].ap,
                             s_dbuf.ap[:, n * TC:(n + 1) * TC], init,
                             op0=ALU.mult, op1=ALU.add),
                         "v", reads=[s_da[j]] + ([s_carry] if tc > 0 else []),
                         extra=[da_toks[n]]
                         + ((h_war + [dbuf_tok]) if n == 0 else []))
                    emit_da(n + 6)
                htok = ("v", pr.cnt["v"])
                s_h.ready = [htok]
                s_h.users = []
                if pend_gate is not None:
                    do_gate(pend_gate)
                    pend_gate = None
                hview = s_h.ap.rearrange("p (n t) -> p n t", t=TC)
                carry_tok = emit(nc.scalar, lambda db=db, hview=hview:
                    nc.scalar.copy(
                        s_carry.ap[:, db * NS:(db + 1) * NS].rearrange(
                            "p (n o) -> p n o", o=1), hview[:, :, TC - 1:TC]), "s",
                    writes=[s_carry], reads=[s_h])
                emit(nc.vector, lambda s_crep=s_crep, s_h=s_h, s_dbuf=s_dbuf:
                    nc.vector.tensor_mul(
                        s_dbuf.ap, s_h.ap, s_crep.ap), "v",
                    reads=[s_h, s_crep],
                    extra=s_dbuf.ready + s_dbuf.users)
                hctok = ("v", pr.cnt["v"])
                s_dbuf.ready = [hctok]
                s_dbuf.users = []
                s_h.users.append(hctok)
                items = [(p_y[i].ap, s_id16.ap, s_dbuf.ap[:, n * TC:(n + 1) * TC],
                          n == 0, False) for n in range(NS)]
                items.append((p_y[i].ap, s_diagD.ap[:, db * 128:(db + 1) * 128],
                              s_uc3[i].ap, False, True))
                mm_group(p_y[i], items, reads=[s_dbuf, s_id16, s_diagD, s_uc3[i]])
                load_uc_sz(tc, db + 1)
                pend_gate = (i, db)
                ndb2, ntc2 = db + 1, tc
                if ndb2 >= NDB:
                    ndb2, ntc2 = 0, tc + 1
                if ntc2 < NCH:
                    da_next = {}

                    def emit_da_pre(n, i2=ndb2 % 2, db2=ndb2, dd=da_next):
                        j = n % 6
                        dd[n] = emit(
                            nc.scalar,
                            lambda: nc.scalar.activation(
                                s_da[j].ap, s_dt[i2].ap, AF.Exp,
                                scale=s_A.ap[:, db2 * NS + n: db2 * NS + n + 1]),
                            "s", writes=[s_da[j]], reads=[s_dt[i2], s_A])

                    for n in range(3):
                        emit_da_pre(n)
                want = ((db + 1) * 28 + 19) // 20
                while p5q and p5_done[0] < want:
                    p5q.pop(0)()
                    p5_done[0] += 1
            if pend_gate is not None:
                do_gate(pend_gate)
                pend_gate = None
            load_reps(tc + 1)
            gtok = ("v", pr.cnt["v"])
            s_g.ready = [gtok]
            s_g.users = []
            for su in range(TC // 128):
                tglob = tc * (TC // 128) + su
                i = su % 2
                items = []
                for db in range(NDB):
                    stat = s_g.ap[:, db * TC + su * 128: db * TC + (su + 1) * 128]
                    for dmh in range(2):
                        items.append((p_p4.ap[:, dmh * 512:(dmh + 1) * 512], stat,
                                      s_wout.ap[:, db * 1024 + dmh * 512: db * 1024 + (dmh + 1) * 512],
                                      db == 0, db == NDB - 1))
                mm_group(p_p4, items, reads=[s_g, s_wout])
                emit(nc.sync, lambda tglob=tglob, i=i: nc.sync.dma_start(
                    s_x0g[i].ap, d_x[tglob * 128:(tglob + 1) * 128, :]), None,
                    writes=[s_x0g[i]], dk=16)
                ln_rows(p_p4.ap, p_p4, stats4[i], s_x1t[i], s_x1t[i].ap, D)
                emit(nc.vector, lambda i=i: nc.vector.tensor_add(
                    s_x1t[i].ap, s_x1t[i].ap, s_x0g[i].ap), "v",
                    writes=[s_x1t[i]], reads=[s_x0g[i]], extra=s_x1t[i].ready)
                emit(nc.sync, lambda tglob=tglob, i=i: nc.sync.dma_start(
                    sc_x1[tglob * 128:(tglob + 1) * 128, :], s_x1t[i].ap), "wX1",
                    reads=[s_x1t[i]], dk=16)
                ln_rows(s_x1t[i].ap, s_x1t[i], stats5[i], s_xh1t[i], s_xh1t[i].ap, D)
                emit(nc.sync, lambda tglob=tglob, i=i: nc.sync.dma_start(
                    sc_xh1[tglob * 128:(tglob + 1) * 128, :], s_xh1t[i].ap), "wH1",
                    reads=[s_xh1t[i]], dk=16)
                if p5q:
                    p5q.pop(0)()
                    p5_done[0] += 1
            for pyh in p_y:
                pyh.users.append(("s", pr.cnt["s"]))
                pyh.users.append(("v", pr.cnt["v"]))
            p5q.extend(p5_steps(tc))
            p5_done[0] = 0
        for f in p5q:
            f()
        barrier()
    stW.close()

    with contextlib.ExitStack() as st6:
        s_wqprojT = sb("s_wqprojT", (128, NHB * 1024), BF16, st6,
                       is_transpose_dest=True)
        s_ah6 = [sb(f"s_ah6{i}", (128, NHB * 128), BF16, st6) for i in range(2)]
        p_p6b = [ps(f"p_p6{i}", (128, 1024), st6) for i in range(2)]
        s_v6 = [sb(f"s_v6{i}", (128, D), F32, st6) for i in range(2)]
        s_x1r = [sb(f"s_x1r{i}", (128, D), F32, st6) for i in range(2)]
        s_o = [sb(f"s_o{i}", (128, D), F32, st6) for i in range(2)]
        stats6 = [{k: sb(f"g6_{k}{i}", (128, 1), F32, st6) for k in STK}
                  for i in range(2)]

        def load_p6(su):
            if su >= TSUB:
                return
            i = su % 2
            emit(nc.sync, lambda su=su, i=i: nc.sync.dma_start(
                s_ah6[i].ap, sc_ahat[:, :, su * 128:(su + 1) * 128].rearrange(
                    "h p t -> p h t")), None, writes=[s_ah6[i]], dk=16)
            emit(nc.scalar, lambda su=su, i=i: nc.scalar.dma_start(
                s_x1r[i].ap, sc_x1[su * 128:(su + 1) * 128, :]), None,
                writes=[s_x1r[i]], dk=16)

        load_p6(0)
        tokps = [(f"wQ{_qi}", pr.cnt.get(f"wQ{_qi}", 0)) for _qi in range(4)]
        pj_tok = {}
        for hb in range(NHB):
            qs = f"qpj{hb // 8}"
            emit(nc.sync, lambda hb=hb: nc.sync.dma_start_transpose(
                s_wqprojT.ap[:, hb * 1024:(hb + 1) * 1024],
                sc_wqproj[:, hb * 128:(hb + 1) * 128]), qs, extra=tokps, dk=16)
        for hb in range(NHB):
            pj_tok[hb] = (f"qpj{hb // 8}", 128)
        s_wqprojT.ready = [(f"qpj{g}", 128) for g in range(4)]

        for su in range(TSUB):
            i = su % 2
            p_p6 = p_p6b[i]
            load_p6(su + 1)
            items = []
            for hb in range(NHB):
                stat = s_ah6[i].ap[:, hb * 128:(hb + 1) * 128]
                for dmh in range(2):
                    items.append((p_p6.ap[:, dmh * 512:(dmh + 1) * 512], stat,
                                  s_wqprojT.ap[:, hb * 1024 + dmh * 512: hb * 1024 + (dmh + 1) * 512],
                                  hb == 0, hb == NHB - 1))
            if su == 0:
                deps = _deps_for([p_p6], [s_ah6[i]], [])
                pr.wait(nc.tensor, "t", deps)
                last = None
                for hb in range(NHB):
                    pr.wait(nc.tensor, "t", [pj_tok[hb]])
                    for dmh in range(2):
                        it = items[hb * 2 + dmh]
                        last = nc.tensor.matmul(it[0], it[1], it[2],
                                                start=it[3], stop=it[4])
                tok6 = pr.inc(last, "t", 1)
                p_p6.ready = [tok6]
                p_p6.users = []
                s_ah6[i].users.append(tok6)
                s_wqprojT.users.append(tok6)
            else:
                mm_group(p_p6, items, reads=[s_ah6[i], s_wqprojT])
            ln_rows(p_p6.ap, p_p6, stats6[i], s_v6[i], s_v6[i].ap, D)
            emit(nc.vector, lambda i=i: nc.vector.tensor_add(
                s_o[i].ap, s_v6[i].ap, s_x1r[i].ap), "v",
                writes=[s_o[i]], reads=[s_v6[i], s_x1r[i]])
            emit(nc.sync, lambda su=su, i=i: nc.sync.dma_start(
                d_out[su * 128:(su + 1) * 128, :], s_o[i].ap), f"dO{i}",
                reads=[s_o[i]], dk=16)

    nc.sync.wait_ge(pr.sem("dO0"), pr.cnt["dO0"])
    nc.sync.wait_ge(pr.sem("dO1"), pr.cnt["dO1"])
    return nc


def _prep_common(inputs):
    f = np.float32

    def r(a):
        return np.ascontiguousarray(a)

    W_in = np.asarray(inputs["L0_attn_W_in"], f)
    W_x = np.asarray(inputs["L0_attn_W_x"], f)
    W_dt = np.asarray(inputs["L0_attn_W_dt"], f)
    W_out = np.asarray(inputs["L0_attn_W_out"], f)
    conv_w = np.asarray(inputs["L0_attn_conv_w"], f)
    conv_b = np.asarray(inputs["L0_attn_conv_b"], f)
    b_dt = np.asarray(inputs["L0_attn_b_dt"], f)
    A_log = np.asarray(inputs["L0_attn_A_log"], f)
    D_p = np.asarray(inputs["L0_attn_D_param"], f)
    Wfc = np.asarray(inputs["L0_ffn_W_fc"], f)
    lnw = np.asarray(inputs["L0_ffn_ln_w"], f)
    lnb = np.asarray(inputs["L0_ffn_ln_b"], f)
    Wproj = np.asarray(inputs["L0_ffn_W_proj"], f)

    consts = np.zeros((128, 4), f)
    consts[:, 0] = 1.0
    consts[:, 1] = 1e-5
    return {
        "w_in_r": r(W_in.reshape(NDD, 128, 4096).transpose(1, 0, 2).reshape(128, NDD * 4096)),
        "w_x_r": r(W_x.reshape(NDB, 128, 160).transpose(1, 0, 2).reshape(128, NDB * 160)),
        "w_dt": r(W_dt),
        "w_out_r": r(W_out.reshape(NDB, 128, 1024).transpose(1, 0, 2).reshape(128, NDB * 1024)),
        "conv_w_r": r(conv_w.reshape(NDB, 128, 4).transpose(1, 0, 2).reshape(128, NDB * 4)),
        "conv_b_r": r(conv_b.reshape(NDB, 128).T),
        "b_dt_r": r(b_dt.reshape(NDB, 128).T),
        "a_log_r": r(A_log.reshape(NDB, 128, NS).transpose(1, 0, 2).reshape(128, NDB * NS)),
        "d_r": r(D_p.reshape(NDB, 128).T),
        "wfc": r(Wfc),
        "wproj": r(Wproj),
        "lnw_r": r(lnw.reshape(NHB, 128).T),
        "lnb_r": r(lnb.reshape(NHB, 128).T),
        "consts": consts,
        "id16": np.eye(128, dtype=BFNP),
        "ones_col16": np.ones((128, 1), dtype=BFNP),
    }


def kernel(**inputs):
    x_all = np.asarray(inputs["L0_attn_x"], np.float32)
    common = _prep_common(inputs)
    nc = build()
    in_maps = [dict(common, x=np.ascontiguousarray(x_all[i])) for i in range(8)]
    res = run_bass_kernel_spmd(nc, in_maps, core_ids=list(range(8)))
    return np.stack([np.asarray(res.results[i]["out"]) for i in range(8)], axis=0)


# version 62
# speedup vs baseline: 1.1853x; 1.0130x over previous
import contextlib
import numpy as np
import ml_dtypes
import concourse.bass as bass
import concourse.mybir as mybir
from concourse.bass_utils import run_bass_kernel_spmd

F32 = mybir.dt.float32
BF16 = mybir.dt.bfloat16
AF = mybir.ActivationFunctionType
ALU = mybir.AluOpType
AX = mybir.AxisListType
BFNP = ml_dtypes.bfloat16

T, D, DI, NS, RK = 2048, 1024, 2048, 16, 128
NDB, NDD, NHB = 16, 8, 32
TSUB = 16
TC, NCH = 512, 4
NW = 4096 * 1024
STK = ("ssq", "ssum", "mu", "e2", "mu2", "var", "lv", "rstd", "nb")


class Pr:

    def __init__(self, nc):
        self.nc = nc
        self.h = {}
        self.cnt = {}
        self.seen = {}

    def sem(self, name):
        if name not in self.h:
            self.h[name] = self.nc.alloc_semaphore("sem_" + name)
            self.cnt[name] = 0
        return self.h[name]

    def wait(self, eng, ename, tokens):
        need = {}
        for (n, v) in tokens:
            if v > need.get(n, 0):
                need[n] = v
        for n, v in sorted(need.items()):
            k = (ename, n)
            if self.seen.get(k, -1) >= v:
                continue
            eng.wait_ge(self.sem(n), v)
            self.seen[k] = v

    def inc(self, ins, name, k=1):
        s = self.sem(name)
        ins.then_inc(s, k)
        self.cnt[name] += k
        return (name, self.cnt[name])


class Buf:
    __slots__ = ("ap", "ready", "users")

    def __init__(self, ap):
        self.ap = ap
        self.ready = []
        self.users = []


_BUFSEM = {}


def bufsem(buf):
    if id(buf) not in _BUFSEM:
        _BUFSEM[id(buf)] = f"dm{len(_BUFSEM)}"
    return _BUFSEM[id(buf)]


def _deps_for(writes, reads, extra):
    deps = list(extra)
    for b in reads:
        deps += b.ready
    for b in writes:
        deps += b.ready + b.users
    return deps


def build(taps=()):
    taps = set(taps)
    _BUFSEM.clear()
    nc = bass.Bass("TRN2", target_bir_lowering=False, debug=False)
    pr = Pr(nc)

    def din(name, shape, dt):
        return nc.dram_tensor(name, list(shape), dt, kind="ExternalInput").ap()

    def dscr(name, shape, dt):
        kind = "ExternalOutput" if name in taps else "Internal"
        return nc.dram_tensor(name, list(shape), dt, kind=kind).ap()

    d_x = din("x", (T, D), F32)
    d_win = din("w_in_r", (128, NDD * 4096), F32)
    d_wx = din("w_x_r", (128, NDB * 160), F32)
    d_wdt = din("w_dt", (RK, DI), F32)
    d_wout = din("w_out_r", (128, NDB * 1024), F32)
    d_cw = din("conv_w_r", (128, NDB * 4), F32)
    d_cb = din("conv_b_r", (128, NDB), F32)
    d_bdt = din("b_dt_r", (128, NDB), F32)
    d_alog = din("a_log_r", (128, NDB * NS), F32)
    d_D = din("d_r", (128, NDB), F32)
    d_wfc = din("wfc", (4096, 1024), F32)
    d_wproj = din("wproj", (1024, 4096), F32)
    d_lnw = din("lnw_r", (128, NHB), F32)
    d_lnb = din("lnb_r", (128, NHB), F32)
    d_consts = din("consts", (128, 4), F32)
    d_id16 = din("id16", (128, 128), BF16)
    d_ones = din("ones_col16", (128, 1), BF16)

    sc_xhat = dscr("sc_xhat", (T, D), BF16)
    sc_ucb = dscr("sc_ucb", (DI, T), BF16)
    sc_sz = dscr("sc_sz", (DI, T), BF16)
    sc_b = dscr("sc_b", (NS, T), BF16)
    sc_c = dscr("sc_c", (NS, T), BF16)
    sc_x1 = dscr("sc_x1", (T, D), F32)
    sc_xh1 = dscr("sc_xh1", (T, D), BF16)
    sc_wqfc = dscr("sc_wqfc", (4096, 1024), BF16)
    sc_wqproj = dscr("sc_wqproj", (1024, 4096), BF16)
    sc_ahat = dscr("sc_ahat", (NHB, 128, T), BF16)
    sc_ab = dscr("sc_ab", (NHB, 128, 512), BF16)
    sc_rows16 = dscr("sc_rows16", (2, T), BF16)
    sc_rows = dscr("sc_rows", (4, T), F32)
    d_out = nc.dram_tensor("out", [T, D], F32, kind="ExternalOutput").ap()

    ENG = {}
    for nm, e in (("v", nc.vector), ("s", nc.scalar), ("t", nc.tensor),
                  ("g", nc.gpsimd), ("y", nc.sync)):
        ENG[id(e)] = nm

    def emit(eng, make, sem, writes=(), reads=(), extra=(), dk=1):
        deps = _deps_for(writes, reads, extra)
        pr.wait(eng, ENG[id(eng)], deps)
        ins = make()
        if sem is None:
            assert dk == 16 and writes
            sem = bufsem(writes[0])
        tok = pr.inc(ins, sem, dk)
        for b in writes:
            b.ready = [tok]
            b.users = []
        for b in reads:
            b.users.append(tok)
        return tok

    FENCE = [[]]

    def sb(name, shape, dt, stack=None, **kw):
        if stack is None:
            b = Buf(nc.alloc_sbuf_tensor(name, list(shape), dt, **kw).ap())
        else:
            h = stack.enter_context(nc.sbuf_tensor(name, list(shape), dt, **kw))
            b = Buf(h.ap())
        b.ready = list(FENCE[0])
        return b

    def ps(name, shape, stack):
        h = stack.enter_context(nc.psum_tensor(name, list(shape), F32))
        return Buf(h.ap())

    def bcastP(ap2d, p=128):
        assert ap2d.shape[0] == 1, ap2d.shape
        dims = [list(dd) for dd in ap2d.ap]
        return bass.AP(ap2d.tensor, ap2d.offset, [[0, p]] + dims[1:])

    def mm_group(pbuf, items, reads, extra=()):
        deps = _deps_for([pbuf], reads, extra)
        pr.wait(nc.tensor, "t", deps)
        last = None
        for (o, l, r, st, sp) in items:
            last = nc.tensor.matmul(o, l, r, start=st, stop=sp)
        tok = pr.inc(last, "t", 1)
        pbuf.ready = [tok]
        pbuf.users = []
        for b in reads:
            b.users.append(tok)
        return tok

    def barrier():
        toks = [(n, c) for n, c in pr.cnt.items() if c > 0]
        for nm, e in (("v", nc.vector), ("s", nc.scalar), ("t", nc.tensor),
                      ("g", nc.gpsimd), ("y", nc.sync)):
            pr.wait(e, nm, toks)

    s_cw = sb("s_cw", (128, NDB * 4), F32)
    s_cb = sb("s_cb", (128, NDB), F32)
    s_bdt = sb("s_bdt", (128, NDB), F32)
    s_A = sb("s_A", (128, NDB * NS), F32)
    s_D = sb("s_D", (128, NDB), F32)
    s_lnw = sb("s_lnw", (128, NHB), F32)
    s_lnb = sb("s_lnb", (128, NHB), F32)
    s_consts = sb("s_consts", (128, 4), F32)
    s_id16 = sb("s_id16", (128, 128), BF16)
    s_ones = sb("s_ones", (128, 1), BF16)
    s_diagD = sb("s_diagD", (128, NDB * 128), BF16)
    s_dtr = sb("s_dtr", (RK, DI), BF16)
    s_carry = sb("s_carry", (128, NDB * NS), BF16)
    s_dump = sb("s_dump", (128, 1024), F32)
    s_wdt = sb("s_wdt", (RK, DI), BF16)
    s_gacc = sb("s_gacc", (128, 80), F32)
    s_thr = {"fc": sb("s_thrpfc", (128, 2), F32), "pj": sb("s_thrppj", (128, 2), F32)}

    ONE = s_consts.ap[:, 0:1]
    EPS = s_consts.ap[:, 1:2]

    for src, dst in ((d_cw, s_cw), (d_cb, s_cb), (d_bdt, s_bdt), (d_alog, s_A),
                     (d_D, s_D), (d_lnw, s_lnw), (d_lnb, s_lnb), (d_consts, s_consts),
                     (d_id16, s_id16), (d_ones, s_ones)):
        emit(nc.sync, lambda s=src, d=dst: nc.sync.dma_start(d.ap, s), None,
             writes=[dst], dk=16)
    emit(nc.gpsimd, lambda: nc.gpsimd.dma_start(s_wdt.ap, d_wdt), None,
         writes=[s_wdt], dk=16)

    emit(nc.scalar, lambda: nc.scalar.activation(s_A.ap, s_A.ap, AF.Exp),
         "s", reads=[s_A], extra=s_A.ready)
    s_A.ready = [("s", pr.cnt["s"])]
    emit(nc.vector, lambda: nc.vector.tensor_scalar(s_A.ap, s_A.ap, -1.0, None,
         op0=ALU.mult), "v", reads=[s_A], extra=s_A.ready)
    s_A.ready = [("v", pr.cnt["v"])]
    for db in range(NDB):
        emit(nc.vector, lambda db=db: nc.vector.tensor_scalar(
            s_diagD.ap[:, db * 128:(db + 1) * 128], s_id16.ap,
            s_D.ap[:, db:db + 1], None, op0=ALU.mult),
            "v", writes=[s_diagD], reads=[s_id16, s_D])

    def ln_rows(src_ap, src_buf, st, out_buf, out_ap, nfeat):
        inv = 1.0 / nfeat
        emit(nc.scalar, lambda: nc.scalar.activation(
            s_dump.ap[:, 0:nfeat], src_ap, AF.Square, accum_out=st["ssq"].ap),
            "s", writes=[st["ssq"]], reads=[src_buf])
        emit(nc.vector, lambda: nc.vector.reduce_sum(st["ssum"].ap, src_ap, axis=AX.X),
            "v", writes=[st["ssum"]], reads=[src_buf], extra=st["ssq"].ready)
        emit(nc.vector, lambda: nc.vector.tensor_scalar(st["mu"].ap, st["ssum"].ap,
            inv, None, op0=ALU.mult), "v", writes=[st["mu"]], reads=[st["ssum"]])
        emit(nc.vector, lambda: nc.vector.tensor_scalar(st["e2"].ap, st["ssq"].ap,
            inv, None, op0=ALU.mult), "v", writes=[st["e2"]], reads=[st["ssq"]])
        emit(nc.vector, lambda: nc.vector.tensor_mul(st["mu2"].ap, st["mu"].ap,
            st["mu"].ap), "v", writes=[st["mu2"]], reads=[st["mu"]])
        emit(nc.vector, lambda: nc.vector.tensor_sub(st["var"].ap, st["e2"].ap,
            st["mu2"].ap), "v", writes=[st["var"]], reads=[st["e2"], st["mu2"]])
        emit(nc.scalar, lambda: nc.scalar.activation(st["lv"].ap, st["var"].ap,
            AF.Ln, bias=EPS), "s", writes=[st["lv"]], reads=[st["var"], s_consts])
        emit(nc.scalar, lambda: nc.scalar.activation(st["rstd"].ap, st["lv"].ap,
            AF.Exp, scale=-0.5), "s", writes=[st["rstd"]], reads=[st["lv"]])
        emit(nc.vector, lambda: nc.vector.scalar_tensor_tensor(st["nb"].ap,
            st["mu"].ap, -1.0, st["rstd"].ap, op0=ALU.mult, op1=ALU.mult),
            "v", writes=[st["nb"]], reads=[st["mu"], st["rstd"]])
        emit(nc.scalar, lambda: nc.scalar.activation(out_ap, src_ap, AF.Identity,
            scale=st["rstd"].ap[:, 0:1], bias=st["nb"].ap[:, 0:1]),
            "s", writes=[out_buf], reads=[src_buf, st["rstd"], st["nb"]])

    stWin = contextlib.ExitStack()
    s_win = sb("s_win", (128, NDD * 4096), BF16, stWin)
    for wk in range(8):
        emit(nc.gpsimd, lambda wk=wk: nc.gpsimd.dma_start(
            s_win.ap[:, wk * 4096:(wk + 1) * 4096],
            d_win[:, wk * 4096:(wk + 1) * 4096]), None,
            writes=[s_win], dk=16)

    s_xhatT = sb("s_xhatT", (128, NDD * T), BF16, stWin)
    with contextlib.ExitStack() as stE:
        NE = 4
        s_xe = [sb(f"s_xe{i}", (128, D), F32, stE) for i in range(NE)]
        s_xh = [sb(f"s_xh{i}", (128, D), BF16, stE) for i in range(2)]
        statsE = [{k: sb(f"e_{k}{i}", (128, 1), F32, stE) for k in STK}
                  for i in range(2)]
        p_xt = [ps(f"p_xt{i}", (128, 128), stE) for i in range(2)]
        xtsem = bufsem(s_xhatT)

        def load_xe(ts):
            if ts < TSUB:
                emit(nc.sync, lambda ts=ts: nc.sync.dma_start(
                    s_xe[ts % NE].ap, d_x[ts * 128:(ts + 1) * 128, :]), None,
                    writes=[s_xe[ts % NE]], dk=16)

        for ts in range(NE - 1):
            load_xe(ts)
        for ts in range(TSUB):
            i = ts % 2
            load_xe(ts + NE - 1)
            ln_rows(s_xe[ts % NE].ap, s_xe[ts % NE], statsE[i], s_xh[i], s_xh[i].ap, D)
            for dd in range(NDD):
                pb = p_xt[dd % 2]
                mm_group(pb, [(pb.ap, s_xh[i].ap[:, dd * 128:(dd + 1) * 128],
                               s_id16.ap, True, True)],
                         reads=[s_xh[i], s_id16])
                emit(nc.vector, lambda pb=pb, dd=dd, ts=ts: nc.vector.tensor_copy(
                    s_xhatT.ap[:, dd * T + ts * 128: dd * T + (ts + 1) * 128],
                    pb.ap), xtsem, reads=[pb])
        s_xhatT.ready = [(xtsem, pr.cnt[xtsem])]
        barrier()

    ucb_spill = {}
    sz_spill = {}
    with contextlib.ExitStack() as st1:
        pu = [ps(f"pu{i}", (128, T), st1) for i in range(2)]
        s_u = [sb(f"s_u{i}", (128, T), F32, st1) for i in range(2)]
        s_cv = [sb(f"s_cv{i}", (128, T), F32, st1) for i in range(2)]
        s_uct = [sb(f"s_uct{i}", (128, T), BF16, st1) for i in range(2)]
        s_szt = [sb(f"s_szt{i}", (128, T), BF16, st1) for i in range(2)]
        s_gw = [sb(f"s_gw{i}", (128, 2048), F32, st1) for i in range(2)]
        s_dumpg = sb("s_dumpg", (128, 2048), F32, st1)

        g_tiles = []
        for k in range(32):
            g_tiles.append((d_wfc[k * 128:(k + 1) * 128, :], 1024))
        for k in range(8):
            g_tiles.append((d_wproj[k * 128:(k + 1) * 128, 0:2048], 2048))
            g_tiles.append((d_wproj[k * 128:(k + 1) * 128, 2048:4096], 2048))

        def gamma_tile(gi):
            src, w = g_tiles[gi]
            i = gi % 2
            emit(nc.sync, lambda: nc.sync.dma_start(s_gw[i].ap[:, 0:w], src), None,
                 writes=[s_gw[i]], dk=16)
            emit(nc.scalar, lambda: nc.scalar.activation(
                s_dumpg.ap[:, 0:w], s_gw[i].ap[:, 0:w], AF.Abs,
                accum_out=s_gacc.ap[:, gi:gi + 1]),
                "s", writes=[s_gacc], reads=[s_gw[i]])

        qtiles = []
        for nm, dsrc, ddst, rows, wdt in (("fc", d_wfc, sc_wqfc, 32, 1024),
                                          ("pj", d_wproj, sc_wqproj, 8, 4096)):
            for r in range(rows):
                for cs in range(0, wdt, 1024):
                    qtiles.append((nm, dsrc, ddst, r, cs))
        qstate = {"k": 0}
        s_qw = [None, None]
        s_qc = [None, None]
        s_qq = [None, None]

        def mk_qbufs():
            qqv = s_dumpg.ap.bitcast(BF16)
            for i in range(2):
                for lst, src, apv in (
                        (s_qw, s_gw[0], s_gw[0].ap[:, i * 1024:(i + 1) * 1024]),
                        (s_qc, s_gw[1], s_gw[1].ap[:, i * 1024:(i + 1) * 1024]),
                        (s_qq, s_dumpg, qqv[:, i * 1024:(i + 1) * 1024])):
                    b = Buf(apv)
                    b.ready = list(src.ready) + list(src.users)
                    lst[i] = b

        def qload(k):
            if k >= len(qtiles):
                return
            nm, dsrc, ddst, r, cs = qtiles[k]
            i = k % 2
            qeng = nc.sync if k % 2 == 0 else nc.scalar
            emit(qeng, lambda i=i, r=r, cs=cs, dsrc=dsrc, qeng=qeng: qeng.dma_start(
                s_qw[i].ap, dsrc[r * 128:(r + 1) * 128, cs:cs + 1024]), None,
                writes=[s_qw[i]], dk=16)

        def qstep():
            k = qstate["k"]
            if k >= len(qtiles):
                return
            qstate["k"] = k + 1
            if k == 0:
                qload(0)
                qload(1)
            nm, dsrc, ddst, r, cs = qtiles[k]
            i = k % 2
            thrp = s_thr[nm]
            emit(nc.vector, lambda i=i, thrp=thrp: nc.vector.tensor_scalar(
                s_qc[i].ap, s_qw[i].ap, thrp.ap[:, 0:1], thrp.ap[:, 1:2],
                op0=ALU.min, op1=ALU.max), "v",
                writes=[s_qc[i]], reads=[s_qw[i], thrp])
            emit(nc.vector, lambda i=i: nc.vector.tensor_sub(
                s_qc[i].ap, s_qw[i].ap, s_qc[i].ap), "v",
                writes=[s_qc[i]], reads=[s_qw[i]])
            qload(k + 2)
            emit(nc.scalar, lambda i=i: nc.scalar.activation(
                s_qq[i].ap, s_qc[i].ap, AF.Sign), "s",
                writes=[s_qq[i]], reads=[s_qc[i]])
            emit(nc.sync, lambda i=i, r=r, cs=cs, ddst=ddst: nc.sync.dma_start(
                ddst[r * 128:(r + 1) * 128, cs:cs + 1024], s_qq[i].ap),
                f"wQ{k % 4}", reads=[s_qq[i]], dk=16)

        def do_thresholds():
            for j, (nm, cols, off) in enumerate((("fc", 32, 0), ("pj", 16, 32))):
                g1 = sb(f"s_g1{nm}", (128, 1), F32, st1)
                emit(nc.vector, lambda cols=cols, off=off, g1=g1:
                     nc.vector.reduce_sum(
                         g1.ap, s_gacc.ap[:, off:off + cols], axis=AX.X), "v",
                     writes=[g1], reads=[s_gacc])
                emit(nc.sync, lambda g1=g1, j=j: nc.sync.dma_start(
                    sc_rows[2:3, j * 128:(j + 1) * 128],
                    g1.ap), "wR", reads=[g1], dk=16)
                gr = sb(f"s_gr{nm}", (1, 128), F32, st1)
                emit(nc.sync, lambda gr=gr, j=j: nc.sync.dma_start(
                    gr.ap, sc_rows[2:3, j * 128:(j + 1) * 128]),
                    None, writes=[gr], extra=g1.users, dk=16)
                gs = sb(f"s_gs{nm}", (1, 1), F32, st1)
                emit(nc.vector, lambda gr=gr, gs=gs: nc.vector.reduce_sum(
                    gs.ap, gr.ap, axis=AX.X), "v", writes=[gs], reads=[gr])
                thr2 = sb(f"s_thr2{nm}", (1, 2), F32, st1)
                emit(nc.vector, lambda gs=gs, thr2=thr2: nc.vector.tensor_scalar(
                    thr2.ap[:, 0:1], gs.ap, 0.5 / NW, 0.5e-6,
                    op0=ALU.mult, op1=ALU.add),
                    "v", writes=[thr2], reads=[gs])
                emit(nc.vector, lambda thr2=thr2: nc.vector.tensor_scalar(
                    thr2.ap[:, 1:2], thr2.ap[:, 0:1], -1.0, None, op0=ALU.mult),
                    "v", reads=[thr2], extra=thr2.ready)
                t2tok = ("v", pr.cnt["v"])
                thr2.ready = [t2tok]
                emit(nc.sync, lambda thr2=thr2, j=j: nc.sync.dma_start(
                    sc_rows[3:4, j * 2:j * 2 + 2], thr2.ap),
                    "wR", reads=[thr2], dk=16)
                thrp = s_thr[nm]
                emit(nc.sync, lambda thrp=thrp, j=j: nc.sync.dma_start(
                    thrp.ap, bcastP(sc_rows[3:4, j * 2:j * 2 + 2])),
                    None, writes=[thrp], extra=thr2.users, dk=16)

        pend_silu = None
        pend_zsilu = None

        def do_zsilu(pi):
            i, db = pi
            emit(nc.scalar, lambda i=i: nc.scalar.activation(
                s_szt[i].ap, pu[i].ap, AF.Silu), "s",
                writes=[s_szt[i]], reads=[pu[i]])
            sz_spill[db] = emit(nc.sync, lambda i=i, db=db: nc.sync.dma_start(
                sc_sz[db * 128:(db + 1) * 128, :], s_szt[i].ap), f"wZ{i}",
                reads=[s_szt[i]], dk=16)

        def do_silu(pi):
            i, db = pi
            cv = s_cv[i]
            emit(nc.scalar, lambda i=i, db=db, cv=cv: nc.scalar.activation(
                s_uct[i].ap, cv.ap, AF.Silu, bias=s_cb.ap[:, db:db + 1]),
                "s", writes=[s_uct[i]], reads=[cv, s_cb])
            ucb_spill[db] = emit(nc.sync, lambda i=i, db=db: nc.sync.dma_start(
                sc_ucb[db * 128:(db + 1) * 128, :], s_uct[i].ap), f"wU{i}",
                reads=[s_uct[i]], dk=16)

        for fi in range(32):
            i = fi % 2
            items = []
            for db in range(NDD):
                for q in range(4):
                    items.append((pu[i].ap[:, q * 512:(q + 1) * 512],
                                  s_win.ap[:, db * 4096 + fi * 128: db * 4096 + (fi + 1) * 128],
                                  s_xhatT.ap[:, db * T + q * 512: db * T + (q + 1) * 512],
                                  db == 0, db == NDD - 1))
            mm_group(pu[i], items, reads=[s_win, s_xhatT])
            if fi < 16:
                db = fi
                emit(nc.scalar, lambda i=i: nc.scalar.copy(s_u[i].ap, pu[i].ap), "s",
                     writes=[s_u[i]], reads=[pu[i]])
                if pend_silu is not None:
                    do_silu(pend_silu)
                    pend_silu = None
                cv = s_cv[i]
                emit(nc.vector, lambda i=i, db=db, cv=cv: nc.vector.tensor_scalar(
                    cv.ap, s_u[i].ap, s_cw.ap[:, db * 4 + 3:db * 4 + 4], None,
                    op0=ALU.mult), "v", writes=[cv], reads=[s_u[i], s_cw])
                for k in range(3):
                    sh = 3 - k
                    emit(nc.vector, lambda i=i, db=db, k=k, sh=sh, cv=cv:
                         nc.vector.scalar_tensor_tensor(
                             cv.ap[:, sh:T], s_u[i].ap[:, 0:T - sh],
                             s_cw.ap[:, db * 4 + k:db * 4 + k + 1],
                             cv.ap[:, sh:T], op0=ALU.mult, op1=ALU.add),
                         "v", writes=[cv], reads=[s_u[i], s_cw])
                pend_silu = (i, db)
            else:
                if pend_silu is not None:
                    do_silu(pend_silu)
                    pend_silu = None
                db = fi - 16
                if pend_zsilu is not None:
                    do_zsilu(pend_zsilu)
                pend_zsilu = (i, db)
            if fi < 24:
                gamma_tile(2 * fi)
                gamma_tile(2 * fi + 1)
            else:
                if fi == 24:
                    do_thresholds()
                    mk_qbufs()
                for _ in range(6):
                    qstep()

        if pend_zsilu is not None:
            do_zsilu(pend_zsilu)
            pend_zsilu = None
        while qstate["k"] < len(qtiles):
            qstep()
        barrier()
    stWin.close()

    qfence = [("v", pr.cnt["v"]), ("s", pr.cnt["s"])]
    for _qi in range(4):
        if pr.cnt.get(f"wQ{_qi}", 0):
            qfence.append((f"wQ{_qi}", pr.cnt[f"wQ{_qi}"]))
    FENCE[0] = qfence

    stW = contextlib.ExitStack()
    s_wout = sb("s_wout", (128, NDB * 1024), BF16, stW)
    for wk in range(8):
        emit(nc.gpsimd, lambda wk=wk: nc.gpsimd.dma_start(
            s_wout.ap[:, wk * 2048:(wk + 1) * 2048],
            d_wout[:, wk * 2048:(wk + 1) * 2048]), None,
            writes=[s_wout], dk=16)
    with contextlib.ExitStack() as st2:
        s_wx = sb("s_wx", (128, NDB * 160), BF16, st2)
        for wk in range(4):
            emit(nc.gpsimd, lambda wk=wk: nc.gpsimd.dma_start(
                s_wx.ap[:, wk * 640:(wk + 1) * 640],
                d_wx[:, wk * 640:(wk + 1) * 640]), None,
                writes=[s_wx], dk=16)
        p_dtr = ps("p_dtr", (RK, T), st2)
        p_bc = ps("p_bc", (32, T), st2)
        s_uc2 = [sb(f"s_uc2{i}", (128, T), BF16, st2) for i in range(4)]

        def load_uc2(db):
            if db >= NDB:
                return
            i = db % 4
            qe = nc.sync if db % 2 == 0 else nc.scalar
            emit(qe, lambda db=db, i=i, qe=qe: qe.dma_start(
                s_uc2[i].ap, sc_ucb[db * 128:(db + 1) * 128, :]), None,
                writes=[s_uc2[i]],
                extra=[("wU0", pr.cnt.get("wU0", 0)),
                       ("wU1", pr.cnt.get("wU1", 0))], dk=16)

        for db in range(3):
            load_uc2(db)
        for db in range(NDB):
            i = db % 4
            load_uc2(db + 3)
            deps = _deps_for([p_dtr, p_bc] if db == 0 else [], [s_wx, s_uc2[i]], [])
            pr.wait(nc.tensor, "t", deps)
            last = None
            for q in range(4):
                last = nc.tensor.matmul(
                    p_dtr.ap[:, q * 512:(q + 1) * 512],
                    s_wx.ap[:, db * 160: db * 160 + 128],
                    s_uc2[i].ap[:, q * 512:(q + 1) * 512],
                    start=(db == 0), stop=(db == NDB - 1))
            for q in range(4):
                last = nc.tensor.matmul(
                    p_bc.ap[:, q * 512:(q + 1) * 512],
                    s_wx.ap[:, db * 160 + 128: db * 160 + 160],
                    s_uc2[i].ap[:, q * 512:(q + 1) * 512],
                    start=(db == 0), stop=(db == NDB - 1))
            tok2 = pr.inc(last, "t", 1)
            s_uc2[i].users.append(tok2)
            if db == NDB - 1:
                p_dtr.ready = [tok2]
                p_bc.ready = [tok2]
        emit(nc.scalar, lambda: nc.scalar.copy(s_dtr.ap, p_dtr.ap), "s",
             writes=[s_dtr], reads=[p_dtr])
        s_bcv = sb("s_bcv", (32, T), BF16, st2)
        emit(nc.scalar, lambda: nc.scalar.copy(s_bcv.ap, p_bc.ap), "s",
             writes=[s_bcv], reads=[p_bc])
        b_spill = emit(nc.sync, lambda: nc.sync.dma_start(sc_b, s_bcv.ap[0:16, :]),
                       "wB", reads=[s_bcv], dk=16)
        c_spill = emit(nc.sync, lambda: nc.sync.dma_start(sc_c, s_bcv.ap[16:32, :]),
                       "wB", reads=[s_bcv], dk=16)
        barrier()

    with contextlib.ExitStack() as st3:
        _sb0 = sb("s_brepb0", (128, NS * TC), BF16, st3)
        s_brepb = [_sb0, _sb0]
        _sc0 = sb("s_crepb0", (128, NS * TC), BF16, st3)
        s_crepb = [_sc0, _sc0]
        s_e = [sb("s_e0", (128, TC), F32, st3)] * 2
        s_dt = [sb(f"s_dt{i}", (128, TC), F32, st3) for i in range(2)]
        s_wt = [sb("s_wt0", (128, TC), BF16, st3)] * 2
        s_da = [sb(f"s_da{i}", (128, TC), BF16, st3) for i in range(6)]
        s_dbufb = [sb(f"s_dbuf{i}", (128, NS * TC), BF16, st3) for i in range(2)]
        _sh0 = sb("s_hb0", (128, NS * TC), BF16, st3)
        s_hb = [_sh0, _sh0]
        s_uc3 = [sb(f"s_uc3{i}", (128, TC), BF16, st3) for i in range(2)]
        s_sz3 = [sb(f"s_sz3{i}", (128, TC), BF16, st3) for i in range(2)]
        s_g = sb("s_g", (128, NDB * TC), BF16, st3)
        p_p3 = [ps(f"p_p3{i}", (RK, TC), st3) for i in range(2)]
        p_p4 = ps("p_p4", (128, 1024), st3)
        p_y = [Buf(p_p4.ap[:, 0:TC]), Buf(p_p4.ap[:, TC:2 * TC])]
        s_x0g = [sb("s_x0g0", (128, D), F32, st3)] * 2
        s_x1t = [sb("s_x1t0", (128, D), F32, st3)] * 2
        s_xh1t = [sb("s_xh1t0", (128, D), BF16, st3)] * 2
        stats4 = [{k: sb(f"g4_{k}{i}", (128, 1), F32, st3) for k in STK}
                  for i in range(2)]
        stats5 = stats4

        s_wq4 = [sb(f"s_wq4_{i}", (128, NDD * 256), BF16, st3,
                    is_transpose_dest=True) for i in range(2)]
        s_xh1Tq = sb("s_xh1Tq", (128, NDD * 512), BF16, st3,
                     is_transpose_dest=True)
        s_abq = [sb(f"s_abq{i}", (128, 512), BF16, st3) for i in range(2)]
        s_abr = [sb(f"s_abr{i}", (128, 512), BF16, st3) for i in range(3)]
        s_sqb = [sb(f"s_sqb{i}", (128, 512), BF16, st3) for i in range(2)]
        p_a = [ps(f"p_a{i}", (128, 512), st3) for i in range(2)]
        p_s1 = ps("p_s1", (1, 512), st3)
        p_s2 = ps("p_s2", (1, 512), st3)
        s_mub = sb("s_mub", (128, 512), BF16, st3)
        s_rsb = sb("s_rsb", (128, 512), BF16, st3)
        s_mu1 = sb("s_mu1", (1, 512), BF16, st3)
        s_rs1 = sb("s_rs1", (1, 512), F32, st3)
        s_rs16 = sb("s_rs16", (1, 512), BF16, st3)
        s_t1 = sb("s_t15", (128, 512), F32, st3)
        s_aht = [sb(f"s_aht{i}", (128, 512), BF16, st3) for i in range(2)]
        tokqs = [(f"wQ{_qi}", pr.cnt.get(f"wQ{_qi}", 0)) for _qi in range(4)]
        ab_dbuf = [Buf(sc_ab[hb]) for hb in range(NHB)]

        def p5_load_slot(tq, g):
            if g >= 16:
                return
            slot = s_wq4[g % 2]
            ssem = bufsem(slot)
            for db in range(NDD):
                emit(nc.sync, lambda g=g, db=db, slot=slot: nc.sync.dma_start_transpose(
                    slot.ap[:, db * 256:(db + 1) * 256],
                    sc_wqfc[g * 256:(g + 1) * 256, db * 128:(db + 1) * 128]), ssem,
                    writes=[slot] if db == 0 else [],
                    extra=(tokqs + slot.ready + slot.users) if db == 0 else [], dk=16)
            slot.ready = [(ssem, pr.cnt[ssem])]

        def p5_xpose(tq):
            t0 = tq * 512
            hsem = bufsem(s_xh1Tq)
            tokh = ("wH1", pr.cnt.get("wH1", 0))
            for db in range(NDD):
                emit(nc.sync, lambda db=db, t0=t0: nc.sync.dma_start_transpose(
                    s_xh1Tq.ap[:, db * 512:(db + 1) * 512],
                    sc_xh1[t0:t0 + 512, db * 128:(db + 1) * 128]), hsem,
                    writes=[s_xh1Tq] if db == 0 else [],
                    extra=([tokh] + s_xh1Tq.ready + s_xh1Tq.users) if db == 0 else [],
                    dk=16)
            s_xh1Tq.ready = [(hsem, pr.cnt[hsem])]

        def p5_colsums(hb, sq_dep):
            i = hb % 2
            mm_group(p_s1, [(p_s1.ap, s_ones.ap, s_abq[i].ap,
                             hb == 0, hb == NHB - 1)],
                     reads=[s_ones, s_abq[i]], extra=sq_dep)
            mm_group(p_s2, [(p_s2.ap, s_ones.ap, s_sqb[i].ap,
                             hb == 0, hb == NHB - 1)],
                     reads=[s_ones, s_sqb[i]])

        def p5_gelu2(hb0):
            if hb0 > 0:
                p5_colsums(hb0 - 1, prev_sq[0])
            for hb in (hb0, hb0 + 1):
                i = hb % 2
                emit(nc.scalar, lambda i=i: nc.scalar.activation(
                    s_abq[i].ap, p_a[i].ap, AF.Gelu), "s",
                    writes=[s_abq[i]], reads=[p_a[i]])
                emit(nc.gpsimd, lambda i=i, hb=hb: nc.gpsimd.dma_start(
                    sc_ab[hb], s_abq[i].ap), f"wAB{i}",
                    writes=[ab_dbuf[hb]], reads=[s_abq[i]], dk=16)
            sqtoks = {}
            for hb in (hb0, hb0 + 1):
                i = hb % 2
                emit(nc.scalar, lambda i=i: nc.scalar.activation(
                    s_sqb[i].ap, s_abq[i].ap, AF.Square), "s",
                    writes=[s_sqb[i]], reads=[s_abq[i]])
                sqtoks[hb] = [("s", pr.cnt["s"])]
            p5_colsums(hb0, sqtoks[hb0])
            prev_sq[0] = sqtoks[hb0 + 1]

        def p5_mm2(tq, hb2):
            if hb2 > 0:
                p5_gelu2(2 * hb2 - 2)
            for hb in (2 * hb2, 2 * hb2 + 1):
                i = hb % 2
                slot = s_wq4[hb2 % 2]
                o = (hb - 2 * hb2) * 128
                items = [(p_a[i].ap,
                          slot.ap[:, db * 256 + o: db * 256 + o + 128],
                          s_xh1Tq.ap[:, db * 512:(db + 1) * 512],
                          db == 0, db == NDD - 1) for db in range(NDD)]
                mm_group(p_a[i], items, reads=[slot, s_xh1Tq])
            p5_load_slot(tq, hb2 + 2)

        def p5_reload(hb):
            if hb >= NHB:
                return
            j = hb % 3
            emit(nc.gpsimd, lambda hb=hb, j=j: nc.gpsimd.dma_start(
                s_abr[j].ap, sc_ab[hb]), None, writes=[s_abr[j]],
                reads=[ab_dbuf[hb]],
                extra=[("wAB0", pr.cnt.get("wAB0", 0)),
                       ("wAB1", pr.cnt.get("wAB1", 0))], dk=16)

        def p5_stats(tq):
            t0 = tq * 512
            p5_gelu2(NHB - 2)
            p5_colsums(NHB - 1, prev_sq[0])
            r_e2 = p_s2.ap[0:1, :]
            r_lv = p_a[0].ap[0:1, 0:512]
            emit(nc.vector, lambda: nc.vector.tensor_scalar(
                s_mu1.ap, p_s1.ap, 1.0 / 4096, None, op0=ALU.mult), "v",
                writes=[s_mu1], reads=[p_s1])
            emit(nc.vector, lambda r_e2=r_e2: nc.vector.tensor_scalar(
                r_e2, r_e2, 1.0 / 4096, None, op0=ALU.mult), "v",
                reads=[p_s2], extra=p_s2.ready)
            emit(nc.vector, lambda: nc.vector.tensor_mul(
                s_rs1.ap, s_mu1.ap, s_mu1.ap), "v", reads=[s_mu1],
                writes=[s_rs1])
            emit(nc.vector, lambda r_e2=r_e2: nc.vector.tensor_sub(
                r_e2, r_e2, s_rs1.ap), "v", reads=[p_s2, s_rs1],
                extra=[("v", pr.cnt["v"])])
            vtok = ("v", pr.cnt["v"])
            emit(nc.scalar, lambda t0=t0: nc.scalar.dma_start(
                sc_rows16[0:1, t0:t0 + 512], s_mu1.ap), "wMRa",
                reads=[s_mu1], dk=16)
            emit(nc.scalar, lambda r_lv=r_lv, r_e2=r_e2: nc.scalar.activation(
                r_lv, r_e2, AF.Ln, bias=EPS[0:1, :]), "s",
                reads=[p_a[0], s_consts], extra=[vtok])
            emit(nc.scalar, lambda r_lv=r_lv: nc.scalar.activation(
                s_rs16.ap, r_lv, AF.Exp, scale=-0.5), "s",
                writes=[s_rs16], reads=[p_a[0]],
                extra=[("s", pr.cnt["s"])] + s_rs16.users)
            emit(nc.scalar, lambda t0=t0: nc.scalar.dma_start(
                sc_rows16[1:2, t0:t0 + 512], s_rs16.ap), "wMRb",
                reads=[s_rs16], dk=16)
            mrtoks = [("wMRa", pr.cnt["wMRa"]), ("wMRb", pr.cnt["wMRb"])]
            emit(nc.scalar, lambda t0=t0: nc.scalar.dma_start(
                s_mub.ap, bcastP(sc_rows16[0:1, t0:t0 + 512])), None,
                writes=[s_mub], extra=mrtoks, dk=16)
            emit(nc.scalar, lambda t0=t0: nc.scalar.dma_start(
                s_rsb.ap, bcastP(sc_rows16[1:2, t0:t0 + 512])), None,
                writes=[s_rsb], extra=mrtoks, dk=16)
            for pb in (p_s1, p_s2, p_a[0], p_a[1]):
                pb.users.append(("v", pr.cnt["v"]))
                pb.users.append(("s", pr.cnt["s"]))
                pb.users.append(("wMRa", pr.cnt["wMRa"]))
                pb.users.append(("wMRb", pr.cnt["wMRb"]))
            for hb in range(3):
                p5_reload(hb)

        def p5_apply4(tq, k):
            t0 = tq * 512
            for hb in range(4 * k, 4 * k + 4):
                j = hb % 3
                i = hb % 2
                emit(nc.vector, lambda j=j: nc.vector.tensor_sub(
                    s_t1.ap, s_abr[j].ap, s_mub.ap), "v",
                    writes=[s_t1], reads=[s_abr[j], s_mub])
                emit(nc.vector, lambda: nc.vector.tensor_mul(
                    s_t1.ap, s_t1.ap, s_rsb.ap), "v",
                    writes=[s_t1], reads=[s_rsb], extra=s_t1.ready)
                emit(nc.vector, lambda i=i, hb=hb: nc.vector.tensor_scalar(
                    s_aht[i].ap, s_t1.ap, s_lnw.ap[:, hb:hb + 1],
                    s_lnb.ap[:, hb:hb + 1], op0=ALU.mult, op1=ALU.add), "v",
                    writes=[s_aht[i]], reads=[s_t1, s_lnw, s_lnb])
                emit(nc.scalar, lambda i=i, hb=hb, t0=t0: nc.scalar.dma_start(
                    sc_ahat[hb:hb + 1, :, t0:t0 + 512], s_aht[i].ap), f"wA{i}",
                    reads=[s_aht[i]], dk=16)
                p5_reload(hb + 3)

        def p5_steps(tq):
            steps = [lambda tq=tq: p5_xpose(tq),
                     lambda tq=tq: p5_load_slot(tq, 0),
                     lambda tq=tq: p5_load_slot(tq, 1)]
            for hb2 in range(NHB // 2):
                steps.append(lambda tq=tq, hb2=hb2: p5_mm2(tq, hb2))
            steps.append(lambda tq=tq: p5_stats(tq))
            for k in range(NHB // 4):
                steps.append(lambda tq=tq, k=k: p5_apply4(tq, k))
            return steps

        prev_sq = [None]
        p5q = []
        p5_done = [0]

        def bc3(dst2d, src, t0):
            base = src[0:1, t0:t0 + TC]
            dims = [list(dd) for dd in base.ap]

            def ap_for(q):
                b4 = src[4 * q:4 * q + 1, t0:t0 + TC]
                return bass.AP(b4.tensor, b4.offset, [[0, 128], [T, 4], [1, TC]])
            return ap_for

        def load_reps(tc):
            if tc >= NCH:
                return
            t0 = tc * TC
            k = tc % 2
            bq = bc3(None, sc_b, t0)
            cq = bc3(None, sc_c, t0)
            bs, cs2 = bufsem(s_brepb[k]), bufsem(s_crepb[k])
            for q in range(4):
                emit(nc.gpsimd, lambda q=q, k=k, bq=bq: nc.gpsimd.dma_start(
                    s_brepb[k].ap[:, q * 4 * TC:(q + 1) * 4 * TC], bq(q)), bs,
                    writes=[s_brepb[k]] if q == 0 else [],
                    extra=([("wB", pr.cnt["wB"])] + qfence
                           + s_brepb[k].ready + s_brepb[k].users)
                    if q == 0 else [], dk=16)
                emit(nc.gpsimd, lambda q=q, k=k, cq=cq: nc.gpsimd.dma_start(
                    s_crepb[k].ap[:, q * 4 * TC:(q + 1) * 4 * TC], cq(q)), cs2,
                    writes=[s_crepb[k]] if q == 0 else [],
                    extra=([("wB", pr.cnt["wB"])] + qfence
                           + s_crepb[k].ready + s_crepb[k].users)
                    if q == 0 else [], dk=16)
            s_brepb[k].ready = [(bs, pr.cnt[bs])]
            s_crepb[k].ready = [(cs2, pr.cnt[cs2])]

        def load_uc_sz(tc, db):
            if db >= NDB:
                tc, db = tc + 1, db - NDB
            if tc >= NCH:
                return
            t0 = tc * TC
            i = db % 2
            emit(nc.sync, lambda db=db, i=i, t0=t0: nc.sync.dma_start(
                s_uc3[i].ap, sc_ucb[db * 128:(db + 1) * 128, t0:t0 + TC]), None,
                writes=[s_uc3[i]],
                extra=[("wU0", pr.cnt.get("wU0", 0)),
                       ("wU1", pr.cnt.get("wU1", 0))], dk=16)
            emit(nc.scalar, lambda db=db, i=i, t0=t0: nc.scalar.dma_start(
                s_sz3[i].ap, sc_sz[db * 128:(db + 1) * 128, t0:t0 + TC]), None,
                writes=[s_sz3[i]],
                extra=[("wZ0", pr.cnt.get("wZ0", 0)),
                       ("wZ1", pr.cnt.get("wZ1", 0))], dk=16)

        load_uc_sz(0, 0)
        pend_gate = None
        da_next = None

        def do_gate(pg):
            i, db = pg
            emit(nc.vector, lambda i=i, db=db: nc.vector.tensor_mul(
                s_g.ap[:, db * TC:(db + 1) * TC], p_y[i].ap, s_sz3[i].ap), "v",
                reads=[p_y[i], s_sz3[i]],
                extra=(s_g.ready + s_g.users) if db == 0 else [])

        for tc in range(NCH):
            t0 = tc * TC
            if tc == 0:
                load_reps(0)
            s_brep = s_brepb[tc % 2]
            s_crep = s_crepb[tc % 2]
            for db in range(NDB):
                i = db % 2
                if tc == 0 and db == 0:
                    mm_group(p_p3[0], [(p_p3[0].ap, s_wdt.ap[0:RK, 0:128],
                                       s_dtr.ap[:, 0:TC], True, True)],
                             reads=[s_wdt, s_dtr])
                    emit(nc.scalar, lambda: nc.scalar.activation(
                        s_e[0].ap, p_p3[0].ap, AF.Exp, bias=s_bdt.ap[:, 0:1]), "s",
                        writes=[s_e[0]], reads=[p_p3[0], s_bdt])
                    emit(nc.scalar, lambda: nc.scalar.activation(
                        s_dt[0].ap, s_e[0].ap, AF.Ln, bias=ONE), "s",
                        writes=[s_dt[0]], reads=[s_e[0], s_consts])
                emit(nc.vector, lambda i=i: nc.vector.tensor_mul(
                    s_wt[i].ap, s_dt[i].ap, s_uc3[i].ap), "v",
                    writes=[s_wt[i]], reads=[s_dt[i], s_uc3[i]])
                da_toks = da_next if da_next is not None else {}
                da_next = None

                def emit_da(n, i=i, db=db, da_toks=da_toks):
                    if n >= NS:
                        return
                    j = n % 6
                    da_toks[n] = emit(
                        nc.scalar, lambda i=i, j=j, db=db, n=n: nc.scalar.activation(
                            s_da[j].ap, s_dt[i].ap, AF.Exp,
                            scale=s_A.ap[:, db * NS + n: db * NS + n + 1]), "s",
                        writes=[s_da[j]], reads=[s_dt[i], s_A])

                for n in range(6):
                    if n not in da_toks:
                        emit_da(n)
                s_dbuf = s_dbufb[i]
                s_h = s_hb[i]
                dbuf_war = s_dbuf.ready + s_dbuf.users
                wtr = bass.AP(s_wt[i].ap.tensor, s_wt[i].ap.offset,
                              [[TC, 128], [0, NS], [1, TC]])
                emit(nc.vector, lambda s_brep=s_brep, s_dbuf=s_dbuf, wtr=wtr:
                     nc.vector.tensor_mul(
                         s_dbuf.ap.rearrange("p (n t) -> p n t", t=TC), wtr,
                         s_brep.ap.rearrange("p (n t) -> p n t", t=TC)),
                     "v", reads=[s_wt[i], s_brep], extra=dbuf_war)
                dbuf_tok = ("v", pr.cnt["v"])
                s_dbuf.ready = [dbuf_tok]
                s_dbuf.users = []
                ndb, ntc = db + 1, tc
                if ndb >= NDB:
                    ndb, ntc = 0, tc + 1
                if ntc < NCH:
                    nt0 = ntc * TC
                    ni = ndb % 2
                    mm_group(p_p3[ni],
                             [(p_p3[ni].ap,
                               s_wdt.ap[:, ndb * 128:(ndb + 1) * 128],
                               s_dtr.ap[:, nt0:nt0 + TC], True, True)],
                             reads=[s_wdt, s_dtr])
                    emit(nc.scalar, lambda ni=ni, ndb=ndb: nc.scalar.activation(
                        s_e[ni].ap, p_p3[ni].ap, AF.Exp,
                        bias=s_bdt.ap[:, ndb:ndb + 1]), "s",
                        writes=[s_e[ni]], reads=[p_p3[ni], s_bdt])
                    emit(nc.scalar, lambda ni=ni: nc.scalar.activation(
                        s_dt[ni].ap, s_e[ni].ap, AF.Ln, bias=ONE), "s",
                        writes=[s_dt[ni]], reads=[s_e[ni], s_consts])
                h_war = s_h.ready + s_h.users
                for n in range(NS):
                    j = n % 6
                    if tc == 0:
                        init = 0.0
                    else:
                        init = s_carry.ap[:, db * NS + n: db * NS + n + 1]
                    emit(nc.vector, lambda j=j, n=n, init=init, s_h=s_h, s_dbuf=s_dbuf:
                         nc.vector.tensor_tensor_scan(
                             s_h.ap[:, n * TC:(n + 1) * TC], s_da[j].ap,
                             s_dbuf.ap[:, n * TC:(n + 1) * TC], init,
                             op0=ALU.mult, op1=ALU.add),
                         "v", reads=[s_da[j]] + ([s_carry] if tc > 0 else []),
                         extra=[da_toks[n]]
                         + ((h_war + [dbuf_tok]) if n == 0 else []))
                    emit_da(n + 6)
                htok = ("v", pr.cnt["v"])
                s_h.ready = [htok]
                s_h.users = []
                if pend_gate is not None:
                    do_gate(pend_gate)
                    pend_gate = None
                hview = s_h.ap.rearrange("p (n t) -> p n t", t=TC)
                carry_tok = emit(nc.scalar, lambda db=db, hview=hview:
                    nc.scalar.copy(
                        s_carry.ap[:, db * NS:(db + 1) * NS].rearrange(
                            "p (n o) -> p n o", o=1), hview[:, :, TC - 1:TC]), "s",
                    writes=[s_carry], reads=[s_h])
                emit(nc.vector, lambda s_crep=s_crep, s_h=s_h, s_dbuf=s_dbuf:
                    nc.vector.tensor_mul(
                        s_dbuf.ap, s_h.ap, s_crep.ap), "v",
                    reads=[s_h, s_crep],
                    extra=s_dbuf.ready + s_dbuf.users)
                hctok = ("v", pr.cnt["v"])
                s_dbuf.ready = [hctok]
                s_dbuf.users = []
                s_h.users.append(hctok)
                items = [(p_y[i].ap, s_id16.ap, s_dbuf.ap[:, n * TC:(n + 1) * TC],
                          n == 0, False) for n in range(NS)]
                items.append((p_y[i].ap, s_diagD.ap[:, db * 128:(db + 1) * 128],
                              s_uc3[i].ap, False, True))
                mm_group(p_y[i], items, reads=[s_dbuf, s_id16, s_diagD, s_uc3[i]])
                load_uc_sz(tc, db + 1)
                pend_gate = (i, db)
                ndb2, ntc2 = db + 1, tc
                if ndb2 >= NDB:
                    ndb2, ntc2 = 0, tc + 1
                if ntc2 < NCH:
                    da_next = {}

                    def emit_da_pre(n, i2=ndb2 % 2, db2=ndb2, dd=da_next):
                        j = n % 6
                        dd[n] = emit(
                            nc.scalar,
                            lambda: nc.scalar.activation(
                                s_da[j].ap, s_dt[i2].ap, AF.Exp,
                                scale=s_A.ap[:, db2 * NS + n: db2 * NS + n + 1]),
                            "s", writes=[s_da[j]], reads=[s_dt[i2], s_A])

                    for n in range(3):
                        emit_da_pre(n)
                want = ((db + 1) * 28 + 19) // 20
                while p5q and p5_done[0] < want:
                    p5q.pop(0)()
                    p5_done[0] += 1
            if pend_gate is not None:
                do_gate(pend_gate)
                pend_gate = None
            load_reps(tc + 1)
            gtok = ("v", pr.cnt["v"])
            s_g.ready = [gtok]
            s_g.users = []
            for su in range(TC // 128):
                tglob = tc * (TC // 128) + su
                i = su % 2
                items = []
                for db in range(NDB):
                    stat = s_g.ap[:, db * TC + su * 128: db * TC + (su + 1) * 128]
                    for dmh in range(2):
                        items.append((p_p4.ap[:, dmh * 512:(dmh + 1) * 512], stat,
                                      s_wout.ap[:, db * 1024 + dmh * 512: db * 1024 + (dmh + 1) * 512],
                                      db == 0, db == NDB - 1))
                mm_group(p_p4, items, reads=[s_g, s_wout])
                emit(nc.sync, lambda tglob=tglob, i=i: nc.sync.dma_start(
                    s_x0g[i].ap, d_x[tglob * 128:(tglob + 1) * 128, :]), None,
                    writes=[s_x0g[i]], dk=16)
                ln_rows(p_p4.ap, p_p4, stats4[i], s_x1t[i], s_x1t[i].ap, D)
                emit(nc.vector, lambda i=i: nc.vector.tensor_add(
                    s_x1t[i].ap, s_x1t[i].ap, s_x0g[i].ap), "v",
                    writes=[s_x1t[i]], reads=[s_x0g[i]], extra=s_x1t[i].ready)
                emit(nc.sync, lambda tglob=tglob, i=i: nc.sync.dma_start(
                    sc_x1[tglob * 128:(tglob + 1) * 128, :], s_x1t[i].ap), "wX1",
                    reads=[s_x1t[i]], dk=16)
                ln_rows(s_x1t[i].ap, s_x1t[i], stats5[i], s_xh1t[i], s_xh1t[i].ap, D)
                emit(nc.sync, lambda tglob=tglob, i=i: nc.sync.dma_start(
                    sc_xh1[tglob * 128:(tglob + 1) * 128, :], s_xh1t[i].ap), "wH1",
                    reads=[s_xh1t[i]], dk=16)
                if p5q:
                    p5q.pop(0)()
                    p5_done[0] += 1
            for pyh in p_y:
                pyh.users.append(("s", pr.cnt["s"]))
                pyh.users.append(("v", pr.cnt["v"]))
            p5q.extend(p5_steps(tc))
            p5_done[0] = 0
        for f in p5q:
            f()
        barrier()
    stW.close()

    with contextlib.ExitStack() as st6:
        s_wqprojT = sb("s_wqprojT", (128, NHB * 1024), BF16, st6,
                       is_transpose_dest=True)
        s_ah6 = [sb(f"s_ah6{i}", (128, NHB * 128), BF16, st6) for i in range(2)]
        p_p6b = [ps(f"p_p6{i}", (128, 1024), st6) for i in range(2)]
        s_v6 = [sb(f"s_v6{i}", (128, D), F32, st6) for i in range(2)]
        s_x1r = [sb(f"s_x1r{i}", (128, D), F32, st6) for i in range(2)]
        s_o = [sb(f"s_o{i}", (128, D), F32, st6) for i in range(2)]
        stats6 = [{k: sb(f"g6_{k}{i}", (128, 1), F32, st6) for k in STK}
                  for i in range(2)]

        def load_p6(su):
            if su >= TSUB:
                return
            i = su % 2
            emit(nc.sync, lambda su=su, i=i: nc.sync.dma_start(
                s_ah6[i].ap, sc_ahat[:, :, su * 128:(su + 1) * 128].rearrange(
                    "h p t -> p h t")), None, writes=[s_ah6[i]], dk=16)
            emit(nc.scalar, lambda su=su, i=i: nc.scalar.dma_start(
                s_x1r[i].ap, sc_x1[su * 128:(su + 1) * 128, :]), None,
                writes=[s_x1r[i]], dk=16)

        load_p6(0)
        tokps = [(f"wQ{_qi}", pr.cnt.get(f"wQ{_qi}", 0)) for _qi in range(4)]
        pj_tok = {}
        for hb in range(NHB):
            qs = f"qpj{hb // 8}"
            emit(nc.sync, lambda hb=hb: nc.sync.dma_start_transpose(
                s_wqprojT.ap[:, hb * 1024:(hb + 1) * 1024],
                sc_wqproj[:, hb * 128:(hb + 1) * 128]), qs, extra=tokps, dk=16)
        for hb in range(NHB):
            pj_tok[hb] = (f"qpj{hb // 8}", 128)
        s_wqprojT.ready = [(f"qpj{g}", 128) for g in range(4)]

        for su in range(TSUB):
            i = su % 2
            p_p6 = p_p6b[i]
            load_p6(su + 1)
            items = []
            for hb in range(NHB):
                stat = s_ah6[i].ap[:, hb * 128:(hb + 1) * 128]
                for dmh in range(2):
                    items.append((p_p6.ap[:, dmh * 512:(dmh + 1) * 512], stat,
                                  s_wqprojT.ap[:, hb * 1024 + dmh * 512: hb * 1024 + (dmh + 1) * 512],
                                  hb == 0, hb == NHB - 1))
            if su == 0:
                deps = _deps_for([p_p6], [s_ah6[i]], [])
                pr.wait(nc.tensor, "t", deps)
                last = None
                for hb in range(NHB):
                    pr.wait(nc.tensor, "t", [pj_tok[hb]])
                    for dmh in range(2):
                        it = items[hb * 2 + dmh]
                        last = nc.tensor.matmul(it[0], it[1], it[2],
                                                start=it[3], stop=it[4])
                tok6 = pr.inc(last, "t", 1)
                p_p6.ready = [tok6]
                p_p6.users = []
                s_ah6[i].users.append(tok6)
                s_wqprojT.users.append(tok6)
            else:
                mm_group(p_p6, items, reads=[s_ah6[i], s_wqprojT])
            ln_rows(p_p6.ap, p_p6, stats6[i], s_v6[i], s_v6[i].ap, D)
            emit(nc.vector, lambda i=i: nc.vector.tensor_add(
                s_o[i].ap, s_v6[i].ap, s_x1r[i].ap), "v",
                writes=[s_o[i]], reads=[s_v6[i], s_x1r[i]])
            emit(nc.sync, lambda su=su, i=i: nc.sync.dma_start(
                d_out[su * 128:(su + 1) * 128, :], s_o[i].ap), f"dO{i}",
                reads=[s_o[i]], dk=16)

    nc.sync.wait_ge(pr.sem("dO0"), pr.cnt["dO0"])
    nc.sync.wait_ge(pr.sem("dO1"), pr.cnt["dO1"])
    return nc


def _prep_common(inputs):
    f = np.float32

    def r(a):
        return np.ascontiguousarray(a)

    W_in = np.asarray(inputs["L0_attn_W_in"], f)
    W_x = np.asarray(inputs["L0_attn_W_x"], f)
    W_dt = np.asarray(inputs["L0_attn_W_dt"], f)
    W_out = np.asarray(inputs["L0_attn_W_out"], f)
    conv_w = np.asarray(inputs["L0_attn_conv_w"], f)
    conv_b = np.asarray(inputs["L0_attn_conv_b"], f)
    b_dt = np.asarray(inputs["L0_attn_b_dt"], f)
    A_log = np.asarray(inputs["L0_attn_A_log"], f)
    D_p = np.asarray(inputs["L0_attn_D_param"], f)
    Wfc = np.asarray(inputs["L0_ffn_W_fc"], f)
    lnw = np.asarray(inputs["L0_ffn_ln_w"], f)
    lnb = np.asarray(inputs["L0_ffn_ln_b"], f)
    Wproj = np.asarray(inputs["L0_ffn_W_proj"], f)

    consts = np.zeros((128, 4), f)
    consts[:, 0] = 1.0
    consts[:, 1] = 1e-5
    return {
        "w_in_r": r(W_in.reshape(NDD, 128, 4096).transpose(1, 0, 2).reshape(128, NDD * 4096)),
        "w_x_r": r(W_x.reshape(NDB, 128, 160).transpose(1, 0, 2).reshape(128, NDB * 160)),
        "w_dt": r(W_dt),
        "w_out_r": r(W_out.reshape(NDB, 128, 1024).transpose(1, 0, 2).reshape(128, NDB * 1024)),
        "conv_w_r": r(conv_w.reshape(NDB, 128, 4).transpose(1, 0, 2).reshape(128, NDB * 4)),
        "conv_b_r": r(conv_b.reshape(NDB, 128).T),
        "b_dt_r": r(b_dt.reshape(NDB, 128).T),
        "a_log_r": r(A_log.reshape(NDB, 128, NS).transpose(1, 0, 2).reshape(128, NDB * NS)),
        "d_r": r(D_p.reshape(NDB, 128).T),
        "wfc": r(Wfc),
        "wproj": r(Wproj),
        "lnw_r": r(lnw.reshape(NHB, 128).T),
        "lnb_r": r(lnb.reshape(NHB, 128).T),
        "consts": consts,
        "id16": np.eye(128, dtype=BFNP),
        "ones_col16": np.ones((128, 1), dtype=BFNP),
    }


def kernel(**inputs):
    x_all = np.asarray(inputs["L0_attn_x"], np.float32)
    common = _prep_common(inputs)
    nc = build()
    in_maps = [dict(common, x=np.ascontiguousarray(x_all[i])) for i in range(8)]
    res = run_bass_kernel_spmd(nc, in_maps, core_ids=list(range(8)))
    return np.stack([np.asarray(res.results[i]["out"]) for i in range(8)], axis=0)
